# version 03
# baseline (speedup 1.0000x reference)
import math
import contextlib
import numpy as np
import ml_dtypes
import concourse.bass as bass
import concourse.mybir as mybir
from concourse.bass_utils import run_bass_kernel_spmd

F32 = mybir.dt.float32
BF16 = mybir.dt.bfloat16
I32 = mybir.dt.int32
AF = mybir.ActivationFunctionType
ALU = mybir.AluOpType
AX = mybir.AxisListType

P = 128
TOK = 4096
NT = TOK // P
D = 1024
KD = D // P
EPS = 1e-5

ALL_STAGES = ("l0a", "l0f", "l1a", "l1f")


class Ctx:

    def __init__(self, nc, stack):
        self.nc = nc
        self.stack = stack
        self.dram = {}
        self.sem = {}
        self.cnt = {}

    def get_sem(self, name):
        if name not in self.sem:
            self.sem[name] = self.nc.alloc_semaphore(name=name)
            self.cnt[name] = 0
        return self.sem[name]

    def inc(self, name, by=1):
        self.cnt[name] += by
        return self.cnt[name]

    def mark(self, ins, name="vch"):
        ins.then_inc(self.get_sem(name), 1)
        return self.inc(name)


LN2C = math.log(2.0)


class RstdPlan:

    ITERS = 4

    def __init__(self, ctx, v_ap, out_ap, z, e, t, pow_scale=-0.5,
                 raw_ln=False):
        self.ctx = ctx
        self.v_ap, self.out_ap = v_ap, out_ap
        self.z, self.e, self.t = z, e, t
        self.pow_scale = pow_scale
        self.raw_ln = raw_ln
        ctx.get_sem("r_s"), ctx.get_sem("r_v")
        self.s0 = ctx.cnt["r_s"]
        self.v0 = ctx.cnt["r_v"]
        ctx.cnt["r_s"] += self.ITERS + 1
        ctx.cnt["r_v"] += self.ITERS + 1
        self.done_val = self.s0 + self.ITERS + 1

    def emit_vector(self, v):
        ctx = self.ctx
        sem = ctx.sem
        ctx.mark(v.tensor_copy(self.z, self.v_ap.bitcast(I32)))
        v.wait_ge(sem["vch"], ctx.cnt["vch"])
        v.tensor_scalar(self.z, self.z, LN2C / (1 << 23), -127.0 * LN2C,
                        ALU.mult, ALU.add).then_inc(sem["r_v"], 1)
        for i in range(self.ITERS):
            v.wait_ge(sem["r_s"], self.s0 + 1 + i)
            ctx.mark(v.tensor_mul(self.t, self.v_ap, self.e))
            v.wait_ge(sem["vch"], ctx.cnt["vch"])
            v.scalar_tensor_tensor(self.z, self.t, -1.0, self.z,
                                   ALU.add, ALU.add).then_inc(sem["r_v"], 1)

    def emit_scalar(self, sc):
        sem = self.ctx.sem
        for i in range(self.ITERS):
            sc.wait_ge(sem["r_v"], self.v0 + 1 + i)
            sc.activation(self.e, self.z, AF.Exp, scale=-1.0
                          ).then_inc(sem["r_s"], 1)
        sc.wait_ge(sem["r_v"], self.v0 + 1 + self.ITERS)
        if self.raw_ln:
            sc.activation(self.out_ap, self.z, AF.Copy).then_inc(sem["r_s"], 1)
        else:
            sc.activation(self.out_ap, self.z, AF.Exp, scale=self.pow_scale
                          ).then_inc(sem["r_s"], 1)



def emit_l0_ffn(ctx, x_in, x_out):
    nc = ctx.nc
    H = 3072
    NHT = H // P
    SPAN = 1024
    NSP = TOK // SPAN
    TPS = SPAN // P
    S = 1.0 / 127.0

    for s in ("f1x", "f1c", "f2x", "fw", "f_tr", "f_ln", "f_lnT_ev",
              "f_gu_mm", "f_gu_ev", "f_th", "f_dn_mm", "f_r_ev", "f_res",
              "f_st", "f_st2", "fwd", "f_xr", "f_xr2", "fw2",
              "f_dm", "f_dbg", "vch", "r_s", "r_v"):
        ctx.get_sem(s)
    sem = ctx.sem

    wgT = ctx.dram["l0f_wgT"]
    wuT = ctx.dram["l0f_wuT"]
    wdT = ctx.dram["l0f_wdT"]

    xr = x_in.ap().rearrange("(n p) d -> n p d", p=P)
    xo = x_out.ap().rearrange("(n p) d -> n p d", p=P)

    with contextlib.ExitStack() as st:
        E2 = st.enter_context
        ident = ctx.sb_ident
        xsp = E2(nc.sbuf_tensor("f_xsp", [P, TPS * D], F32))
        xres = [E2(nc.sbuf_tensor(f"f_xres{i}", [P, D], F32)) for i in range(2)]
        s1 = E2(nc.sbuf_tensor("f_s1", [P, NT], F32))
        s2 = E2(nc.sbuf_tensor("f_s2", [P, NT], F32))
        mu = E2(nc.sbuf_tensor("f_mu", [P, NT], F32))
        rstd = E2(nc.sbuf_tensor("f_rstd", [P, NT], F32))
        nsc = [E2(nc.sbuf_tensor(f"f_nsc{i}", [P, NT], F32)) for i in range(2)]
        dump = E2(nc.sbuf_tensor("f_dump", [P, D], F32))
        lnst = E2(nc.sbuf_tensor("f_lnst", [P, D], BF16))
        lnT = E2(nc.sbuf_tensor("f_lnT", [P, KD * SPAN], BF16))
        hmT = E2(nc.sbuf_tensor("f_hmT", [P, NHT * SPAN], BF16))
        wgt = [E2(nc.sbuf_tensor(f"f_wg{i}", [P, KD * P], BF16)) for i in range(2)]
        wut = [E2(nc.sbuf_tensor(f"f_wu{i}", [P, KD * P], BF16)) for i in range(2)]
        wdres = E2(nc.sbuf_tensor("f_wdres", [P, NHT * D], BF16))
        tanh_t = [E2(nc.sbuf_tensor(f"f_tanh{i}", [P, SPAN], F32)) for i in range(2)]
        us = E2(nc.sbuf_tensor("f_us", [P, SPAN], F32))
        rsp = E2(nc.sbuf_tensor("f_rsp", [P, 4 * D], F32))
        t1 = E2(nc.sbuf_tensor("f_t1", [P, NT], F32))
        s1b = E2(nc.sbuf_tensor("f_s1b", [P, TPS], F32))
        s2b = E2(nc.sbuf_tensor("f_s2b", [P, TPS], F32))
        mu2 = E2(nc.sbuf_tensor("f_mu2", [P, TPS], F32))
        rstd2 = E2(nc.sbuf_tensor("f_rstd2", [P, TPS], F32))
        outst = [E2(nc.sbuf_tensor(f"f_outst{i}", [P, D], F32)) for i in range(2)]
        pars = E2(nc.psum_tensor("f_pars", [P, 4096], F32))
        parsb = pars.bitcast(BF16)

        with nc.Block() as blk:
            @blk.sync
            def _(sy):
                for sp in range(NSP):
                    if sp >= 1:
                        sy.wait_ge(sem["f_dm"], 2 * TPS * sp)
                    sy.dma_start(
                        xsp[:, :],
                        xr[sp * TPS:(sp + 1) * TPS, :, :].transpose([1, 0, 2]),
                    ).then_inc(sem["f1x"], 16)
                    ctx.inc("f1x", 16)

            @blk.gpsimd
            def _(g):
                g.dma_start(
                    wdres[:, :],
                    wdT.ap().rearrange("(n p) d -> p n d", p=P),
                ).then_inc(sem["fwd"], 16)
                ctx.inc("fwd", 16)

            @blk.scalar
            def _(sc):
                for sp in range(NSP):
                    sc.wait_ge(sem["f1x"], 16 * (sp + 1))
                    for t in range(TPS):
                        g = sp * TPS + t
                        src = xsp[:, t * D:(t + 1) * D]
                        if g > 0:
                            sc.wait_ge(sem["f_dm"], 2 * g)
                        sc.activation(dump[:, :], src, AF.Copy,
                                      accum_out=s1[:, g:g + 1]
                                      ).then_inc(sem["f_dm"], 1)
                        ctx.inc("f_dm", 1)
                        sc.wait_ge(sem["f_dm"], ctx.cnt["f_dm"])
                        ins = sc.activation(dump[:, :], src, AF.Square,
                                            accum_out=s2[:, g:g + 1]
                                            ).then_inc(sem["f_dm"], 1)
                        ctx.inc("f_dm", 1)

        if getattr(ctx, "debug", False):
            with nc.Block() as blk:
                @blk.gpsimd
                def _(g):
                    g.dma_start(ctx.dram["d_s1"].ap(), s1[:, :]).then_inc(sem["f_dbg"], 16)
                    g.dma_start(ctx.dram["d_s2"].ap(), s2[:, :]).then_inc(sem["f_dbg"], 16)
                    ctx.inc("f_dbg", 32)
                    g.wait_ge(sem["f_dbg"], ctx.cnt["f_dbg"])

        plan_pre = RstdPlan(ctx, s2[:, :], rstd[:, :], nsc[0][:, :],
                            nsc[1][:, :], t1[:, :])
        with nc.Block() as blk:
            @blk.vector
            def _(v):
                ctx.mark(v.tensor_scalar_mul(mu[:, :], s1[:, :], 1.0 / D))
                ctx.mark(v.tensor_scalar_mul(s2[:, :], s2[:, :], 1.0 / D))
                v.wait_ge(sem["vch"], ctx.cnt["vch"])
                ctx.mark(v.tensor_mul(t1[:, :], mu[:, :], mu[:, :]))
                v.wait_ge(sem["vch"], ctx.cnt["vch"])
                ctx.mark(v.tensor_sub(s2[:, :], s2[:, :], t1[:, :]))
                v.wait_ge(sem["vch"], ctx.cnt["vch"])
                ctx.mark(v.tensor_scalar_add(s2[:, :], s2[:, :], EPS))
                v.wait_ge(sem["vch"], ctx.cnt["vch"])
                plan_pre.emit_vector(v)
                v.wait_ge(sem["r_s"], plan_pre.done_val)
                ctx.mark(v.tensor_mul(mu[:, :], mu[:, :], rstd[:, :]))
                v.wait_ge(sem["vch"], ctx.cnt["vch"])
                ctx.mark(v.tensor_scalar_mul(mu[:, :], mu[:, :], -1.0))
                v.wait_ge(sem["vch"], ctx.cnt["vch"])

            @blk.scalar
            def _(sc):
                plan_pre.emit_scalar(sc)

        z2 = E2(nc.sbuf_tensor("f_z2", [P, 4], F32))
        e2 = E2(nc.sbuf_tensor("f_e2", [P, 4], F32))
        tt2 = E2(nc.sbuf_tensor("f_tt2", [P, 4], F32))
        plans2 = [RstdPlan(ctx, s2b[:, 0:4], rstd2[:, 0:4], z2[:, :],
                           e2[:, :], tt2[:, :]) for _ in range(NSP * 2)]

        if getattr(ctx, "debug", False):
            with nc.Block() as blk:
                @blk.gpsimd
                def _(g):
                    g.dma_start(ctx.dram["d_rstd"].ap(), rstd[:, :]).then_inc(sem["f_dbg"], 16)
                    g.dma_start(ctx.dram["d_mu"].ap(), mu[:, :]).then_inc(sem["f_dbg"], 16)
                    ctx.inc("f_dbg", 32)
                    g.wait_ge(sem["f_dbg"], ctx.cnt["f_dbg"])

        with nc.Block() as blk:
            @blk.sync
            def _(sy):
                def res_io(sp):
                    for t in range(TPS):
                        g = sp * TPS + t
                        if g >= 2:
                            sy.wait_ge(sem["f_res"], g - 1)
                        sy.dma_start(xres[g % 2][:, :], xr[g, :, :]
                                     ).then_inc(sem["f_xr" if g % 2 == 0 else "f_xr2"], 16)
                        ctx.inc("f_xr" if g % 2 == 0 else "f_xr2", 16)
                        sy.wait_ge(sem["f_res"], g + 1)
                        sy.dma_start(xo[g, :, :],
                                     outst[g % 2][:, :]
                                     ).then_inc(sem["f_st" if g % 2 == 0 else "f_st2"], 16)
                        ctx.inc("f_st" if g % 2 == 0 else "f_st2", 16)
                for sp in range(NSP):
                    if sp >= 1:
                        sy.wait_ge(sem["f_ln"], sp * TPS)
                    sy.dma_start(
                        xsp[:, :],
                        xr[sp * TPS:(sp + 1) * TPS, :, :].transpose([1, 0, 2]),
                    ).then_inc(sem["f2x"], 16)
                    ctx.inc("f2x", 16)
                    res_io(sp)
                sy.wait_ge(sem["f_st"], ctx.cnt["f_st"])
                sy.wait_ge(sem["f_st2"], ctx.cnt["f_st2"])

            @blk.gpsimd
            def _(g):
                for sp in range(NSP):
                    for ht in range(NHT):
                        idx = sp * NHT + ht
                        if idx >= 2:
                            g.wait_ge(sem["f_gu_ev"], idx - 1)
                        g.dma_start(
                            wgt[ht % 2][:, :],
                            wgT.ap()[:, ht * P:(ht + 1) * P]
                               .rearrange("(k p) h -> p k h", p=P),
                        ).then_inc(sem["fw" if ht % 2 == 0 else "fw2"], 16)
                        g.dma_start(
                            wut[ht % 2][:, :],
                            wuT.ap()[:, ht * P:(ht + 1) * P]
                               .rearrange("(k p) h -> p k h", p=P),
                        ).then_inc(sem["fw" if ht % 2 == 0 else "fw2"], 16)
                        ctx.inc("fw" if ht % 2 == 0 else "fw2", 32)

            @blk.tensor
            def _(te):
                te.wait_ge(sem["fwd"], 16)
                for sp in range(NSP):
                    for t in range(TPS):
                        g = sp * TPS + t
                        te.wait_ge(sem["f_ln"], g + 1)
                        if g >= 2:
                            te.wait_ge(sem["f_lnT_ev"], g - 1)
                        if sp >= 1 and t == 0:
                            te.wait_ge(sem["f_r_ev"], sp * 4)
                        pcb = (t % 2) * 1024
                        for k in range(KD):
                            ins = te.transpose(
                                parsb[:, pcb + k * P:pcb + (k + 1) * P],
                                lnst[:, k * P:(k + 1) * P], ident[:, :])
                        ins.then_inc(sem["f_tr"], 1)
                        ctx.inc("f_tr", 1)
                    for ht in range(NHT):
                        idx = sp * NHT + ht
                        te.wait_ge(sem["fw" if ht % 2 == 0 else "fw2"],
                                   32 * (idx // 2 + 1))
                        if ht == 0:
                            te.wait_ge(sem["f_lnT_ev"], (sp + 1) * TPS)
                        if idx >= 2:
                            te.wait_ge(sem["f_gu_ev"], idx - 1)
                        gc = (ht % 2) * 1024
                        uc = 2048 + (ht % 2) * 1024
                        for k in range(KD):
                            wgap = wgt[ht % 2][:, k * P:(k + 1) * P]
                            wuap = wut[ht % 2][:, k * P:(k + 1) * P]
                            mv0 = lnT[:, k * SPAN:k * SPAN + 512]
                            mv1 = lnT[:, k * SPAN + 512:k * SPAN + 1024]
                            te.matmul(pars[:, gc:gc + 512], wgap, mv0,
                                      start=(k == 0), stop=(k == KD - 1))
                            te.matmul(pars[:, gc + 512:gc + 1024], wgap, mv1,
                                      start=(k == 0), stop=(k == KD - 1))
                            te.matmul(pars[:, uc:uc + 512], wuap, mv0,
                                      start=(k == 0), stop=(k == KD - 1))
                            ins = te.matmul(pars[:, uc + 512:uc + 1024], wuap, mv1,
                                            start=(k == 0), stop=(k == KD - 1))
                        ins.then_inc(sem["f_gu_mm"], 1)
                        ctx.inc("f_gu_mm", 1)
                    for q in range(4):
                        qi = sp * 4 + q
                        dc = (q % 2) * 2048
                        if q == 0:
                            te.wait_ge(sem["f_gu_ev"], (sp + 1) * NHT)
                        if qi >= 2:
                            te.wait_ge(sem["f_r_ev"], qi - 1)
                        if q >= 2:
                            te.wait_ge(sem["f_res"], sp * TPS + 2 * (q - 1))
                        for ht in range(NHT):
                            te.wait_ge(sem["f_gu_ev"], sp * NHT + ht + 1)
                            for tt in range(2):
                                stat = hmT[:, ht * SPAN + q * 256 + tt * P:
                                           ht * SPAN + q * 256 + (tt + 1) * P]
                                for nb in range(2):
                                    ins = te.matmul(
                                        pars[:, dc + tt * 1024 + nb * 512:
                                             dc + tt * 1024 + (nb + 1) * 512],
                                        stat,
                                        wdres[:, ht * D + nb * 512:ht * D + (nb + 1) * 512],
                                        start=(ht == 0), stop=(ht == NHT - 1))
                        ins.then_inc(sem["f_dn_mm"], 1)
                        ctx.inc("f_dn_mm", 1)

            @blk.scalar
            def _(sc):
                for sp in range(NSP):
                    for ht in range(NHT):
                        idx = sp * NHT + ht
                        sc.wait_ge(sem["f_gu_mm"], idx + 1)
                        if idx >= 2:
                            sc.wait_ge(sem["f_gu_ev"], idx - 1)
                        gc = (ht % 2) * 1024
                        sc.activation(tanh_t[ht % 2][:, :], pars[:, gc:gc + 1024],
                                      AF.Tanh, scale=0.5 * S,
                                      ).then_inc(sem["f_th"], 1)
                        ctx.inc("f_th", 1)
                    for half in range(2):
                        plans2[sp * 2 + half].emit_scalar(sc)

            @blk.vector
            def _(v):
                for sp in range(NSP):
                    v.wait_ge(sem["f2x"], 16 * (sp + 1))
                    for t in range(TPS):
                        g = sp * TPS + t
                        v.wait_ge(sem["f_tr"], g)
                        v.tensor_scalar(
                            lnst[:, :], xsp[:, t * D:(t + 1) * D],
                            rstd[:, g:g + 1], mu[:, g:g + 1], ALU.mult, ALU.add,
                        ).then_inc(sem["f_ln"], 1)
                        ctx.inc("f_ln", 1)
                        v.wait_ge(sem["f_tr"], g + 1)
                        pcb = (t % 2) * 1024
                        v.tensor_copy(
                            bass.AP(lnT, t * P, [[KD * SPAN, P], [SPAN, KD], [1, P]]),
                            parsb[:, pcb:pcb + KD * P],
                        ).then_inc(sem["f_lnT_ev"], 1)
                        ctx.inc("f_lnT_ev", 1)
                    for ht in range(NHT):
                        idx = sp * NHT + ht
                        gc = (ht % 2) * 1024
                        uc = 2048 + (ht % 2) * 1024
                        v.wait_ge(sem["f_th"], idx + 1)
                        ctx.mark(v.tensor_scalar_mul(us[:, :],
                                                     pars[:, uc:uc + 1024],
                                                     0.5 * S * S))
                        v.wait_ge(sem["vch"], ctx.cnt["vch"])
                        ctx.mark(v.tensor_mul(dump[:, :], pars[:, gc:gc + 1024],
                                              us[:, :]))
                        v.wait_ge(sem["vch"], ctx.cnt["vch"])
                        v.scalar_tensor_tensor(
                            hmT[:, ht * SPAN:(ht + 1) * SPAN],
                            tanh_t[ht % 2][:, :], 1.0,
                            dump[:, :], ALU.add, ALU.mult,
                        ).then_inc(sem["f_gu_ev"], 1)
                        ctx.inc("f_gu_ev", 1)
                    for half in range(2):
                        for q in (2 * half, 2 * half + 1):
                            qi = sp * 4 + q
                            v.wait_ge(sem["f_dn_mm"], qi + 1)
                            dc = (q % 2) * 2048
                            v.tensor_scalar_mul(rsp[:, (q % 2) * 2048:(q % 2) * 2048 + 2048],
                                                pars[:, dc:dc + 2048], S,
                                                ).then_inc(sem["f_r_ev"], 1)
                            ctx.inc("f_r_ev", 1)
                        v.wait_ge(sem["f_r_ev"], ctx.cnt["f_r_ev"])
                        for j in range(4):
                            src = rsp[:, j * D:(j + 1) * D]
                            v.wait_ge(sem["vch"], ctx.cnt["vch"])
                            ctx.mark(v.reduce_sum(s1b[:, j:j + 1], src, axis=AX.X))
                            ctx.mark(v.tensor_mul(dump[:, :], src, src))
                            v.wait_ge(sem["vch"], ctx.cnt["vch"])
                            ctx.mark(v.reduce_sum(s2b[:, j:j + 1], dump[:, :],
                                                  axis=AX.X))
                        v.wait_ge(sem["vch"], ctx.cnt["vch"])
                        ctx.mark(v.tensor_scalar_mul(mu2[:, 0:4], s1b[:, 0:4],
                                                     1.0 / D))
                        v.tensor_scalar_mul(s2b[:, 0:4], s2b[:, 0:4], 1.0 / D)
                        v.wait_ge(sem["vch"], ctx.cnt["vch"])
                        ctx.mark(v.tensor_mul(t1[:, 0:4], mu2[:, 0:4], mu2[:, 0:4]))
                        v.wait_ge(sem["vch"], ctx.cnt["vch"])
                        ctx.mark(v.tensor_sub(s2b[:, 0:4], s2b[:, 0:4], t1[:, 0:4]))
                        v.wait_ge(sem["vch"], ctx.cnt["vch"])
                        ctx.mark(v.tensor_scalar_add(s2b[:, 0:4], s2b[:, 0:4], EPS))
                        v.wait_ge(sem["vch"], ctx.cnt["vch"])
                        pl = plans2[sp * 2 + half]
                        pl.emit_vector(v)
                        v.wait_ge(sem["r_s"], pl.done_val)
                        ctx.mark(v.tensor_mul(mu2[:, 0:4], mu2[:, 0:4],
                                              rstd2[:, 0:4]))
                        v.wait_ge(sem["vch"], ctx.cnt["vch"])
                        ctx.mark(v.tensor_scalar_mul(mu2[:, 0:4], mu2[:, 0:4],
                                                     -1.0))
                        v.wait_ge(sem["vch"], ctx.cnt["vch"])
                        for j in range(4):
                            t = half * 4 + j
                            g = sp * TPS + t
                            if g >= 2:
                                v.wait_ge(sem["f_st" if g % 2 == 0 else "f_st2"],
                                          16 * ((g - 2) // 2 + 1))
                            v.wait_ge(sem["f_xr" if g % 2 == 0 else "f_xr2"],
                                      16 * (g // 2 + 1))
                            ob = outst[g % 2]
                            ctx.mark(v.tensor_scalar(
                                ob[:, :], rsp[:, j * D:(j + 1) * D],
                                rstd2[:, j:j + 1], mu2[:, j:j + 1], ALU.mult, ALU.add))
                            v.wait_ge(sem["vch"], ctx.cnt["vch"])
                            v.tensor_add(ob[:, :], ob[:, :],
                                         xres[g % 2][:, :],
                                         ).then_inc(sem["f_res"], 1)
                            ctx.inc("f_res", 1)






def emit_l0_attn(ctx, x_in, x_out):
    nc = ctx.nc
    DI = 2048
    NH = 32
    HD = 64
    CD = 2304
    NJ = CD // P
    DIP = 4384
    SPAN = 512
    NSP2 = getattr(ctx, "dbg_nsp2", TOK // SPAN)
    CPS = SPAN // P
    HALO = 4
    JW = SPAN + HALO

    sems = ("a1x", "a1y", "a2y", "a_xr2", "a_st2", "a_dm", "a2x", "a_ln", "a_tr", "a_lnT_ev", "aw",
            "a_z_mm", "a_z_ev", "a_xb_mm", "a_xb_ev", "a_cvacc", "a_sg",
            "a_cv", "a_dt_mm", "a_pc", "a_cs_mm", "a_tr7", "a_b7",
            "a_g_mm", "a_g_ev", "a_r_mm", "a_ex2", "a_mt", "a_yd_mm",
            "a_sbf", "a_yo_mm", "a_bt_rdy", "a_sc_mm", "a_sp_ev", "a_st_mm",
            "a_su", "a_ep", "a_ye", "a_sgz", "a_gat", "a_zg", "a_sq",
            "a_yn", "a_ynT_ev", "a_op_mm", "a_op_ev", "a_xr", "a_res",
            "a_st", "a_ex", "a_bt_ev", "a_dtA", "a_dsc", "a_csd", "a_dsub",
            "a_dsr", "a_efac", "a_btd", "a_wo", "a_oph", "a_zg1", "a_dmk")
    for s in sems:
        ctx.get_sem(s)
    sem = ctx.sem

    win = ctx.dram["l0a_win"]
    wout = ctx.dram["l0a_wout"]
    xr = x_in.ap().rearrange("(n p) d -> n p d", p=P)
    xo = x_out.ap().rearrange("(n p) d -> n p d", p=P)

    with contextlib.ExitStack() as st:
        E2 = st.enter_context
        ident = ctx.sb_ident
        pars = E2(nc.psum_tensor("a_pars", [P, 4096], F32))
        parsb = pars.bitcast(BF16)
        s1 = E2(nc.sbuf_tensor("a_s1", [P, NT], F32))
        s2 = E2(nc.sbuf_tensor("a_s2", [P, NT], F32))
        a1 = E2(nc.sbuf_tensor("a_a1", [P, NT], F32))
        c1 = E2(nc.sbuf_tensor("a_c1", [P, NT], F32))
        nz = [E2(nc.sbuf_tensor(f"a_nz{i}", [P, NT], F32)) for i in range(3)]
        xsp = [E2(nc.sbuf_tensor(f"a_xsp{i}", [P, D], F32)) for i in range(2)]
        lnst = E2(nc.sbuf_tensor("a_lnst", [P, D], BF16))
        lnTs = E2(nc.sbuf_tensor("a_lnTs", [P, KD * SPAN], BF16))
        winb = E2(nc.sbuf_tensor("a_winb", [P, KD * DIP], BF16))
        woutb = E2(nc.sbuf_tensor("a_woutb", [P, 8 * D], BF16))
        cwb = E2(nc.sbuf_tensor("a_cwb", [P, NJ * 4], F32))
        cbb = E2(nc.sbuf_tensor("a_cbb", [P, NJ], F32))
        dtbb = E2(nc.sbuf_tensor("a_dtbb", [P, NH], F32))
        ab = E2(nc.sbuf_tensor("a_ab", [P, NH], F32))
        dpb = E2(nc.sbuf_tensor("a_dpb", [P, NH], F32))
        nwb = E2(nc.sbuf_tensor("a_nwb", [P, DI], BF16))
        triub = E2(nc.sbuf_tensor("a_triub", [P, P], F32))
        onesb = E2(nc.sbuf_tensor("a_onesb", [P, P], F32))
        hselb = E2(nc.sbuf_tensor("a_hselb", [32, NH * P], F32))
        xbc = E2(nc.sbuf_tensor("a_xbc", [P, NJ * JW], BF16))
        xtail = E2(nc.sbuf_tensor("a_xtail", [P, NJ * HALO], BF16))
        cvt = E2(nc.sbuf_tensor("a_cvt", [P, SPAN], F32))
        sgc = E2(nc.sbuf_tensor("a_sgc", [P, SPAN], BF16))
        zsb = E2(nc.sbuf_tensor("a_zsb", [P, DI], BF16))
        dtc = E2(nc.sbuf_tensor("a_dtc", [P, NH], F32))
        act_ = E2(nc.sbuf_tensor("a_ac", [P, NH], F32))
        acsT = E2(nc.sbuf_tensor("a_acsT", [P, NH], F32))
        nacsT = E2(nc.sbuf_tensor("a_nacsT", [P, NH], F32))
        acshm = E2(nc.sbuf_tensor("a_acshm", [32, P], F32))
        dsdt = E2(nc.sbuf_tensor("a_dsdt", [P, NH], F32))
        einb = E2(nc.sbuf_tensor("a_einb", [P, NH], F32))
        etbv = E2(nc.sbuf_tensor("a_etbv", [P, NH], F32))
        tA = E2(nc.sbuf_tensor("a_tA", [P, NH], F32))
        jnk = E2(nc.sbuf_tensor("a_jnk", [P, 2], F32))
        tB = E2(nc.sbuf_tensor("a_tB", [P, NH], F32))
        tC = E2(nc.sbuf_tensor("a_tC", [P, NH], F32))
        zdt = E2(nc.sbuf_tensor("a_zdt", [P, NH], F32))
        edt = E2(nc.sbuf_tensor("a_edt", [P, NH], F32))
        tdt = E2(nc.sbuf_tensor("a_tdt", [P, NH], F32))
        lnp = E2(nc.sbuf_tensor("a_lnp", [P, NH], F32))
        x_tm = E2(nc.sbuf_tensor("a_x_tm", [P, DI], BF16))
        btm = E2(nc.sbuf_tensor("a_btm", [P, P], BF16))
        bsc = E2(nc.sbuf_tensor("a_bsc", [P, P], BF16))
        gtm = E2(nc.sbuf_tensor("a_gtm", [P, P], F32))
        expdt = E2(nc.sbuf_tensor("a_expdt", [P, P], F32))
        dmm = E2(nc.sbuf_tensor("a_dmm", [P, P], F32))
        mtb = E2(nc.sbuf_tensor("a_mtb", [P, P], BF16))
        sbf = E2(nc.sbuf_tensor("a_sbf", [P, HD], BF16))
        spr = E2(nc.sbuf_tensor("a_spr", [64, P], BF16))
        S = E2(nc.sbuf_tensor("a_S", [P, NH * HD], F32))
        epart = E2(nc.sbuf_tensor("a_epart", [P, HD], F32))
        t2b = E2(nc.sbuf_tensor("a_t2b", [P, HD], F32))
        ych = E2(nc.sbuf_tensor("a_ych", [P, DI], F32))
        ynb = E2(nc.sbuf_tensor("a_ynb", [P, DI], BF16))
        ynT = E2(nc.sbuf_tensor("a_ynT", [P, 16 * P], BF16))
        s2c = E2(nc.sbuf_tensor("a_s2c", [P, 1], F32))
        rstdc = E2(nc.sbuf_tensor("a_rstdc", [P, 1], F32))
        zc1 = E2(nc.sbuf_tensor("a_zc1", [P, 1], F32))
        ec1 = E2(nc.sbuf_tensor("a_ec1", [P, 1], F32))
        tc1 = E2(nc.sbuf_tensor("a_tc1", [P, 1], F32))
        s1r = E2(nc.sbuf_tensor("a_s1r", [P, 1], F32))
        s2r = E2(nc.sbuf_tensor("a_s2r", [P, 1], F32))
        mur = E2(nc.sbuf_tensor("a_mur", [P, 1], F32))
        rstdr = E2(nc.sbuf_tensor("a_rstdr", [P, 1], F32))
        rsb = E2(nc.sbuf_tensor("a_rsb", [P, D], F32))
        xres = [E2(nc.sbuf_tensor(f"a_xres{i}", [P, D], F32)) for i in range(2)]
        outst = [E2(nc.sbuf_tensor(f"a_outst{i}", [P, D], F32)) for i in range(2)]

        plans_dt, plans_rm, plans_po = [], [], []

        with nc.Block() as blk:
            @blk.gpsimd
            def _(g):
                g.dma_start(winb[:, :], win.ap().rearrange("(k p) d -> p k d", p=P)
                            ).then_inc(sem["aw"], 16)
                g.dma_start(woutb[:, :],
                            wout.ap()[0:1024, :].rearrange("(k p) d -> p k d", p=P)
                            ).then_inc(sem["aw"], 16)
                for t, nmk in ((cwb, "l0a_cw"), (cbb, "l0a_cb"), (dtbb, "l0a_dtb"),
                               (ab, "l0a_alog"), (dpb, "l0a_dpar"), (nwb, "l0a_nw"),
                               (triub, "l0a_triu"), (onesb, "l0a_ones"),
                               (hselb, "l0a_hsel"),):
                    g.dma_start(t[:, :], ctx.dram[nmk].ap()).then_inc(sem["aw"], 16)
                ctx.inc("aw", 16 * 11)

            @blk.sync
            def _(sy):
                for g in range(NT):
                    if g >= 2:
                        sy.wait_ge(sem["a_dm"], 4 * (g - 1))
                    sy.dma_start(xsp[g % 2][:, :], xr[g, :, :]
                                 ).then_inc(sem["a1x" if g % 2 == 0 else "a1y"], 16)
                    ctx.inc("a1x" if g % 2 == 0 else "a1y", 16)

            @blk.scalar
            def _(sc):
                for g in range(NT):
                    sc.wait_ge(sem["a1x" if g % 2 == 0 else "a1y"],
                               16 * (g // 2 + 1))
                    src = xsp[g % 2][:, :]
                    if g > 0:
                        sc.wait_ge(sem["a_dm"], 4 * g)
                    sc.activation(ych[:, 0:D], src, AF.Copy,
                                  accum_out=s1[:, g:g + 1]).then_inc(sem["a_dm"], 1)
                    ctx.inc("a_dm", 1)
                    sc.wait_ge(sem["a_dm"], ctx.cnt["a_dm"])
                    sc.activation(ych[:, 0:D], src, AF.Square,
                                  accum_out=s2[:, g:g + 1]).then_inc(sem["a_dm"], 3)
                    ctx.inc("a_dm", 3)

        planA = RstdPlan(ctx, s2[:, :], a1[:, :], nz[0][:, :], nz[1][:, :],
                         nz[2][:, :])
        with nc.Block() as blk:
            @blk.vector
            def _(v):
                ctx.mark(v.tensor_scalar_mul(c1[:, :], s1[:, :], 1.0 / D))
                ctx.mark(v.tensor_scalar_mul(s2[:, :], s2[:, :], 1.0 / D))
                v.wait_ge(sem["vch"], ctx.cnt["vch"])
                ctx.mark(v.tensor_mul(nz[0][:, :], c1[:, :], c1[:, :]))
                v.wait_ge(sem["vch"], ctx.cnt["vch"])
                ctx.mark(v.tensor_sub(s2[:, :], s2[:, :], nz[0][:, :]))
                v.wait_ge(sem["vch"], ctx.cnt["vch"])
                ctx.mark(v.tensor_scalar_add(s2[:, :], s2[:, :], EPS))
                v.wait_ge(sem["vch"], ctx.cnt["vch"])
                planA.emit_vector(v)
                v.wait_ge(sem["r_s"], planA.done_val)
                ctx.mark(v.tensor_mul(c1[:, :], c1[:, :], a1[:, :]))
                v.wait_ge(sem["vch"], ctx.cnt["vch"])
                ctx.mark(v.tensor_scalar_mul(c1[:, :], c1[:, :], -1.0))
                v.wait_ge(sem["vch"], ctx.cnt["vch"])
                v.wait_ge(sem["a_ex"], 1)
                ctx.mark(v.tensor_scalar_mul(ab[:, :], ab[:, :], -1.0))
                v.wait_ge(sem["vch"], ctx.cnt["vch"])
                ctx.mark(v.memset(S[:, :], 0.0))
                ctx.mark(v.memset(jnk[:, :], 0.0))
                v.wait_ge(sem["vch"], ctx.cnt["vch"])

            @blk.scalar
            def _(sc):
                planA.emit_scalar(sc)
                sc.wait_ge(sem["aw"], 16 * 11)
                sc.activation(ab[:, :], ab[:, :], AF.Exp).then_inc(sem["a_ex"], 1)
                ctx.inc("a_ex", 1)

        for _ch in range(NT):
            plans_dt.append(RstdPlan(ctx, tB[:, :], lnp[:, :], zdt[:, :],
                                     edt[:, :], tdt[:, :], raw_ln=True))
            plans_rm.append(RstdPlan(ctx, s2c[:, :], rstdc[:, :], zc1[:, :],
                                     ec1[:, :], tc1[:, :]))
            plans_po.append(RstdPlan(ctx, s2r[:, :], rstdr[:, :], zc1[:, :],
                                     ec1[:, :], tc1[:, :]))

        with nc.Block() as blk:
            @blk.sync
            def _(sy):
                for sp in range(NSP2):
                    for t in range(CPS):
                        g = sp * CPS + t
                        if g >= 2:
                            sy.wait_ge(sem["a_ln"], g - 1)
                        sy.dma_start(xsp[g % 2][:, :], xr[g, :, :]
                                     ).then_inc(sem["a2x" if g % 2 == 0 else "a2y"], 16)
                        ctx.inc("a2x" if g % 2 == 0 else "a2y", 16)
                    for cl in range(CPS):
                        ch = sp * CPS + cl
                        if ch >= 2:
                            sy.wait_ge(sem["a_res"], ch - 1)
                        sy.dma_start(xres[ch % 2][:, :], xr[ch, :, :]
                                     ).then_inc(sem["a_xr" if ch % 2 == 0 else "a_xr2"], 16)
                        ctx.inc("a_xr" if ch % 2 == 0 else "a_xr2", 16)
                        sy.wait_ge(sem["a_res"], ch + 1)
                        sy.dma_start(xo[ch, :, :], outst[ch % 2][:, :]
                                     ).then_inc(sem["a_st" if ch % 2 == 0 else "a_st2"], 16)
                        ctx.inc("a_st" if ch % 2 == 0 else "a_st2", 16)
                sy.wait_ge(sem["a_st"], ctx.cnt["a_st"])
                sy.wait_ge(sem["a_st2"], ctx.cnt["a_st2"])

            @blk.gpsimd
            def _(g):
                for ch in range(NSP2 * CPS):
                    for half in range(2):
                        hi = ch * 2 + half
                        if hi == 0:
                            continue
                        g.wait_ge(sem["a_oph"], hi)
                        src_half = (hi % 2)
                        g.dma_start(
                            woutb[:, :],
                            wout.ap()[src_half * 1024:(src_half + 1) * 1024, :]
                                .rearrange("(k p) d -> p k d", p=P)
                        ).then_inc(sem["a_wo"], 16)
                        ctx.inc("a_wo", 16)

            @blk.tensor
            def _(te):
                te.wait_ge(sem["aw"], 16 * 11)
                for sp in range(NSP2):
                    xb = xbc
                    for t in range(CPS):
                        g = sp * CPS + t
                        te.wait_ge(sem["a_ln"], g + 1)
                        if g >= 2:
                            te.wait_ge(sem["a_lnT_ev"], g - 1)
                        if sp >= 1 and t == 0:
                            te.wait_ge(sem["a_dmk"], sp * CPS * NH)
                            te.wait_ge(sem["a_z_ev"], sp * CPS)
                            te.wait_ge(sem["a_g_ev"], sp * CPS)
                        pcb = (t % 2) * 1024
                        for k in range(KD):
                            ins = te.transpose(parsb[:, pcb + k * P:pcb + (k + 1) * P],
                                               lnst[:, k * P:(k + 1) * P],
                                               ident[:, :])
                        ins.then_inc(sem["a_tr"], 1)
                        ctx.inc("a_tr", 1)
                    te.wait_ge(sem["a_lnT_ev"], (sp + 1) * CPS)
                    if sp >= 1:
                        te.wait_ge(sem["a_op_ev"], sp * CPS)
                        te.wait_ge(sem["a_su"], sp * CPS * NH)
                        te.wait_ge(sem["a_ep"], sp * CPS * NH)
                    for j in range(NJ):
                        jj = sp * NJ + j
                        pc = 2048 + (j % 3) * 512
                        if jj >= 3:
                            te.wait_ge(sem["a_xb_ev"], jj - 2)
                        for k in range(KD):
                            ins = te.matmul(pars[:, pc:pc + 512],
                                            winb[:, k * DIP + 2048 + j * P:
                                                 k * DIP + 2048 + (j + 1) * P],
                                            lnTs[:, k * SPAN:(k + 1) * SPAN],
                                            start=(k == 0), stop=(k == KD - 1))
                        ins.then_inc(sem["a_xb_mm"], 1)
                        ctx.inc("a_xb_mm", 1)
                    for cl in range(CPS):
                        ch = sp * CPS + cl
                        co = HALO + cl * P
                        if ch >= 1:
                            te.wait_ge(sem["a_ye"], ch * NH)
                            te.wait_ge(sem["a_dmk"], ch * NH)
                            te.wait_ge(sem["a_zg"], ch)
                        for k in range(KD):
                            stat = lnTs[:, k * SPAN + cl * P:k * SPAN + (cl + 1) * P]
                            for nb in range(4):
                                ins = te.matmul(pars[:, nb * 512:(nb + 1) * 512],
                                                stat,
                                                winb[:, k * DIP + nb * 512:
                                                     k * DIP + (nb + 1) * 512],
                                                start=(k == 0), stop=(k == KD - 1))
                        ins.then_inc(sem["a_z_mm"], 1)
                        ctx.inc("a_z_mm", 1)
                        te.wait_ge(sem["a_b7"], 6 * ch)
                        for k in range(KD):
                            stat = lnTs[:, k * SPAN + cl * P:k * SPAN + (cl + 1) * P]
                            ins = te.matmul(pars[:, 3584:3616], stat,
                                            winb[:, k * DIP + 4352:k * DIP + 4384],
                                            start=(k == 0), stop=(k == KD - 1))
                        ins.then_inc(sem["a_dt_mm"], 1)
                        ctx.inc("a_dt_mm", 1)
                        te.wait_ge(sem["a_pc"], ch + 1)
                        te.wait_ge(sem["a_b7"], 6 * ch + 1)
                        te.matmul(pars[:, 3616:3648], triub[:, :], act_[:, :],
                                  start=True, stop=True)
                        te.matmul(bass.AP(pars, 3648, [[4096, 32], [1, P]]),
                                  act_[:, :], triub[:, :], start=True, stop=True)
                        te.matmul(pars[:, 3776:3808], onesb[:, :], act_[:, :],
                                  start=True, stop=True).then_inc(sem["a_cs_mm"], 1)
                        ctx.inc("a_cs_mm", 1)
                        te.wait_ge(sem["a_cv"], sp * NJ + NJ)
                        for grp in range(2):
                            te.wait_ge(sem["a_b7"], 6 * ch + 2 + grp)
                            for jx in range(grp * 8, grp * 8 + 8):
                                ins = te.transpose(
                                    parsb[:, 7168 + (jx % 8) * P:
                                          7168 + (jx % 8) * P + P],
                                    xb[:, jx * JW + co:jx * JW + co + P],
                                    ident[:, :])
                            ins.then_inc(sem["a_tr7"], 1)
                            ctx.inc("a_tr7", 1)
                        if ch >= 1:
                            te.wait_ge(sem["a_su"], ch * NH)
                        te.transpose(parsb[:, 6144:6144 + P],
                                     xb[:, 16 * JW + co:16 * JW + co + P],
                                     ident[:, :]).then_inc(sem["a_bt_ev"], 1)
                        ctx.inc("a_bt_ev", 1)
                        te.wait_ge(sem["a_z_ev"], ch + 1)
                        if ch >= 1:
                            te.wait_ge(sem["a_g_ev"], ch)
                        te.matmul(pars[:, 0:P],
                                  xb[:, 16 * JW + co:16 * JW + co + P],
                                  xb[:, 17 * JW + co:17 * JW + co + P],
                                  start=True, stop=True).then_inc(sem["a_g_mm"], 1)
                        ctx.inc("a_g_mm", 1)
                        for h in range(NH):
                            hp = h % 2
                            hidx = ch * NH + h
                            if h == 0:
                                te.wait_ge(sem["a_g_ev"], ch + 1)
                            if h >= 2:
                                te.wait_ge(sem["a_dmk"], hidx - 1)
                            te.matmul(pars[:, hp * 512:hp * 512 + P],
                                      hselb[:, h * P:(h + 1) * P], acshm[:, :],
                                      start=True, stop=True).then_inc(sem["a_r_mm"], 1)
                            ctx.inc("a_r_mm", 1)
                            te.wait_ge(sem["a_mt"], hidx + 1)
                            if h >= 2:
                                te.wait_ge(sem["a_ye"], hidx - 1)
                            te.matmul(pars[:, 1024 + hp * 512:1024 + hp * 512 + HD],
                                      mtb[:, :], x_tm[:, h * HD:(h + 1) * HD],
                                      start=True, stop=True).then_inc(sem["a_yd_mm"], 1)
                            ctx.inc("a_yd_mm", 1)
                            te.wait_ge(sem["a_sbf"], hidx + 1)
                            if h >= 2:
                                te.wait_ge(sem["a_ep"], hidx - 1)
                            elif ch >= 1:
                                te.wait_ge(sem["a_op_ev"], ch)
                            te.matmul(pars[:, 2048 + hp * 512:2048 + hp * 512 + HD],
                                      xb[:, 17 * JW + co:17 * JW + co + P],
                                      sbf[:, :], start=True, stop=True
                                      ).then_inc(sem["a_yo_mm"], 1)
                            ctx.inc("a_yo_mm", 1)
                            te.wait_ge(sem["a_bt_rdy"], hidx + 1)
                            if h == 0:
                                pass
                            else:
                                te.wait_ge(sem["a_sp_ev"], hidx)
                            te.matmul(bass.AP(pars, 3072, [[4096, HD], [1, P]]),
                                      x_tm[:, h * HD:(h + 1) * HD], bsc[:, :],
                                      start=True, stop=True).then_inc(sem["a_sc_mm"], 1)
                            ctx.inc("a_sc_mm", 1)
                            te.wait_ge(sem["a_sp_ev"], hidx + 1)
                            if h >= 1:
                                te.wait_ge(sem["a_su"], hidx)
                            te.transpose(parsb[:, 6400:6400 + HD], spr[:, :],
                                         ident[0:HD, 0:HD]).then_inc(sem["a_st_mm"], 1)
                            ctx.inc("a_st_mm", 1)
                        te.wait_ge(sem["a_yn"], ch + 1)
                        for grp in range(2):
                            te.wait_ge(sem["a_b7"], 6 * ch + 4 + grp)
                            for k2 in range(grp * 8, grp * 8 + 8):
                                ins = te.transpose(
                                    parsb[:, 7168 + (k2 % 8) * P:
                                          7168 + (k2 % 8) * P + P],
                                    ynb[:, k2 * P:(k2 + 1) * P], ident[:, :])
                            ins.then_inc(sem["a_tr7"], 1)
                            ctx.inc("a_tr7", 1)
                        te.wait_ge(sem["a_ye"], (ch + 1) * NH)
                        te.wait_ge(sem["a_ep"], (ch + 1) * NH)
                        te.wait_ge(sem["a_ynT_ev"], ch + 1)
                        for half in range(2):
                            hi = ch * 2 + half
                            if hi > 0:
                                te.wait_ge(sem["a_wo"], 16 * hi)
                            for k2h in range(8):
                                k2 = half * 8 + k2h
                                stat = ynT[:, k2 * P:(k2 + 1) * P]
                                for nb in range(2):
                                    ins = te.matmul(pars[:, 2048 + nb * 512:
                                                         2048 + (nb + 1) * 512],
                                                    stat,
                                                    woutb[:, k2h * D + nb * 512:
                                                          k2h * D + (nb + 1) * 512],
                                                    start=(k2 == 0), stop=(k2 == 15))
                            ins.then_inc(sem["a_oph"], 1)
                            ctx.inc("a_oph", 1)

            @blk.scalar
            def _(sc):
                for sp in range(NSP2):
                    for j in range(NJ):
                        jj = sp * NJ + j
                        sc.wait_ge(sem["a_cvacc"], jj + 1)
                        if jj >= 1:
                            sc.wait_ge(sem["a_cv"], jj)
                        sc.activation(sgc[:, :], cvt[:, :], AF.Sigmoid
                                      ).then_inc(sem["a_sg"], 1)
                        ctx.inc("a_sg", 1)
                    for cl in range(CPS):
                        ch = sp * CPS + cl
                        sc.wait_ge(sem["a_dtA"], ch + 1)
                        sc.activation(tC[:, :], tA[:, :], AF.Abs).then_inc(sem["a_dsc"], 1)
                        ctx.inc("a_dsc", 1)
                        sc.wait_ge(sem["a_dsc"], ctx.cnt["a_dsc"])
                        sc.activation(tC[:, :], tC[:, :], AF.Exp, scale=-1.0
                                      ).then_inc(sem["a_dsc"], 1)
                        ctx.inc("a_dsc", 1)
                        plans_dt[ch].emit_scalar(sc)
                        sc.wait_ge(sem["a_dsc"], ctx.cnt["a_dsc"] + 0)
                        sc.activation(tC[:, :], tA[:, :], AF.Relu).then_inc(sem["a_dsc"], 1)
                        ctx.inc("a_dsc", 1)
                        sc.wait_ge(sem["a_csd"], 2 * ch + 1)
                        sc.activation(einb[:, :], acsT[:, :], AF.Exp
                                      ).then_inc(sem["a_efac"], 1)
                        ctx.inc("a_efac", 1)
                        sc.wait_ge(sem["a_csd"], 2 * ch + 2)
                        sc.activation(etbv[:, :], tB[:, :], AF.Exp).then_inc(sem["a_efac"], 1)
                        ctx.inc("a_efac", 1)
                        sc.wait_ge(sem["a_dsub"], ch + 1)
                        sc.activation(dsdt[:, :], dsdt[:, :], AF.Exp
                                      ).then_inc(sem["a_efac"], 1)
                        ctx.inc("a_efac", 1)
                        for h in range(NH):
                            hp = h % 2
                            hidx = ch * NH + h
                            sc.wait_ge(sem["a_dmk"], hidx + 1)
                            if h >= 1:
                                sc.wait_ge(sem["a_mt"], hidx)
                            sc.activation(expdt[:, :], dmm[:, :], AF.Exp
                                          ).then_inc(sem["a_ex2"], 1)
                            ctx.inc("a_ex2", 1)
                            sc.wait_ge(sem["a_yo_mm"], hidx + 1)
                            if h >= 1:
                                sc.wait_ge(sem["a_ye"], hidx)
                            sc.activation(epart[:, :],
                                          pars[:, 2048 + hp * 512:2048 + hp * 512 + HD],
                                          AF.Copy, scale=einb[:, h:h + 1]
                                          ).then_inc(sem["a_ep"], 1)
                            ctx.inc("a_ep", 1)
                        sc.wait_ge(sem["a_zg1"], ch + 1)
                        sc.activation(zsb[:, :], zsb[:, :], AF.Sigmoid
                                      ).then_inc(sem["a_sgz"], 1)
                        ctx.inc("a_sgz", 1)
                        sc.wait_ge(sem["a_gat"], ch + 1)
                        sc.activation(ynb[:, :], ych[:, :], AF.Square,
                                      accum_out=s2c[:, :]).then_inc(sem["a_sq"], 1)
                        ctx.inc("a_sq", 1)
                        plans_rm[ch].emit_scalar(sc)
                        plans_po[ch].emit_scalar(sc)

            @blk.vector
            def _(v):
                for sp in range(NSP2):
                    xb = xbc
                    for t in range(CPS):
                        g = sp * CPS + t
                        v.wait_ge(sem["a2x" if g % 2 == 0 else "a2y"],
                                  16 * (g // 2 + 1))
                        v.wait_ge(sem["a_tr"], g)
                        v.tensor_scalar(lnst[:, :], xsp[g % 2][:, :],
                                        a1[:, g:g + 1], c1[:, g:g + 1],
                                        ALU.mult, ALU.add).then_inc(sem["a_ln"], 1)
                        ctx.inc("a_ln", 1)
                        v.wait_ge(sem["a_tr"], g + 1)
                        pcb = (t % 2) * 1024
                        v.tensor_copy(
                            bass.AP(lnTs, t * P, [[KD * SPAN, P], [SPAN, KD], [1, P]]),
                            parsb[:, pcb:pcb + KD * P]).then_inc(sem["a_lnT_ev"], 1)
                        ctx.inc("a_lnT_ev", 1)
                    for j in range(NJ):
                        jj = sp * NJ + j
                        pc = 2048 + (j % 3) * 512
                        v.wait_ge(sem["a_xb_mm"], jj + 1)
                        if sp % 4 == 0:
                            ctx.mark(v.memset(xb[:, j * JW:j * JW + HALO], 0.0))
                        else:
                            ctx.mark(v.tensor_copy(
                                xb[:, j * JW:j * JW + HALO],
                                xtail[:, j * HALO:(j + 1) * HALO]))
                        ctx.mark(v.tensor_copy(xb[:, j * JW + HALO:j * JW + JW],
                                               pars[:, pc:pc + 512]))
                        v.wait_ge(sem["vch"], ctx.cnt["vch"])
                        v.engine_nop().then_inc(sem["a_xb_ev"], 1)
                        ctx.inc("a_xb_ev", 1)
                        ctx.mark(v.tensor_copy(xtail[:, j * HALO:(j + 1) * HALO],
                                               xb[:, j * JW + SPAN:j * JW + JW]))
                        v.wait_ge(sem["vch"], ctx.cnt["vch"])
                        if jj >= 1:
                            v.wait_ge(sem["a_sg"], jj)
                        ctx.mark(v.tensor_scalar(cvt[:, :],
                                                 xb[:, j * JW + 1:j * JW + 1 + SPAN],
                                                 cwb[:, 4 * j:4 * j + 1], None,
                                                 ALU.mult))
                        for i in (1, 2, 3):
                            v.wait_ge(sem["vch"], ctx.cnt["vch"])
                            ctx.mark(v.scalar_tensor_tensor(
                                cvt[:, :], xb[:, j * JW + 1 + i:j * JW + 1 + i + SPAN],
                                cwb[:, 4 * j + i:4 * j + i + 1], cvt[:, :],
                                ALU.mult, ALU.add))
                        v.wait_ge(sem["vch"], ctx.cnt["vch"])
                        ctx.mark(v.tensor_scalar(cvt[:, :], cvt[:, :],
                                                 cbb[:, j:j + 1], None, ALU.add))
                        v.wait_ge(sem["vch"], ctx.cnt["vch"])
                        v.engine_nop().then_inc(sem["a_cvacc"], 1)
                        ctx.inc("a_cvacc", 1)
                        v.wait_ge(sem["a_sg"], jj + 1)
                        v.tensor_mul(xb[:, j * JW + HALO:j * JW + JW],
                                     cvt[:, :], sgc[:, :]).then_inc(sem["a_cv"], 1)
                        ctx.inc("a_cv", 1)
                    for cl in range(CPS):
                        ch = sp * CPS + cl
                        co = HALO + cl * P
                        v.wait_ge(sem["a_dt_mm"], ch + 1)
                        if ch >= 1:
                            v.wait_ge(sem["a_dsc"], 3 * ch)
                        ctx.mark(v.tensor_copy(tA[:, :], pars[:, 3584:3616]))
                        v.wait_ge(sem["vch"], ctx.cnt["vch"])
                        ctx.mark(v.tensor_add(tA[:, :], tA[:, :], dtbb[:, :]))
                        v.wait_ge(sem["vch"], ctx.cnt["vch"])
                        v.engine_nop().then_inc(sem["a_dtA"], 1)
                        ctx.inc("a_dtA", 1)
                        v.engine_nop().then_inc(sem["a_b7"], 1)
                        ctx.inc("a_b7", 1)
                        v.wait_ge(sem["a_dsc"], 2 + 3 * ch)
                        ctx.mark(v.tensor_scalar_add(tB[:, :], tC[:, :], 1.0))
                        v.wait_ge(sem["vch"], ctx.cnt["vch"])
                        pld = plans_dt[ch]
                        pld.emit_vector(v)
                        v.wait_ge(sem["r_s"], pld.done_val)
                        v.wait_ge(sem["a_dsc"], 3 + 3 * ch)
                        ctx.mark(v.tensor_add(dtc[:, :], lnp[:, :], tC[:, :]))
                        v.wait_ge(sem["vch"], ctx.cnt["vch"])
                        ctx.mark(v.tensor_mul(act_[:, :], dtc[:, :], ab[:, :]))
                        v.wait_ge(sem["vch"], ctx.cnt["vch"])
                        v.engine_nop().then_inc(sem["a_pc"], 1)
                        ctx.inc("a_pc", 1)
                        v.wait_ge(sem["a_cs_mm"], ch + 1)
                        ctx.mark(v.tensor_copy(acsT[:, :], pars[:, 3616:3648]))
                        ctx.mark(v.tensor_copy(acshm[:, :],
                                               bass.AP(pars, 3648, [[4096, 32], [1, P]])))
                        ctx.mark(v.tensor_copy(tB[:, :], pars[:, 3776:3808]))
                        v.wait_ge(sem["vch"], ctx.cnt["vch"])
                        v.engine_nop().then_inc(sem["a_b7"], 1)
                        ctx.inc("a_b7", 1)
                        ctx.mark(v.tensor_scalar_mul(nacsT[:, :], acsT[:, :], -1.0))
                        v.wait_ge(sem["vch"], ctx.cnt["vch"])
                        v.engine_nop().then_inc(sem["a_csd"], 1)
                        ctx.inc("a_csd", 1)
                        v.engine_nop().then_inc(sem["a_csd"], 1)
                        ctx.inc("a_csd", 1)
                        ctx.mark(v.tensor_sub(dsdt[:, :], tB[:, :], acsT[:, :]))
                        v.wait_ge(sem["vch"], ctx.cnt["vch"])
                        v.engine_nop().then_inc(sem["a_dsub"], 1)
                        ctx.inc("a_dsub", 1)
                        v.wait_ge(sem["a_efac"], 3 * ch + 3)
                        ctx.mark(v.tensor_mul(dsdt[:, :], dsdt[:, :], dtc[:, :]))
                        v.wait_ge(sem["vch"], ctx.cnt["vch"])
                        v.engine_nop().then_inc(sem["a_dsr"], 1)
                        ctx.inc("a_dsr", 1)
                        v.wait_ge(sem["a_z_mm"], ch + 1)
                        if ch >= 1:
                            v.wait_ge(sem["a_gat"], ch)
                        ctx.mark(v.tensor_copy(zsb[:, :], pars[:, 0:2048]))
                        v.wait_ge(sem["vch"], ctx.cnt["vch"])
                        v.engine_nop().then_inc(sem["a_z_ev"], 1)
                        ctx.inc("a_z_ev", 1)
                        for grp in range(2):
                            v.wait_ge(sem["a_tr7"], ch * 4 + grp + 1)
                            ctx.mark(v.tensor_copy(
                                x_tm[:, grp * 8 * P:(grp + 1) * 8 * P],
                                parsb[:, 7168:7168 + 8 * P]))
                            v.wait_ge(sem["vch"], ctx.cnt["vch"])
                            v.engine_nop().then_inc(sem["a_b7"], 1)
                            ctx.inc("a_b7", 1)
                        v.wait_ge(sem["a_bt_ev"], ch + 1)
                        ctx.mark(v.tensor_copy(btm[:, :], parsb[:, 6144:6144 + P]))
                        v.wait_ge(sem["vch"], ctx.cnt["vch"])
                        v.engine_nop().then_inc(sem["a_btd"], 1)
                        ctx.inc("a_btd", 1)
                        v.wait_ge(sem["a_g_mm"], ch + 1)
                        ctx.mark(v.tensor_mul(gtm[:, :], pars[:, 0:P], triub[:, :]))
                        v.wait_ge(sem["vch"], ctx.cnt["vch"])
                        v.engine_nop().then_inc(sem["a_g_ev"], 1)
                        ctx.inc("a_g_ev", 1)
                        if ch % 16 == 0 and ch > 0:
                            ctx.mark(v.memset(S[:, :], 0.0))
                            v.wait_ge(sem["vch"], ctx.cnt["vch"])
                        for h in range(NH):
                            hp = h % 2
                            hidx = ch * NH + h
                            v.wait_ge(sem["a_r_mm"], hidx + 1)
                            if h >= 1:
                                v.wait_ge(sem["a_ex2"], hidx)
                            ctx.mark(v.scalar_tensor_tensor(
                                dmm[:, :], pars[:, hp * 512:hp * 512 + P],
                                nacsT[:, h:h + 1], triub[:, :], ALU.add, ALU.mult))
                            v.wait_ge(sem["vch"], ctx.cnt["vch"])
                            v.engine_nop().then_inc(sem["a_dmk"], 1)
                            ctx.inc("a_dmk", 1)
                            v.wait_ge(sem["a_ex2"], hidx + 1)
                            if h >= 1:
                                v.wait_ge(sem["a_yd_mm"], hidx)
                            ctx.mark(v.scalar_tensor_tensor(
                                mtb[:, :], expdt[:, :], dtc[:, h:h + 1], gtm[:, :],
                                ALU.mult, ALU.mult))
                            v.wait_ge(sem["vch"], ctx.cnt["vch"])
                            v.engine_nop().then_inc(sem["a_mt"], 1)
                            ctx.inc("a_mt", 1)
                            if h >= 1:
                                v.wait_ge(sem["a_yo_mm"], hidx)
                            ctx.mark(v.tensor_copy(sbf[:, :],
                                                   S[:, h * HD:(h + 1) * HD]))
                            v.wait_ge(sem["vch"], ctx.cnt["vch"])
                            v.engine_nop().then_inc(sem["a_sbf"], 1)
                            ctx.inc("a_sbf", 1)
                            if h >= 1:
                                v.wait_ge(sem["a_sc_mm"], hidx)
                            v.wait_ge(sem["a_btd"], ch + 1)
                            v.wait_ge(sem["a_dsr"], ch + 1)
                            ctx.mark(v.tensor_scalar(bsc[:, :], btm[:, :],
                                                     dsdt[:, h:h + 1], None,
                                                     ALU.mult))
                            v.wait_ge(sem["vch"], ctx.cnt["vch"])
                            v.engine_nop().then_inc(sem["a_bt_rdy"], 1)
                            ctx.inc("a_bt_rdy", 1)
                            v.wait_ge(sem["a_sc_mm"], hidx + 1)
                            ctx.mark(v.tensor_copy(
                                spr[:, :], bass.AP(pars, 3072, [[4096, HD], [1, P]])))
                            v.wait_ge(sem["vch"], ctx.cnt["vch"])
                            v.engine_nop().then_inc(sem["a_sp_ev"], 1)
                            ctx.inc("a_sp_ev", 1)
                            v.wait_ge(sem["a_ep"], hidx + 1)
                            ctx.mark(v.scalar_tensor_tensor(
                                t2b[:, :], x_tm[:, h * HD:(h + 1) * HD],
                                dpb[:, h:h + 1], epart[:, :], ALU.mult, ALU.add))
                            v.wait_ge(sem["vch"], ctx.cnt["vch"])
                            v.wait_ge(sem["a_yd_mm"], hidx + 1)
                            v.tensor_add(ych[:, h * HD:(h + 1) * HD], t2b[:, :],
                                         pars[:, 1024 + hp * 512:1024 + hp * 512 + HD]
                                         ).then_inc(sem["a_ye"], 1)
                            ctx.inc("a_ye", 1)
                            v.wait_ge(sem["a_st_mm"], hidx + 1)
                            v.wait_ge(sem["a_efac"], 3 * ch + 3)
                            ctx.mark(v.scalar_tensor_tensor(
                                S[:, h * HD:(h + 1) * HD], S[:, h * HD:(h + 1) * HD],
                                etbv[:, h:h + 1], parsb[:, 6400:6400 + HD],
                                ALU.mult, ALU.add))
                            v.wait_ge(sem["vch"], ctx.cnt["vch"])
                            v.engine_nop().then_inc(sem["a_su"], 1)
                            ctx.inc("a_su", 1)
                        ctx.mark(v.tensor_mul(ych[:, :], ych[:, :], zsb[:, :]))
                        v.wait_ge(sem["vch"], ctx.cnt["vch"])
                        v.engine_nop().then_inc(sem["a_zg1"], 1)
                        ctx.inc("a_zg1", 1)
                        v.wait_ge(sem["a_sgz"], ch + 1)
                        ctx.mark(v.tensor_mul(ych[:, :], ych[:, :], zsb[:, :]))
                        v.wait_ge(sem["vch"], ctx.cnt["vch"])
                        v.engine_nop().then_inc(sem["a_gat"], 1)
                        ctx.inc("a_gat", 1)
                        v.engine_nop().then_inc(sem["a_zg"], 1)
                        ctx.inc("a_zg", 1)
                        v.wait_ge(sem["a_sq"], ch + 1)
                        ctx.mark(v.tensor_scalar(s2c[:, :], s2c[:, :], 1.0 / DI,
                                                 EPS, ALU.mult, ALU.add))
                        v.wait_ge(sem["vch"], ctx.cnt["vch"])
                        plr = plans_rm[ch]
                        plr.emit_vector(v)
                        v.wait_ge(sem["r_s"], plr.done_val)
                        ctx.mark(v.tensor_scalar(ych[:, :], ych[:, :],
                                                 rstdc[:, 0:1], None, ALU.mult))
                        v.wait_ge(sem["vch"], ctx.cnt["vch"])
                        ctx.mark(v.tensor_mul(ych[:, :], ych[:, :], nwb[:, :]))
                        v.wait_ge(sem["vch"], ctx.cnt["vch"])
                        if ch >= 1:
                            v.wait_ge(sem["a_ynT_ev"], ch)
                        ctx.mark(v.tensor_copy(ynb[:, :], ych[:, :]))
                        v.wait_ge(sem["vch"], ctx.cnt["vch"])
                        v.engine_nop().then_inc(sem["a_yn"], 1)
                        ctx.inc("a_yn", 1)
                        for grp in range(2):
                            v.wait_ge(sem["a_tr7"], ch * 4 + 2 + grp + 1)
                            ctx.mark(v.tensor_copy(
                                ynT[:, grp * 8 * P:(grp + 1) * 8 * P],
                                parsb[:, 7168:7168 + 8 * P]))
                            v.wait_ge(sem["vch"], ctx.cnt["vch"])
                            v.engine_nop().then_inc(sem["a_b7"], 1)
                            ctx.inc("a_b7", 1)
                        v.engine_nop().then_inc(sem["a_ynT_ev"], 1)
                        ctx.inc("a_ynT_ev", 1)
                        v.wait_ge(sem["a_oph"], 2 * (ch + 1))
                        ctx.mark(v.tensor_copy(rsb[:, :], pars[:, 2048:3072]))
                        v.wait_ge(sem["vch"], ctx.cnt["vch"])
                        v.engine_nop().then_inc(sem["a_op_ev"], 1)
                        ctx.inc("a_op_ev", 1)
                        ctx.mark(v.reduce_sum(s1r[:, :], rsb[:, :], axis=AX.X))
                        ctx.mark(v.tensor_mul(x_tm.bitcast(F32)[:, 0:D],
                                              rsb[:, :], rsb[:, :]))
                        v.wait_ge(sem["vch"], ctx.cnt["vch"])
                        ctx.mark(v.reduce_sum(s2r[:, :], x_tm.bitcast(F32)[:, 0:D],
                                              axis=AX.X))
                        v.wait_ge(sem["vch"], ctx.cnt["vch"])
                        ctx.mark(v.tensor_scalar_mul(mur[:, :], s1r[:, :], 1.0 / D))
                        v.tensor_scalar_mul(s2r[:, :], s2r[:, :], 1.0 / D)
                        v.wait_ge(sem["vch"], ctx.cnt["vch"])
                        ctx.mark(v.tensor_mul(tc1[:, :], mur[:, :], mur[:, :]))
                        v.wait_ge(sem["vch"], ctx.cnt["vch"])
                        ctx.mark(v.tensor_sub(s2r[:, :], s2r[:, :], tc1[:, :]))
                        v.wait_ge(sem["vch"], ctx.cnt["vch"])
                        ctx.mark(v.tensor_scalar_add(s2r[:, :], s2r[:, :], EPS))
                        v.wait_ge(sem["vch"], ctx.cnt["vch"])
                        plp = plans_po[ch]
                        plp.emit_vector(v)
                        v.wait_ge(sem["r_s"], plp.done_val)
                        ctx.mark(v.tensor_mul(mur[:, :], mur[:, :], rstdr[:, :]))
                        v.wait_ge(sem["vch"], ctx.cnt["vch"])
                        ctx.mark(v.tensor_scalar_mul(mur[:, :], mur[:, :], -1.0))
                        v.wait_ge(sem["vch"], ctx.cnt["vch"])
                        if ch >= 2:
                            v.wait_ge(sem["a_st" if ch % 2 == 0 else "a_st2"],
                                      16 * ((ch - 2) // 2 + 1))
                        v.wait_ge(sem["a_xr" if ch % 2 == 0 else "a_xr2"],
                                  16 * (ch // 2 + 1))
                        ob = outst[ch % 2]
                        ctx.mark(v.tensor_scalar(ob[:, :], rsb[:, :],
                                                 rstdr[:, 0:1], mur[:, 0:1],
                                                 ALU.mult, ALU.add))
                        v.wait_ge(sem["vch"], ctx.cnt["vch"])
                        v.tensor_add(ob[:, :], ob[:, :], xres[ch % 2][:, :]
                                     ).then_inc(sem["a_res"], 1)
                        ctx.inc("a_res", 1)

        if getattr(ctx, "debug_l0a", False):
            with nc.Block() as blk:
                @blk.gpsimd
                def _(g):
                    for nm, buf in (("da_lnTs", lnTs), ("da_xbc", xbc), ("da_xtm", x_tm),
                                    ("da_zsb", zsb), ("da_ych", ych),
                                    ("da_dtc", dtc), ("da_acsT", acsT),
                                    ("da_S", S), ("da_ynT", ynT),
                                    ("da_rsb", rsb), ("da_dsdt", dsdt),
                                    ("da_einb", einb), ("da_etbv", etbv)):
                        g.dma_start(ctx.dram[nm].ap(), buf[:, :]).then_inc(
                            ctx.get_sem("a_dbg"), 16)
                        ctx.inc("a_dbg", 16)
                    g.wait_ge(sem["a_dbg"], ctx.cnt["a_dbg"])




def emit_l1_attn(ctx, x_in, x_out):
    nc = ctx.nc
    NB = 2
    SEQ = 2048
    PADT = 2112
    NST = 26
    NHD = 16
    HD = 64
    SC = 0.125
    SHIFT = 3.0

    sems = ("t1x", "t_dm", "t2x", "t_ln", "t_tr", "t_lnT_ev", "tw", "t_wq",
            "t_qk_mm", "t_qk_ev", "t_lsc", "t_v_mm", "t_v_ev", "t_d_mm",
            "t_ex", "t_av_mm", "t_rc", "t_ao_ev", "t_at_mm", "t_at_ev",
            "t_op_mm", "t_op_ev", "t_xr", "t_res", "t_st", "t_pz", "t_wo",
            "t_wq2")
    for s in sems:
        ctx.get_sem(s)
    sem = ctx.sem

    wqkvT = ctx.dram["l1a_wqkvT"]
    woutT = ctx.dram["l1a_woutT"]
    boutc = ctx.dram["l1a_bout"]

    xflat = x_in.ap()
    oflat = x_out.ap()
    xr = xflat.rearrange("(n p) d -> n p d", p=P)

    def nrows(w, pr):
        base = 40 * w + pr
        return max(0, min(20, (SEQ - base + 1) // 2))

    with contextlib.ExitStack() as st:
        E2 = st.enter_context
        ident = ctx.sb_ident
        pars = E2(nc.psum_tensor("t_pars", [P, 4096], F32))
        parsb = pars.bitcast(BF16)
        s1 = E2(nc.sbuf_tensor("t_s1", [P, NT], F32))
        s2 = E2(nc.sbuf_tensor("t_s2", [P, NT], F32))
        a1 = E2(nc.sbuf_tensor("t_a1", [P, NT], F32))
        c1 = E2(nc.sbuf_tensor("t_c1", [P, NT], F32))
        nz = [E2(nc.sbuf_tensor(f"t_nz{i}", [P, NT], F32)) for i in range(3)]
        xsp = [E2(nc.sbuf_tensor("t_xsp0", [P, D], F32))] * 2
        lnst = E2(nc.sbuf_tensor("t_lnst", [P, D], BF16))
        ln2T = E2(nc.sbuf_tensor("t_ln2T", [P, KD * PADT], BF16))
        qkT = E2(nc.sbuf_tensor("t_qkT", [P, 16 * PADT], BF16))
        vslot = E2(nc.sbuf_tensor("t_vslot", [P, NST * D], BF16))
        wqb = [E2(nc.sbuf_tensor(f"t_wqb{i}", [P, KD * P], BF16)) for i in range(2)]
        wob = E2(nc.sbuf_tensor("t_wob", [P, KD * D], BF16))
        wvb = wob
        bob = E2(nc.sbuf_tensor("t_bob", [P, D], BF16))
        lnslot = [E2(nc.sbuf_tensor("t_lnslot0", [P, KD * P], BF16))] * 2
        ones20 = E2(nc.sbuf_tensor("t_ones20", [P, 1], BF16))
        shcol = E2(nc.sbuf_tensor("t_shcol", [P, 1], F32))
        attnu = E2(nc.sbuf_tensor("t_attnu", [P, 2 * P], BF16))
        rcol = E2(nc.sbuf_tensor("t_rcol", [P, 1], F32))
        asoT = E2(nc.sbuf_tensor("t_asoT", [P, 8 * P], BF16))
        rsb = E2(nc.sbuf_tensor("t_rsb", [P, D], F32))
        s1b = E2(nc.sbuf_tensor("t_s1b", [P, 1], F32))
        s2b = E2(nc.sbuf_tensor("t_s2b", [P, 1], F32))
        mub = E2(nc.sbuf_tensor("t_mub", [P, 1], F32))
        rstdb = E2(nc.sbuf_tensor("t_rstdb", [P, 1], F32))
        zc1 = E2(nc.sbuf_tensor("t_zc1", [P, 1], F32))
        ec1 = E2(nc.sbuf_tensor("t_ec1", [P, 1], F32))
        tc1 = E2(nc.sbuf_tensor("t_tc1", [P, 1], F32))
        xres = [E2(nc.sbuf_tensor("t_xres0", [P, D], F32))] * 2
        outst = [E2(nc.sbuf_tensor("t_outst0", [P, D], F32))] * 2

        with nc.Block() as blk:
            @blk.gpsimd
            def _(g):
                g.dma_start(bob[:, :], boutc.ap()).then_inc(sem["tw"], 16)
                g.dma_start(wvb[:, :],
                            wqkvT.ap()[:, 2048:3072].rearrange("(k p) h -> p k h", p=P)
                            ).then_inc(sem["tw"], 16)
                ctx.inc("tw", 32)

            @blk.sync
            def _(sy):
                for g in range(NT):
                    if g >= 1:
                        sy.wait_ge(sem["t_dm"], 4 * g)
                    sy.dma_start(xsp[0][:, :], xr[g, :, :]
                                 ).then_inc(sem["t1x"], 16)
                    ctx.inc("t1x", 16)

            @blk.scalar
            def _(sc):
                for g in range(NT):
                    sc.wait_ge(sem["t1x"], 16 * (g + 1))
                    src = xsp[g % 2][:, :]
                    if g > 0:
                        sc.wait_ge(sem["t_dm"], 4 * g)
                    sc.activation(rsb[:, :], src, AF.Copy,
                                  accum_out=s1[:, g:g + 1]).then_inc(sem["t_dm"], 1)
                    ctx.inc("t_dm", 1)
                    sc.wait_ge(sem["t_dm"], ctx.cnt["t_dm"])
                    sc.activation(rsb[:, :], src, AF.Square,
                                  accum_out=s2[:, g:g + 1]).then_inc(sem["t_dm"], 3)
                    ctx.inc("t_dm", 3)

            @blk.vector
            def _(v):
                ctx.mark(v.memset(ones20[:, :], 1.0))
                ctx.mark(v.memset(shcol[:, :], -SHIFT))
                ctx.mark(v.memset(xres[0][:, :], 0.0))
                ctx.mark(v.memset(attnu[:, :], 0.0))
                v.wait_ge(sem["vch"], ctx.cnt["vch"])

        planT = RstdPlan(ctx, s2[:, :], a1[:, :], nz[0][:, :], nz[1][:, :],
                         nz[2][:, :])
        with nc.Block() as blk:
            @blk.vector
            def _(v):
                ctx.mark(v.tensor_scalar_mul(c1[:, :], s1[:, :], 1.0 / D))
                ctx.mark(v.tensor_scalar_mul(s2[:, :], s2[:, :], 1.0 / D))
                v.wait_ge(sem["vch"], ctx.cnt["vch"])
                ctx.mark(v.tensor_mul(nz[0][:, :], c1[:, :], c1[:, :]))
                v.wait_ge(sem["vch"], ctx.cnt["vch"])
                ctx.mark(v.tensor_sub(s2[:, :], s2[:, :], nz[0][:, :]))
                v.wait_ge(sem["vch"], ctx.cnt["vch"])
                ctx.mark(v.tensor_scalar_add(s2[:, :], s2[:, :], EPS))
                v.wait_ge(sem["vch"], ctx.cnt["vch"])
                planT.emit_vector(v)
                v.wait_ge(sem["r_s"], planT.done_val)
                ctx.mark(v.tensor_mul(c1[:, :], c1[:, :], a1[:, :]))
                v.wait_ge(sem["vch"], ctx.cnt["vch"])
                ctx.mark(v.tensor_scalar_mul(c1[:, :], c1[:, :], -1.0))
                v.wait_ge(sem["vch"], ctx.cnt["vch"])

            @blk.scalar
            def _(sc):
                planT.emit_scalar(sc)

        plans_t = [RstdPlan(ctx, s2b[:, :], rstdb[:, :], zc1[:, :], ec1[:, :],
                            tc1[:, :]) for _ in range(NB * NST)]

        def dbg_dump(names_bufs):
            if not getattr(ctx, "debug_l1a", False):
                return
            with nc.Block() as blk2:
                @blk2.gpsimd
                def _(g):
                    for nm, buf in names_bufs:
                        g.dma_start(ctx.dram[nm].ap(), buf[:, :]).then_inc(
                            ctx.get_sem("t_dbg"), 16)
                        ctx.inc("t_dbg", 16)
                    g.wait_ge(sem["t_dbg"], ctx.cnt["t_dbg"])

        for b in range(NB):
            with nc.Block() as blk:
                @blk.sync
                def _(sy):
                    for t in range(16):
                        g = b * 16 + t
                        if t >= 1:
                            sy.wait_ge(sem["t_ln"], b * 16 + t)
                        sy.dma_start(xsp[0][:, :], xr[g, :, :]
                                     ).then_inc(sem["t2x"], 16)
                        ctx.inc("t2x", 16)
                    for st_ in range(NST):
                        sti = b * NST + st_
                        if st_ >= 1:
                            sy.wait_ge(sem["t_res"], b * NST + st_)
                        for i in range(4):
                            w, pr = (4 * st_ + i) // 2, (4 * st_ + i) % 2
                            nr = nrows(w, pr)
                            off = (b * SEQ + 40 * w + pr)
                            sy.dma_start(
                                xres[st_ % 2][32 * i:32 * i + nr, :],
                                bass.AP(xflat.tensor, off * D, [[2 * D, nr], [1, D]])
                            ).then_inc(sem["t_xr"], 16)
                            ctx.inc("t_xr", 16)
                        sy.wait_ge(sem["t_res"], b * NST + st_ + 1)
                        for i in range(4):
                            w, pr = (4 * st_ + i) // 2, (4 * st_ + i) % 2
                            nr = nrows(w, pr)
                            off = (b * SEQ + 40 * w + pr)
                            sy.dma_start(
                                bass.AP(oflat.tensor, off * D, [[2 * D, nr], [1, D]]),
                                outst[st_ % 2][32 * i:32 * i + nr, :]
                            ).then_inc(sem["t_st"], 16)
                            ctx.inc("t_st", 16)
                    if b == NB - 1:
                        sy.wait_ge(sem["t_st"], ctx.cnt["t_st"])

                @blk.gpsimd
                def _(g):
                    for dt_ in range(16):
                        idx = b * 16 + dt_
                        if idx >= 2:
                            g.wait_ge(sem["t_qk_mm"], idx - 1)
                        g.dma_start(
                            wqb[idx % 2][:, :],
                            wqkvT.ap()[:, dt_ * P:(dt_ + 1) * P]
                                 .rearrange("(k p) h -> p k h", p=P)
                        ).then_inc(sem["t_wq" if idx % 2 == 0 else "t_wq2"], 16)
                        ctx.inc("t_wq" if idx % 2 == 0 else "t_wq2", 16)
                    g.wait_ge(sem["t_v_mm"], (b + 1) * NST)
                    g.dma_start(wob[:, :],
                                woutT.ap().rearrange("(k p) d -> p k d", p=P)
                                ).then_inc(sem["t_wo"], 16)
                    ctx.inc("t_wo", 16)
                    if b < NB - 1:
                        g.wait_ge(sem["t_op_mm"], (b + 1) * NST)
                        g.dma_start(wvb[:, :],
                                    wqkvT.ap()[:, 2048:3072]
                                    .rearrange("(k p) h -> p k h", p=P)
                                    ).then_inc(sem["t_wo"], 16)
                        ctx.inc("t_wo", 16)
                    if getattr(ctx, "debug_l1a", False) and b == 0:
                        g.wait_ge(sem["t_pz"], 1)
                        g.wait_ge(sem["t_lnT_ev"], 16)
                        g.dma_start(ctx.dram["dt_ln2Te"].ap(), ln2T[:, :]
                                    ).then_inc(ctx.get_sem("t_dbg"), 16)
                        ctx.inc("t_dbg", 16)
                        g.wait_ge(sem["t_qk_mm"], 16)
                        g.dma_start(ctx.dram["dt_ln2Tq"].ap(), ln2T[:, :]
                                    ).then_inc(sem["t_dbg"], 16)
                        ctx.inc("t_dbg", 16)
                        g.wait_ge(sem["t_v_ev"], 26)
                        g.dma_start(ctx.dram["dt_ln2Tv"].ap(), ln2T[:, :]
                                    ).then_inc(sem["t_dbg"], 16)
                        ctx.inc("t_dbg", 16)
                        g.wait_ge(sem["t_av_mm"], 26 * 16)
                        g.dma_start(ctx.dram["dt_ln2Ta"].ap(), ln2T[:, :]
                                    ).then_inc(sem["t_dbg"], 16)
                        ctx.inc("t_dbg", 16)

                @blk.tensor
                def _(te):
                    te.wait_ge(sem["tw"], 32)
                    for t in range(16):
                        g = b * 16 + t
                        te.wait_ge(sem["t_ln"], g + 1)
                        if g >= 2:
                            te.wait_ge(sem["t_lnT_ev"], g - 1)
                        if b > 0 and t == 0:
                            te.wait_ge(sem["t_op_ev"], b * NST)
                        pcb = (t % 2) * 1024
                        for k in range(KD):
                            ins = te.transpose(parsb[:, pcb + k * P:pcb + (k + 1) * P],
                                               lnst[:, k * P:(k + 1) * P],
                                               ident[:, :])
                        ins.then_inc(sem["t_tr"], 1)
                        ctx.inc("t_tr", 1)
                    for dt_ in range(16):
                        idx = b * 16 + dt_
                        te.wait_ge(sem["t_wq" if idx % 2 == 0 else "t_wq2"],
                                   16 * (idx // 2 + 1))
                        if dt_ == 0:
                            te.wait_ge(sem["t_lnT_ev"], (b + 1) * 16)
                            te.wait_ge(sem["t_pz"], b + 1)
                        if idx >= 1:
                            te.wait_ge(sem["t_qk_ev"], 5 * idx)
                        for k in range(KD):
                            wk = wqb[idx % 2][:, k * P:(k + 1) * P]
                            for tb in range(5):
                                nn = 512 if tb < 4 else PADT - 2048
                                ins = te.matmul(
                                    pars[:, tb * 512:tb * 512 + nn], wk,
                                    ln2T[:, k * PADT + tb * 512:k * PADT + tb * 512 + nn],
                                    start=(k == 0), stop=(k == KD - 1))
                        ins.then_inc(sem["t_qk_mm"], 1)
                        ctx.inc("t_qk_mm", 1)
                    for st_ in range(NST):
                        sti = b * NST + st_
                        te.wait_ge(sem["t_lsc"], sti + 1)
                        if b > 0:
                            te.wait_ge(sem["t_wo"], 32 * b)
                        if sti >= 2:
                            te.wait_ge(sem["t_v_ev"], sti - 1)
                        if st_ % 2 == 1 and st_ == 1:
                            te.wait_ge(sem["t_qk_ev"], (b * 16 + 16) * 5)
                        pc = 2560 if st_ % 2 == 0 else 0
                        for k in range(KD):
                            for nb2 in range(2):
                                ins = te.matmul(
                                    pars[:, pc + nb2 * 512:pc + (nb2 + 1) * 512],
                                    lnslot[sti % 2][:, k * P:(k + 1) * P],
                                    wvb[:, k * D + nb2 * 512:k * D + (nb2 + 1) * 512],
                                    start=(k == 0), stop=(k == KD - 1))
                        ins.then_inc(sem["t_v_mm"], 1)
                        ctx.inc("t_v_mm", 1)
                    for st_ in range(NST):
                        for h in range(NHD):
                            ai = (b * NST + st_) * NHD + h
                            par = ai % 2
                            if st_ == 0 and h == 0:
                                te.wait_ge(sem["t_v_ev"], (b + 1) * NST)
                            if ai >= 2:
                                te.wait_ge(sem["t_ex"], ai - 1)
                            qd = h // 2
                            qp = (h % 2) * 64
                            kd_ = 8 + h // 2
                            for i in range(4):
                                w, pr = (4 * st_ + i) // 2, (4 * st_ + i) % 2
                                col = 40 * w + pr
                                ins = te.matmul(
                                    pars[32 * i:32 * i + 20,
                                         par * 512 + 32 * i:par * 512 + 32 * i + 20],
                                    qkT[qp:qp + 64,
                                        kd_ * PADT + col:kd_ * PADT + col + 40:2],
                                    qkT[qp:qp + 64,
                                        qd * PADT + col:qd * PADT + col + 40:2],
                                    start=True, stop=True,
                                    tile_position=(qp, 32 * i))
                                if i == 3:
                                    ins.then_inc(sem["t_d_mm"], 1)
                            ctx.inc("t_d_mm", 1)
                            te.wait_ge(sem["t_ex"], ai + 1)
                            if st_ == 0 and h == 0:
                                te.wait_ge(sem["t_qk_ev"], 80 * (b + 1))
                            if ai >= 2:
                                te.wait_ge(sem["t_ao_ev"], ai - 1)
                            avc = 1024 + par * 512
                            te.matmul(pars[:, avc:avc + HD],
                                      attnu[:, par * P:(par + 1) * P],
                                      vslot[:, st_ * D + h * HD:st_ * D + (h + 1) * HD],
                                      start=True, stop=True)
                            ins = te.matmul(pars[:, avc + HD:avc + HD + 1],
                                            attnu[:, par * P:(par + 1) * P],
                                            ones20[:, :],
                                            start=True, stop=True)
                            ins.then_inc(sem["t_av_mm"], 1)
                            ctx.inc("t_av_mm", 1)
                    for st_ in range(NST):
                        sti = b * NST + st_
                        te.wait_ge(sem["t_ao_ev"], (b * NST + st_ + 1) * NHD
                                   if False else (b * NST + st_) * NHD + NHD)
                        if sti >= 2:
                            te.wait_ge(sem["t_at_ev"], sti - 1)
                        pcb = 4096 + (st_ % 2) * 1024
                        for k in range(KD):
                            ins = te.transpose(
                                parsb[:, pcb + k * P:pcb + (k + 1) * P],
                                vslot[:, st_ * D + k * P:st_ * D + (k + 1) * P],
                                ident[:, :])
                        ins.then_inc(sem["t_at_mm"], 1)
                        ctx.inc("t_at_mm", 1)
                        te.wait_ge(sem["t_at_ev"], sti + 1)
                        te.wait_ge(sem["t_wo"], 32 * b + 16)
                        if sti >= 1:
                            te.wait_ge(sem["t_op_ev"], sti)
                        for k in range(KD):
                            for nb2 in range(2):
                                ins = te.matmul(
                                    pars[:, 3072 + nb2 * 512:3072 + (nb2 + 1) * 512],
                                    asoT[:, k * P:(k + 1) * P],
                                    wob[:, k * D + nb2 * 512:k * D + (nb2 + 1) * 512],
                                    start=(k == 0), stop=(k == KD - 1))
                        ins.then_inc(sem["t_op_mm"], 1)
                        ctx.inc("t_op_mm", 1)

                @blk.scalar
                def _(sc):
                    for st_ in range(NST):
                        for h in range(NHD):
                            ai = (b * NST + st_) * NHD + h
                            par = ai % 2
                            sc.wait_ge(sem["t_d_mm"], ai + 1)
                            if ai >= 2:
                                sc.wait_ge(sem["t_av_mm"], ai - 1)
                            for i in range(4):
                                ins = sc.activation(
                                    attnu[32 * i:32 * i + 20,
                                          par * P + 32 * i:par * P + 32 * i + 20],
                                    pars[32 * i:32 * i + 20,
                                         par * 512 + 32 * i:par * 512 + 32 * i + 20],
                                    AF.Exp, scale=SC, bias=shcol[32 * i:32 * i + 20, 0:1])
                            ins.then_inc(sem["t_ex"], 1)
                            ctx.inc("t_ex", 1)
                    for st_ in range(NST):
                        plans_t[b * NST + st_].emit_scalar(sc)

                @blk.vector
                def _(v):
                    for t in range(16):
                        g = b * 16 + t
                        v.wait_ge(sem["t2x"], 16 * (g + 1))
                        v.wait_ge(sem["t_tr"], g)
                        v.tensor_scalar(lnst[:, :], xsp[t % 2][:, :],
                                        a1[:, g:g + 1], c1[:, g:g + 1],
                                        ALU.mult, ALU.add).then_inc(sem["t_ln"], 1)
                        ctx.inc("t_ln", 1)
                        v.wait_ge(sem["t_tr"], g + 1)
                        pcb = (t % 2) * 1024
                        for kk in range(KD):
                            ins = v.tensor_copy(
                                ln2T[:, kk * PADT + t * P:kk * PADT + (t + 1) * P],
                                parsb[:, pcb + kk * P:pcb + (kk + 1) * P])
                        ins.then_inc(sem["t_lnT_ev"], 1)
                        ctx.inc("t_lnT_ev", 1)
                    for kk in range(KD):
                        ctx.mark(v.memset(
                            ln2T[:, kk * PADT + SEQ:(kk + 1) * PADT], 0.0))
                    v.wait_ge(sem["vch"], ctx.cnt["vch"])
                    v.engine_nop().then_inc(sem["t_pz"], 1)
                    ctx.inc("t_pz", 1)
                    for dt_ in range(16):
                        idx = b * 16 + dt_
                        v.wait_ge(sem["t_qk_mm"], idx + 1)
                        for tb in range(5):
                            nn = 512 if tb < 4 else PADT - 2048
                            ctx.mark(v.tensor_copy(
                                qkT[:, dt_ * PADT + tb * 512:dt_ * PADT + tb * 512 + nn],
                                pars[:, tb * 512:tb * 512 + nn]))
                            v.wait_ge(sem["vch"], ctx.cnt["vch"])
                            v.engine_nop().then_inc(sem["t_qk_ev"], 1)
                            ctx.inc("t_qk_ev", 1)
                    for st_ in range(NST):
                        sti = b * NST + st_
                        if sti >= 2:
                            v.wait_ge(sem["t_v_mm"], sti - 1)
                        w0 = 2 * st_
                        for k in range(KD):
                            ctx.mark(v.tensor_copy(
                                lnslot[sti % 2][:, k * P:(k + 1) * P],
                                bass.AP(ln2T, k * PADT + 40 * w0,
                                        [[KD * PADT, P], [40, 2], [1, 2], [2, 32]])))
                        v.wait_ge(sem["vch"], ctx.cnt["vch"])
                        v.engine_nop().then_inc(sem["t_lsc"], 1)
                        ctx.inc("t_lsc", 1)
                        v.wait_ge(sem["t_v_mm"], sti + 1)
                        pc = 2560 if st_ % 2 == 0 else 0
                        ctx.mark(v.tensor_copy(vslot[:, st_ * D:(st_ + 1) * D],
                                               pars[:, pc:pc + 1024]))
                        v.wait_ge(sem["vch"], ctx.cnt["vch"])
                        v.engine_nop().then_inc(sem["t_v_ev"], 1)
                        ctx.inc("t_v_ev", 1)
                    for st_ in range(NST):
                        for h in range(NHD):
                            ai = (b * NST + st_) * NHD + h
                            par = ai % 2
                            avc = 1024 + par * 512
                            v.wait_ge(sem["t_av_mm"], ai + 1)
                            v.wait_ge(sem["t_ao_ev"], ctx.cnt["t_ao_ev"])
                            ctx.mark(v.tensor_scalar_add(
                                rcol[:, :], pars[:, avc + HD:avc + HD + 1], 1e-30))
                            v.wait_ge(sem["vch"], ctx.cnt["vch"])
                            ctx.mark(v.reciprocal(rcol[:, :], rcol[:, :]))
                            v.wait_ge(sem["vch"], ctx.cnt["vch"])
                            v.tensor_scalar(vslot[:, st_ * D + h * HD:
                                                  st_ * D + (h + 1) * HD],
                                            pars[:, avc:avc + HD],
                                            rcol[:, 0:1], None, ALU.mult
                                            ).then_inc(sem["t_ao_ev"], 1)
                            ctx.inc("t_ao_ev", 1)
                    for st_ in range(NST):
                        sti = b * NST + st_
                        v.wait_ge(sem["t_at_mm"], sti + 1)
                        pcb = 4096 + (st_ % 2) * 1024
                        ctx.mark(v.tensor_copy(asoT[:, :], parsb[:, pcb:pcb + 8 * P]))
                        v.wait_ge(sem["vch"], ctx.cnt["vch"])
                        v.engine_nop().then_inc(sem["t_at_ev"], 1)
                        ctx.inc("t_at_ev", 1)
                        v.wait_ge(sem["t_op_mm"], sti + 1)
                        ctx.mark(v.tensor_add(rsb[:, :], pars[:, 3072:4096],
                                              bob[:, :]))
                        v.wait_ge(sem["vch"], ctx.cnt["vch"])
                        v.engine_nop().then_inc(sem["t_op_ev"], 1)
                        ctx.inc("t_op_ev", 1)
                        ctx.mark(v.reduce_sum(s1b[:, :], rsb[:, :], axis=AX.X))
                        ctx.mark(v.tensor_mul(xsp[0][:, :], rsb[:, :], rsb[:, :]))
                        v.wait_ge(sem["vch"], ctx.cnt["vch"])
                        ctx.mark(v.reduce_sum(s2b[:, :], xsp[0][:, :], axis=AX.X))
                        v.wait_ge(sem["vch"], ctx.cnt["vch"])
                        ctx.mark(v.tensor_scalar_mul(mub[:, :], s1b[:, :], 1.0 / D))
                        v.tensor_scalar_mul(s2b[:, :], s2b[:, :], 1.0 / D)
                        v.wait_ge(sem["vch"], ctx.cnt["vch"])
                        ctx.mark(v.tensor_mul(tc1[:, :], mub[:, :], mub[:, :]))
                        v.wait_ge(sem["vch"], ctx.cnt["vch"])
                        ctx.mark(v.tensor_sub(s2b[:, :], s2b[:, :], tc1[:, :]))
                        v.wait_ge(sem["vch"], ctx.cnt["vch"])
                        ctx.mark(v.tensor_scalar_add(s2b[:, :], s2b[:, :], EPS))
                        v.wait_ge(sem["vch"], ctx.cnt["vch"])
                        pl = plans_t[sti]
                        pl.emit_vector(v)
                        v.wait_ge(sem["r_s"], pl.done_val)
                        ctx.mark(v.tensor_mul(mub[:, :], mub[:, :], rstdb[:, :]))
                        v.wait_ge(sem["vch"], ctx.cnt["vch"])
                        ctx.mark(v.tensor_scalar_mul(mub[:, :], mub[:, :], -1.0))
                        v.wait_ge(sem["vch"], ctx.cnt["vch"])
                        if sti >= 1:
                            v.wait_ge(sem["t_st"], 64 * sti)
                        v.wait_ge(sem["t_xr"], 64 * (sti + 1))
                        ob = outst[st_ % 2]
                        ctx.mark(v.tensor_scalar(ob[:, :], rsb[:, :],
                                                 rstdb[:, 0:1], mub[:, 0:1],
                                                 ALU.mult, ALU.add))
                        v.wait_ge(sem["vch"], ctx.cnt["vch"])
                        v.tensor_add(ob[:, :], ob[:, :], xres[st_ % 2][:, :]
                                     ).then_inc(sem["t_res"], 1)
                        ctx.inc("t_res", 1)
            if b == 0:
                dbg_dump((("dt_ln2T", ln2T), ("dt_qkT", qkT), ("dt_vslot", vslot)))



def emit_l1_ffn(ctx, x_in, x_out):
    nc = ctx.nc
    I = 2048
    NHT = I // P
    NE = 8
    NX = 9
    sems = ("m1x", "m_dm", "m2x", "m_hs", "m_dm2", "m3x", "m_ln", "m_tr",
            "m_lnT_ev", "m_rtw", "m_rt_mm", "m_rt_ev", "m_wq", "m_wt_ev",
            "m_wb_mm", "m_wb_ev", "m_xe", "mw", "m_gu_mm", "m_th", "m_gu_ev",
            "m_hm_st", "m_hm_st2", "m_hm_ld", "m_hm_ld2", "m_dw", "m_dw2",
            "mw2", "m_dn_mm", "m_dn_ev", "m_xr2",
            "m_xr3", "m_xr4", "m_xr5", "m_res", "m_st", "m_st2", "m_sc2",
            "m_q", "m_u")
    for s in sems:
        ctx.get_sem(s)
    sem = ctx.sem

    gw = ctx.dram["l1f_gate"]
    uw = ctx.dram["l1f_up"]
    dw = ctx.dram["l1f_down"]
    shg = ctx.dram["l1f_shg"]
    shu = ctx.dram["l1f_shu"]
    shd = ctx.dram["l1f_shd"]
    rw = ctx.dram["l1f_router"]
    lnw = ctx.dram["l1f_lnw"]
    lnb = ctx.dram["l1f_lnb"]
    e8 = ctx.dram["l1f_e8"]
    hm_s = ctx.dram["l1f_hms"]

    xr = x_in.ap().rearrange("(n p) d -> n p d", p=P)
    xo = x_out.ap().rearrange("(n p) d -> n p d", p=P)

    with contextlib.ExitStack() as st:
        E2 = st.enter_context
        ident = ctx.sb_ident
        stP = contextlib.ExitStack()
        EP = stP.enter_context
        s1 = E2(nc.sbuf_tensor("m_s1", [P, NT], F32))
        s2 = E2(nc.sbuf_tensor("m_s2", [P, NT], F32))
        a1 = E2(nc.sbuf_tensor("m_a1", [P, NT], F32))
        c1 = E2(nc.sbuf_tensor("m_c1", [P, NT], F32))
        a2 = E2(nc.sbuf_tensor("m_a2", [P, NT], F32))
        c2 = E2(nc.sbuf_tensor("m_c2", [P, NT], F32))
        nz = [E2(nc.sbuf_tensor(f"m_nz{i}", [P, NT], F32)) for i in range(3)]
        dump = E2(nc.sbuf_tensor("m_dump", [P, D], F32))
        pars = E2(nc.psum_tensor("m_pars", [P, 4096], F32))
        parsb = pars.bitcast(BF16)
        xsp = EP(nc.sbuf_tensor("m_xsp", [P, 8 * D], F32))
        hst = EP(nc.sbuf_tensor("m_hst", [P, D], F32))
        lnst = EP(nc.sbuf_tensor("m_lnst", [P, D], BF16))
        lnT = EP(nc.sbuf_tensor("m_lnT", [P, KD * TOK], BF16))
        rtw = EP(nc.sbuf_tensor("m_rtw", [P, KD * 8], BF16))
        lg = EP(nc.sbuf_tensor("m_lg", [P, NT * 8], F32))
        mx = EP(nc.sbuf_tensor("m_mx", [P, NT], F32))
        scv = EP(nc.sbuf_tensor("m_scv", [P, NT], F32))
        taken = EP(nc.sbuf_tensor("m_taken", [P, NT], F32))
        eqv = EP(nc.sbuf_tensor("m_eqv", [P, NT], F32))
        wts = EP(nc.sbuf_tensor("m_wts", [P, NT * 8], F32))
        wtT = EP(nc.sbuf_tensor("m_wtT", [8, TOK], F32))
        e8sb = EP(nc.sbuf_tensor("m_e8sb", [8, 8 * P], F32))
        lnwb = EP(nc.sbuf_tensor("m_lnwb", [P, D], F32))
        lnbb = EP(nc.sbuf_tensor("m_lnbb", [P, D], F32))

        with nc.Block() as blk:
            @blk.sync
            def _(sy):
                for sp in range(4):
                    if sp >= 1:
                        sy.wait_ge(sem["m_dm"], 32 * sp)
                    sy.dma_start(xsp[:, :], xr[sp * 8:(sp + 1) * 8, :, :]
                                 .transpose([1, 0, 2])).then_inc(sem["m1x"], 16)
                    ctx.inc("m1x", 16)

            @blk.gpsimd
            def _(g):
                g.dma_start(rtw[:, :], rw.ap().rearrange("(k p) e -> p k e", p=P)
                            ).then_inc(sem["m_rtw"], 16)
                g.dma_start(e8sb[:, :], e8.ap()).then_inc(sem["m_rtw"], 16)
                g.dma_start(lnwb[:, :], lnw.ap()).then_inc(sem["m_rtw"], 16)
                g.dma_start(lnbb[:, :], lnb.ap()).then_inc(sem["m_rtw"], 16)
                ctx.inc("m_rtw", 64)

            @blk.scalar
            def _(sc):
                for sp in range(4):
                    sc.wait_ge(sem["m1x"], 16 * (sp + 1))
                    for t in range(8):
                        g = sp * 8 + t
                        src = xsp[:, t * D:(t + 1) * D]
                        if g > 0:
                            sc.wait_ge(sem["m_dm"], 4 * g)
                        sc.activation(dump[:, :], src, AF.Copy,
                                      accum_out=s1[:, g:g + 1]).then_inc(sem["m_dm"], 1)
                        ctx.inc("m_dm", 1)
                        sc.wait_ge(sem["m_dm"], ctx.cnt["m_dm"])
                        sc.activation(dump[:, :], src, AF.Square,
                                      accum_out=s2[:, g:g + 1]).then_inc(sem["m_dm"], 3)
                        ctx.inc("m_dm", 3)

        plan1 = RstdPlan(ctx, s2[:, :], a1[:, :], nz[0][:, :], nz[1][:, :],
                         nz[2][:, :])
        with nc.Block() as blk:
            @blk.vector
            def _(v):
                ctx.mark(v.tensor_scalar_mul(c1[:, :], s1[:, :], 1.0 / D))
                ctx.mark(v.tensor_scalar_mul(s2[:, :], s2[:, :], 1.0 / D))
                v.wait_ge(sem["vch"], ctx.cnt["vch"])
                ctx.mark(v.tensor_mul(nz[0][:, :], c1[:, :], c1[:, :]))
                v.wait_ge(sem["vch"], ctx.cnt["vch"])
                ctx.mark(v.tensor_sub(s2[:, :], s2[:, :], nz[0][:, :]))
                v.wait_ge(sem["vch"], ctx.cnt["vch"])
                ctx.mark(v.tensor_scalar_add(s2[:, :], s2[:, :], EPS))
                v.wait_ge(sem["vch"], ctx.cnt["vch"])
                plan1.emit_vector(v)
                v.wait_ge(sem["r_s"], plan1.done_val)
                ctx.mark(v.tensor_mul(c1[:, :], c1[:, :], a1[:, :]))
                v.wait_ge(sem["vch"], ctx.cnt["vch"])
                ctx.mark(v.tensor_scalar_mul(c1[:, :], c1[:, :], -1.0))
                v.wait_ge(sem["vch"], ctx.cnt["vch"])

            @blk.scalar
            def _(sc):
                plan1.emit_scalar(sc)

        with nc.Block() as blk:
            @blk.sync
            def _(sy):
                for sp in range(4):
                    if sp >= 1:
                        sy.wait_ge(sem["m_hs"], 8 * sp)
                    sy.dma_start(xsp[:, :], xr[sp * 8:(sp + 1) * 8, :, :]
                                 .transpose([1, 0, 2])).then_inc(sem["m2x"], 16)
                    ctx.inc("m2x", 16)

            @blk.vector
            def _(v):
                for sp in range(4):
                    v.wait_ge(sem["m2x"], 16 * (sp + 1))
                    for t in range(8):
                        g = sp * 8 + t
                        if g > 0:
                            v.wait_ge(sem["m_dm2"], 4 * g)
                        v.tensor_scalar(hst[:, :], xsp[:, t * D:(t + 1) * D],
                                        a1[:, g:g + 1], c1[:, g:g + 1],
                                        ALU.mult, ALU.add).then_inc(sem["m_hs"], 1)
                        ctx.inc("m_hs", 1)

            @blk.scalar
            def _(sc):
                for g in range(NT):
                    sc.wait_ge(sem["m_hs"], g + 1)
                    if g > 0:
                        sc.wait_ge(sem["m_dm2"], 4 * g)
                    sc.activation(dump[:, :], hst[:, :], AF.Copy,
                                  accum_out=s1[:, g:g + 1]).then_inc(sem["m_dm2"], 1)
                    ctx.inc("m_dm2", 1)
                    sc.wait_ge(sem["m_dm2"], ctx.cnt["m_dm2"])
                    sc.activation(dump[:, :], hst[:, :], AF.Square,
                                  accum_out=s2[:, g:g + 1]).then_inc(sem["m_dm2"], 3)
                    ctx.inc("m_dm2", 3)

        plan2 = RstdPlan(ctx, s2[:, :], a2[:, :], nz[0][:, :], nz[1][:, :],
                         nz[2][:, :])
        with nc.Block() as blk:
            @blk.vector
            def _(v):
                ctx.mark(v.tensor_scalar_mul(c2[:, :], s1[:, :], 1.0 / D))
                ctx.mark(v.tensor_scalar_mul(s2[:, :], s2[:, :], 1.0 / D))
                v.wait_ge(sem["vch"], ctx.cnt["vch"])
                ctx.mark(v.tensor_mul(nz[0][:, :], c2[:, :], c2[:, :]))
                v.wait_ge(sem["vch"], ctx.cnt["vch"])
                ctx.mark(v.tensor_sub(s2[:, :], s2[:, :], nz[0][:, :]))
                v.wait_ge(sem["vch"], ctx.cnt["vch"])
                ctx.mark(v.tensor_scalar_add(s2[:, :], s2[:, :], EPS))
                v.wait_ge(sem["vch"], ctx.cnt["vch"])
                plan2.emit_vector(v)
                v.wait_ge(sem["r_s"], plan2.done_val)
                ctx.mark(v.tensor_mul(c2[:, :], c2[:, :], a2[:, :]))
                v.wait_ge(sem["vch"], ctx.cnt["vch"])
                ctx.mark(v.tensor_scalar_mul(c2[:, :], c2[:, :], -1.0))
                v.wait_ge(sem["vch"], ctx.cnt["vch"])

            @blk.scalar
            def _(sc):
                plan2.emit_scalar(sc)

        with nc.Block() as blk:
            @blk.sync
            def _(sy):
                for sp in range(4):
                    if sp >= 1:
                        sy.wait_ge(sem["m_ln"], 8 * sp)
                    sy.dma_start(xsp[:, :], xr[sp * 8:(sp + 1) * 8, :, :]
                                 .transpose([1, 0, 2])).then_inc(sem["m3x"], 16)
                    ctx.inc("m3x", 16)

            @blk.tensor
            def _(te):
                te.wait_ge(sem["m_rtw"], 64)
                for g in range(NT):
                    te.wait_ge(sem["m_ln"], g + 1)
                    if g >= 2:
                        te.wait_ge(sem["m_lnT_ev"], g - 1)
                    pcb = (g % 2) * 1024
                    for k in range(KD):
                        ins = te.transpose(parsb[:, pcb + k * P:pcb + (k + 1) * P],
                                           lnst[:, k * P:(k + 1) * P], ident[:, :])
                    ins.then_inc(sem["m_tr"], 1)
                    ctx.inc("m_tr", 1)
                for g in range(NT):
                    te.wait_ge(sem["m_lnT_ev"], g + 1)
                    if g >= 2:
                        te.wait_ge(sem["m_rt_ev"], g - 1)
                    pc = 2048 + (g % 2) * 512
                    for k in range(KD):
                        ins = te.matmul(pars[:, pc:pc + 8],
                                        lnT[:, k * TOK + g * P:k * TOK + (g + 1) * P],
                                        rtw[:, k * 8:(k + 1) * 8],
                                        start=(k == 0), stop=(k == KD - 1))
                    ins.then_inc(sem["m_rt_mm"], 1)
                    ctx.inc("m_rt_mm", 1)

            @blk.vector
            def _(v):
                v.wait_ge(sem["m_rtw"], 64)
                for sp in range(4):
                    v.wait_ge(sem["m3x"], 16 * (sp + 1))
                    for t in range(8):
                        g = sp * 8 + t
                        v.wait_ge(sem["m_tr"], g)
                        ctx.mark(v.tensor_scalar(hst[:, :], xsp[:, t * D:(t + 1) * D],
                                                 a1[:, g:g + 1], c1[:, g:g + 1],
                                                 ALU.mult, ALU.add))
                        v.wait_ge(sem["vch"], ctx.cnt["vch"])
                        ctx.mark(v.tensor_scalar(hst[:, :], hst[:, :],
                                                 a2[:, g:g + 1], c2[:, g:g + 1],
                                                 ALU.mult, ALU.add))
                        v.wait_ge(sem["vch"], ctx.cnt["vch"])
                        ctx.mark(v.tensor_mul(hst[:, :], hst[:, :], lnwb[:, :]))
                        v.wait_ge(sem["vch"], ctx.cnt["vch"])
                        v.tensor_add(lnst[:, :], hst[:, :], lnbb[:, :]
                                     ).then_inc(sem["m_ln"], 1)
                        ctx.inc("m_ln", 1)
                        v.wait_ge(sem["m_tr"], g + 1)
                        pcb = (g % 2) * 1024
                        v.tensor_copy(
                            bass.AP(lnT, g * P, [[KD * TOK, P], [TOK, KD], [1, P]]),
                            parsb[:, pcb:pcb + KD * P],
                        ).then_inc(sem["m_lnT_ev"], 1)
                        ctx.inc("m_lnT_ev", 1)
                for g in range(NT):
                    v.wait_ge(sem["m_rt_mm"], g + 1)
                    pc = 2048 + (g % 2) * 512
                    v.tensor_copy(lg[:, g * 8:(g + 1) * 8], pars[:, pc:pc + 8]
                                  ).then_inc(sem["m_rt_ev"], 1)
                    ctx.inc("m_rt_ev", 1)

        if getattr(ctx, "debug", False):
            with nc.Block() as blk:
                @blk.gpsimd
                def _(g):
                    g.dma_start(ctx.dram["d_lnT2"].ap(), lnT[:, :]).then_inc(ctx.get_sem("m_dbg2"), 16)
                    ctx.inc("m_dbg2", 16)
                    g.wait_ge(sem["m_dbg2"], ctx.cnt["m_dbg2"])

        navail = nz[0]
        with nc.Block() as blk:
            @blk.vector
            def _(v):
                v.wait_ge(sem["m_rt_ev"], NT)
                ctx.mark(v.reduce_max(mx[:, :],
                                      bass.AP(lg, 0, [[NT * 8, P], [8, NT], [1, 8]]),
                                      axis=AX.X))
                v.wait_ge(sem["vch"], ctx.cnt["vch"])
                v.engine_nop().then_inc(sem["m_wq"], 1)
                ctx.inc("m_wq", 1)
                ctx.mark(v.memset(taken[:, :], 0.0))
                v.wait_ge(sem["m_sc2"], 1)
                ctx.mark(v.tensor_scalar_add(scv[:, :], scv[:, :], 1.0))
                v.wait_ge(sem["vch"], ctx.cnt["vch"])
                ctx.mark(v.reciprocal(scv[:, :], scv[:, :]))
                v.wait_ge(sem["vch"], ctx.cnt["vch"])
                for e in range(NE):
                    le = bass.AP(lg, e, [[NT * 8, P], [8, NT]])
                    ctx.mark(v.tensor_tensor(eqv[:, :], le, mx[:, :], ALU.is_ge))
                    ctx.mark(v.tensor_scalar(navail[:, :], taken[:, :], -1.0, 1.0,
                                             ALU.mult, ALU.add))
                    v.wait_ge(sem["vch"], ctx.cnt["vch"])
                    ctx.mark(v.tensor_mul(eqv[:, :], eqv[:, :], navail[:, :]))
                    v.wait_ge(sem["vch"], ctx.cnt["vch"])
                    ctx.mark(v.tensor_add(taken[:, :], taken[:, :], eqv[:, :]))
                    ctx.mark(v.tensor_mul(
                        bass.AP(wts, e, [[NT * 8, P], [8, NT]]),
                        eqv[:, :], scv[:, :]))
                    v.wait_ge(sem["vch"], ctx.cnt["vch"])
                v.engine_nop().then_inc(sem["m_wq"], 1)
                ctx.inc("m_wq", 1)

            @blk.scalar
            def _(sc):
                sc.wait_ge(sem["m_wq"], 1)
                sc.activation(scv[:, :], mx[:, :], AF.Exp, scale=-1.0
                              ).then_inc(sem["m_sc2"], 1)
                ctx.inc("m_sc2", 1)

        with nc.Block() as blk:
            @blk.tensor
            def _(te):
                for g in range(NT):
                    if g >= 2:
                        te.wait_ge(sem["m_wt_ev"], g - 1)
                    pc = (g % 2) * 512
                    te.transpose(pars[0:8, pc:pc + P],
                                 wts[:, g * 8:(g + 1) * 8], ctx.sb_identf[:, :]
                                 ).then_inc(sem["m_wq"], 1)
                    ctx.inc("m_wq", 1)

            @blk.vector
            def _(v):
                for g in range(NT):
                    v.wait_ge(sem["m_wq"], 2 + g + 1)
                    pc = (g % 2) * 512
                    v.tensor_copy(wtT[:, g * P:(g + 1) * P],
                                  bass.AP(pars, pc, [[4096, 8], [1, P]])
                                  ).then_inc(sem["m_wt_ev"], 1)
                    ctx.inc("m_wt_ev", 1)

        if getattr(ctx, "debug", False):
            with nc.Block() as blk:
                @blk.gpsimd
                def _(g):
                    g.dma_start(ctx.dram["d_lg"].ap(), lg[:, :]).then_inc(sem["m_dbg2"], 16)
                    g.dma_start(ctx.dram["d_wts"].ap(), wts[:, :]).then_inc(sem["m_dbg2"], 16)
                    g.dma_start(ctx.dram["d_wtT"].ap(), wtT[:, :]).then_inc(sem["m_dbg2"], 16)
                    ctx.inc("m_dbg2", 48)
                    g.wait_ge(sem["m_dbg2"], ctx.cnt["m_dbg2"])

        with contextlib.ExitStack() as stA:
            EA = stA.enter_context
            wbc = EA(nc.sbuf_tensor("m_wbc", [P, TOK], F32))
            gwt = [EA(nc.sbuf_tensor(f"m_gw{i}", [P, KD * P], BF16)) for i in range(2)]
            uwt = [EA(nc.sbuf_tensor(f"m_uw{i}", [P, KD * P], BF16)) for i in range(2)]
            a1p = EA(nc.sbuf_tensor("m_a1p", [P, I], F32))
            u1p = EA(nc.sbuf_tensor("m_u1p", [P, I], F32))
            qp = EA(nc.sbuf_tensor("m_qp", [P, I], F32))
            thp = EA(nc.sbuf_tensor("m_thp", [P, I], F32))
            hmst = [EA(nc.sbuf_tensor(f"m_hmst{i}", [P, I], BF16)) for i in range(2)]

            with nc.Block() as blk:
                @blk.gpsimd
                def _(g):
                    for e in range(NX):
                        for ht in range(NHT):
                            idx = e * NHT + ht
                            if idx >= 2:
                                g.wait_ge(sem["m_gu_mm"], 2 * (idx - 1))
                            if e < NE:
                                gsrc = gw.ap()[e, :, ht * P:(ht + 1) * P]
                                usrc = uw.ap()[e, :, ht * P:(ht + 1) * P]
                            else:
                                gsrc = shg.ap()[:, ht * P:(ht + 1) * P]
                                usrc = shu.ap()[:, ht * P:(ht + 1) * P]
                            mws = "mw" if idx % 2 == 0 else "mw2"
                            g.dma_start(gwt[idx % 2][:, :],
                                        gsrc.rearrange("(k p) h -> p k h", p=P)
                                        ).then_inc(sem[mws], 16)
                            g.dma_start(uwt[idx % 2][:, :],
                                        usrc.rearrange("(k p) h -> p k h", p=P)
                                        ).then_inc(sem[mws], 16)
                            ctx.inc(mws, 32)

                @blk.tensor
                def _(te):
                    te.wait_ge(sem["m_wq"], ctx.cnt["m_wq"])
                    te.wait_ge(sem["m_wt_ev"], NT)
                    for e in range(NX):
                        if e < NE:
                            te.wait_ge(sem["m_xe"], e)
                            for g2 in range(NT):
                                pc = 2048 + (g2 % 2) * 512
                                if g2 >= 2:
                                    te.wait_ge(sem["m_wb_ev"], e * NT + g2 - 1)
                                te.matmul(pars[:, pc:pc + P],
                                          e8sb[:, e * P:(e + 1) * P],
                                          wtT[:, g2 * P:(g2 + 1) * P],
                                          start=True, stop=True
                                          ).then_inc(sem["m_wb_mm"], 1)
                                ctx.inc("m_wb_mm", 1)
                        for ht in range(NHT):
                            idx = e * NHT + ht
                            te.wait_ge(sem["mw" if idx % 2 == 0 else "mw2"],
                                       32 * (idx // 2 + 1))
                            if e < NE and ht == 0:
                                te.wait_ge(sem["m_wb_ev"], (e + 1) * NT)
                            for half in range(2):
                                hi = idx * 2 + half
                                if hi >= 1:
                                    te.wait_ge(sem["m_q"], hi)
                                    te.wait_ge(sem["m_u"], hi)
                                for k in range(KD):
                                    mvb = lnT[:, k * TOK + half * 2048:
                                              k * TOK + (half + 1) * 2048]
                                    wga = gwt[idx % 2][:, k * P:(k + 1) * P]
                                    wua = uwt[idx % 2][:, k * P:(k + 1) * P]
                                    for nb in range(4):
                                        te.matmul(pars[:, nb * 512:(nb + 1) * 512],
                                                  wga, mvb[:, nb * 512:(nb + 1) * 512],
                                                  start=(k == 0), stop=(k == KD - 1))
                                    for nb in range(4):
                                        ins = te.matmul(
                                            pars[:, 2048 + nb * 512:2048 + (nb + 1) * 512],
                                            wua, mvb[:, nb * 512:(nb + 1) * 512],
                                            start=(k == 0), stop=(k == KD - 1))
                                ins.then_inc(sem["m_gu_mm"], 1)
                                ctx.inc("m_gu_mm", 1)

                @blk.scalar
                def _(sc):
                    for e in range(NX):
                        for ht in range(NHT):
                            for half in range(2):
                                hi = (e * NHT + ht) * 2 + half
                                sc.wait_ge(sem["m_q"], hi + 1)
                                sc.activation(thp[:, :], a1p[:, :], AF.Sigmoid,
                                              scale=1.0).then_inc(sem["m_th"], 1)
                                ctx.inc("m_th", 1)

                @blk.vector
                def _(v):
                    for e in range(NX):
                        if e < NE:
                            for g2 in range(NT):
                                v.wait_ge(sem["m_wb_mm"], e * NT + g2 + 1)
                                pc = 2048 + (g2 % 2) * 512
                                v.tensor_copy(wbc[:, g2 * P:(g2 + 1) * P],
                                              pars[:, pc:pc + P]
                                              ).then_inc(sem["m_wb_ev"], 1)
                                ctx.inc("m_wb_ev", 1)
                        for ht in range(NHT):
                            for half in range(2):
                                hi = (e * NHT + ht) * 2 + half
                                v.wait_ge(sem["m_gu_mm"], hi + 1)
                                if e < NE:
                                    ctx.mark(v.tensor_mul(
                                        a1p[:, :], pars[:, 0:2048],
                                        wbc[:, half * 2048:(half + 1) * 2048]))
                                    v.wait_ge(sem["vch"], ctx.cnt["vch"])
                                    v.engine_nop().then_inc(sem["m_q"], 1)
                                    ctx.inc("m_q", 1)
                                    ctx.mark(v.tensor_mul(
                                        u1p[:, :], pars[:, 2048:4096],
                                        wbc[:, half * 2048:(half + 1) * 2048]))
                                else:
                                    ctx.mark(v.tensor_copy(a1p[:, :], pars[:, 0:2048]))
                                    v.wait_ge(sem["vch"], ctx.cnt["vch"])
                                    v.engine_nop().then_inc(sem["m_q"], 1)
                                    ctx.inc("m_q", 1)
                                    ctx.mark(v.tensor_copy(u1p[:, :], pars[:, 2048:4096]))
                                v.wait_ge(sem["vch"], ctx.cnt["vch"])
                                v.engine_nop().then_inc(sem["m_u"], 1)
                                ctx.inc("m_u", 1)
                                ctx.mark(v.tensor_mul(qp[:, :], a1p[:, :], u1p[:, :]))
                                v.wait_ge(sem["vch"], ctx.cnt["vch"])
                                v.wait_ge(sem["m_th"], hi + 1)
                                if hi >= 2:
                                    v.wait_ge(sem["m_hm_st" if hi % 2 == 0
                                                  else "m_hm_st2"],
                                              16 * ((hi - 2) // 2 + 1))
                                v.tensor_mul(
                                    hmst[hi % 2][:, :], thp[:, :], qp[:, :]
                                    ).then_inc(sem["m_gu_ev"], 2)
                                ctx.inc("m_gu_ev", 2)
                        if e < NE:
                            v.engine_nop().then_inc(sem["m_xe"], 1)
                            ctx.inc("m_xe", 1)

                @blk.sync
                def _(sy):
                    for e in range(NX):
                        for ht in range(NHT):
                            for half in range(2):
                                hi = (e * NHT + ht) * 2 + half
                                sy.wait_ge(sem["m_gu_ev"], 2 * (hi + 1))
                                sy.dma_start(
                                    hm_s.ap()[e, ht, :, half * 2048:(half + 1) * 2048],
                                    hmst[hi % 2][:, :]
                                    ).then_inc(sem["m_hm_st" if hi % 2 == 0
                                                   else "m_hm_st2"], 16)
                                ctx.inc("m_hm_st" if hi % 2 == 0 else "m_hm_st2", 16)
                    sy.wait_ge(sem["m_hm_st"], ctx.cnt["m_hm_st"])
                    sy.wait_ge(sem["m_hm_st2"], ctx.cnt["m_hm_st2"])

        stP.close()

        with contextlib.ExitStack() as stB:
            EB = stB.enter_context
            racc = EB(nc.sbuf_tensor("m_racc", [P, 16 * D], F32))
            dwb = [EB(nc.sbuf_tensor(f"m_dwb{i}", [P, NHT * D], BF16)) for i in range(2)]
            hmb = [EB(nc.sbuf_tensor(f"m_hmb{i}", [P, NHT * 512], BF16)) for i in range(2)]
            xres = [EB(nc.sbuf_tensor(f"m_xres{i}", [P, D], F32)) for i in range(4)]
            outst = [EB(nc.sbuf_tensor(f"m_outst{i}", [P, D], F32)) for i in range(2)]
            rin = EB(nc.sbuf_tensor("m_rin", [P, D], F32))
            s1b = EB(nc.sbuf_tensor("m_s1b", [P, 4], F32))
            s2b = EB(nc.sbuf_tensor("m_s2b", [P, 4], F32))
            mu3 = EB(nc.sbuf_tensor("m_mu3", [P, 4], F32))
            rstd3 = EB(nc.sbuf_tensor("m_rstd3", [P, 4], F32))
            z3 = EB(nc.sbuf_tensor("m_z3", [P, 4], F32))
            e3 = EB(nc.sbuf_tensor("m_e3", [P, 4], F32))
            t3 = EB(nc.sbuf_tensor("m_t3", [P, 4], F32))
            plans3 = [RstdPlan(ctx, s2b[:, :], rstd3[:, :], z3[:, :], e3[:, :],
                               t3[:, :]) for _ in range(8)]

            with nc.Block() as blk:
                @blk.gpsimd
                def _(g):
                    for sweep in range(2):
                        for e in range(NX):
                            ei = sweep * NX + e
                            if ei >= 2:
                                g.wait_ge(sem["m_dn_mm"], (ei - 1) * 4)
                            src = (dw.ap()[e].rearrange("(n p) d -> p n d", p=P)
                                   if e < NE else
                                   shd.ap().rearrange("(n p) d -> p n d", p=P))
                            g.dma_start(dwb[ei % 2][:, :], src
                                        ).then_inc(sem["m_dw" if ei % 2 == 0
                                                       else "m_dw2"], 16)
                            ctx.inc("m_dw" if ei % 2 == 0 else "m_dw2", 16)

                @blk.sync
                def _(sy):
                    for sweep in range(2):
                        for e in range(NX):
                            for gi in range(4):
                                grp = sweep * 4 + gi
                                li = (sweep * NX + e) * 4 + gi
                                if li >= 2:
                                    sy.wait_ge(sem["m_dn_mm"], li - 1)
                                sy.dma_start(
                                    hmb[li % 2][:, :],
                                    hm_s.ap()[e, :, :, grp * 512:(grp + 1) * 512]
                                        .transpose([1, 0, 2])
                                ).then_inc(sem["m_hm_ld" if li % 2 == 0
                                                   else "m_hm_ld2"], 16)
                                ctx.inc("m_hm_ld" if li % 2 == 0 else "m_hm_ld2", 16)
                        for grp4 in range(4):
                            for j in range(4):
                                gt = sweep * 16 + grp4 * 4 + j
                                if gt >= 4:
                                    sy.wait_ge(sem["m_res"], gt - 3)
                                xrs = ("m_xr2", "m_xr3", "m_xr4", "m_xr5")[gt % 4]
                                sy.dma_start(xres[gt % 4][:, :], xr[gt, :, :]
                                             ).then_inc(sem[xrs], 16)
                                ctx.inc(xrs, 16)
                            for j in range(4):
                                gt = sweep * 16 + grp4 * 4 + j
                                sy.wait_ge(sem["m_res"], gt + 1)
                                sy.dma_start(xo[gt, :, :], outst[gt % 2][:, :]
                                             ).then_inc(sem["m_st" if gt % 2 == 0
                                                            else "m_st2"], 16)
                                ctx.inc("m_st" if gt % 2 == 0 else "m_st2", 16)
                    sy.wait_ge(sem["m_st"], ctx.cnt["m_st"])
                    sy.wait_ge(sem["m_st2"], ctx.cnt["m_st2"])

                @blk.tensor
                def _(te):
                    for sweep in range(2):
                        for e in range(NX):
                            ei = sweep * NX + e
                            te.wait_ge(sem["m_dw" if ei % 2 == 0 else "m_dw2"],
                                       16 * (ei // 2 + 1))
                            for gi in range(4):
                                li = ei * 4 + gi
                                te.wait_ge(sem["m_hm_ld" if li % 2 == 0
                                                   else "m_hm_ld2"],
                                           16 * (li // 2 + 1))
                                if li >= 1:
                                    te.wait_ge(sem["m_dn_ev"], li)
                                for ht in range(NHT):
                                    for tt in range(4):
                                        stat = hmb[li % 2][:, ht * 512 + tt * P:
                                                           ht * 512 + (tt + 1) * P]
                                        for nb in range(2):
                                            ins = te.matmul(
                                                pars[:, tt * 1024 + nb * 512:
                                                     tt * 1024 + (nb + 1) * 512],
                                                stat,
                                                dwb[ei % 2][:, ht * D + nb * 512:
                                                            ht * D + (nb + 1) * 512],
                                                start=(ht == 0), stop=(ht == NHT - 1))
                                ins.then_inc(sem["m_dn_mm"], 1)
                                ctx.inc("m_dn_mm", 1)

                @blk.vector
                def _(v):
                    for sweep in range(2):
                        ctx.mark(v.memset(racc[:, :], 0.0))
                        v.wait_ge(sem["vch"], ctx.cnt["vch"])
                        for e in range(NX):
                            ei = sweep * NX + e
                            for gi in range(4):
                                li = ei * 4 + gi
                                v.wait_ge(sem["m_dn_mm"], li + 1)
                                sl = racc[:, gi * 4096:(gi + 1) * 4096]
                                ctx.mark(v.tensor_add(sl, sl, pars[:, 0:4096]))
                                v.wait_ge(sem["vch"], ctx.cnt["vch"])
                                v.engine_nop().then_inc(sem["m_dn_ev"], 1)
                                ctx.inc("m_dn_ev", 1)
                        for gi in range(4):
                            grp = sweep * 4 + gi
                            for j in range(4):
                                gt = grp * 4 + j
                                v.wait_ge(
                                    sem[("m_xr2", "m_xr3", "m_xr4", "m_xr5")[gt % 4]],
                                    16 * (gt // 4 + 1))
                                ctx.mark(v.tensor_scalar(
                                    rin[:, :], xres[gt % 4][:, :],
                                    a1[:, gt:gt + 1], c1[:, gt:gt + 1],
                                    ALU.mult, ALU.add))
                                v.wait_ge(sem["vch"], ctx.cnt["vch"])
                                ctx.mark(v.tensor_add(
                                    rin[:, :], rin[:, :],
                                    racc[:, (gi * 4 + j) * D:(gi * 4 + j + 1) * D]))
                                v.wait_ge(sem["vch"], ctx.cnt["vch"])
                                ctx.mark(v.tensor_copy(
                                    racc[:, (gi * 4 + j) * D:(gi * 4 + j + 1) * D],
                                    rin[:, :]))
                                ctx.mark(v.reduce_sum(s1b[:, j:j + 1], rin[:, :],
                                                      axis=AX.X))
                                ctx.mark(v.tensor_mul(dump[:, :], rin[:, :], rin[:, :]))
                                v.wait_ge(sem["vch"], ctx.cnt["vch"])
                                ctx.mark(v.reduce_sum(s2b[:, j:j + 1], dump[:, :],
                                                      axis=AX.X))
                            v.wait_ge(sem["vch"], ctx.cnt["vch"])
                            ctx.mark(v.tensor_scalar_mul(mu3[:, :], s1b[:, :], 1.0 / D))
                            v.tensor_scalar_mul(s2b[:, :], s2b[:, :], 1.0 / D)
                            v.wait_ge(sem["vch"], ctx.cnt["vch"])
                            ctx.mark(v.tensor_mul(t3[:, :], mu3[:, :], mu3[:, :]))
                            v.wait_ge(sem["vch"], ctx.cnt["vch"])
                            ctx.mark(v.tensor_sub(s2b[:, :], s2b[:, :], t3[:, :]))
                            v.wait_ge(sem["vch"], ctx.cnt["vch"])
                            ctx.mark(v.tensor_scalar_add(s2b[:, :], s2b[:, :], EPS))
                            v.wait_ge(sem["vch"], ctx.cnt["vch"])
                            pl = plans3[grp]
                            pl.emit_vector(v)
                            v.wait_ge(sem["r_s"], pl.done_val)
                            ctx.mark(v.tensor_mul(mu3[:, :], mu3[:, :], rstd3[:, :]))
                            v.wait_ge(sem["vch"], ctx.cnt["vch"])
                            ctx.mark(v.tensor_scalar_mul(mu3[:, :], mu3[:, :], -1.0))
                            v.wait_ge(sem["vch"], ctx.cnt["vch"])
                            for j in range(4):
                                gt = grp * 4 + j
                                if gt >= 2:
                                    v.wait_ge(sem["m_st" if gt % 2 == 0 else "m_st2"],
                                              16 * ((gt - 2) // 2 + 1))
                                ob = outst[gt % 2]
                                ctx.mark(v.tensor_scalar(
                                    ob[:, :],
                                    racc[:, (gi * 4 + j) * D:(gi * 4 + j + 1) * D],
                                    rstd3[:, j:j + 1], mu3[:, j:j + 1],
                                    ALU.mult, ALU.add))
                                v.wait_ge(sem["vch"], ctx.cnt["vch"])
                                v.tensor_add(ob[:, :], ob[:, :], xres[gt % 4][:, :]
                                             ).then_inc(sem["m_res"], 1)
                                ctx.inc("m_res", 1)

                @blk.scalar
                def _(sc):
                    for grp in range(8):
                        plans3[grp].emit_scalar(sc)

        with nc.Block() as blk:
            @blk.vector
            def _(v):
                v.wait_ge(sem["m_st"], ctx.cnt["m_st"])
                v.wait_ge(sem["m_st2"], ctx.cnt["m_st2"])





def build_program(stages=ALL_STAGES, debug=False, debug_l0a=False, dbg_nsp2=None,
                  debug_l1a=False):
    import os
    nc = bass.Bass("TRN2", target_bir_lowering=False,
                   detect_race_conditions=bool(os.environ.get("RACEDET")))
    stack = contextlib.ExitStack()
    ctx = Ctx(nc, stack)
    ctx.debug = debug
    ctx.debug_l0a = debug_l0a
    ctx.debug_l1a = debug_l1a
    if dbg_nsp2 is not None:
        ctx.dbg_nsp2 = dbg_nsp2
    if debug_l1a:
        for nm, shp, dt in (("dt_ln2Te", [P, KD * 2112], BF16),
                            ("dt_ln2Tq", [P, KD * 2112], BF16),
                            ("dt_ln2Tv", [P, KD * 2112], BF16),
                            ("dt_ln2Ta", [P, KD * 2112], BF16),
                            ("dt_ln2T", [P, KD * 2112], BF16),
                            ("dt_qkT", [P, 16 * 2112], BF16),
                            ("dt_vslot", [P, 26 * D], BF16)):
            ctx.dram[nm] = nc.dram_tensor(nm, shp, dt, kind="ExternalOutput")
    if debug_l0a:
        for nm, shp, dt in (("da_lnTs", [P, 8 * 512], BF16),
                            ("da_xbc", [P, 18 * 516], BF16), ("da_xtm", [P, 2048], BF16),
                            ("da_zsb", [P, 2048], BF16), ("da_ych", [P, 2048], F32),
                            ("da_dtc", [P, 32], F32), ("da_acsT", [P, 32], F32),
                            ("da_S", [P, 2048], F32), ("da_ynT", [P, 2048], BF16),
                            ("da_rsb", [P, 1024], F32), ("da_dsdt", [P, 32], F32),
                            ("da_einb", [P, 32], F32), ("da_etbv", [P, 32], F32)):
            ctx.dram[nm] = nc.dram_tensor(nm, shp, dt, kind="ExternalOutput")
    if debug:
        for nm, shp, dt in (("d_lnT2", [P, KD * TOK], BF16), ("d_lg", [P, NT * 8], F32),
                            ("d_wts", [P, NT * 8], F32), ("d_wtT", [8, TOK], F32),
                            ("d_s1", [P, NT], F32), ("d_s2", [P, NT], F32),
                            ("d_rstd", [P, NT], F32), ("d_mu", [P, NT], F32),
                            ("d_lnT", [P, 8192], BF16), ("d_hmT", [P, 24576], BF16),
                            ("d_rsp", [P, 4096], F32)):
            ctx.dram[nm] = nc.dram_tensor(nm, shp, dt, kind="ExternalOutput")

    dr = ctx.dram
    dr["x"] = nc.dram_tensor("x", [TOK, D], F32, kind="ExternalInput")
    dr["out"] = nc.dram_tensor("out", [TOK, D], F32, kind="ExternalOutput")
    dr["ident"] = nc.dram_tensor("ident", [P, P], BF16, kind="ExternalInput")
    dr["identf"] = nc.dram_tensor("identf", [P, P], F32, kind="ExternalInput")
    if "l0f" in stages:
        dr["l0f_wgT"] = nc.dram_tensor("l0f_wgT", [D, 3072], I32, kind="ExternalInput")
        dr["l0f_wuT"] = nc.dram_tensor("l0f_wuT", [D, 3072], I32, kind="ExternalInput")
        dr["l0f_wdT"] = nc.dram_tensor("l0f_wdT", [3072, D], I32, kind="ExternalInput")
    if "l0a" in stages:
        dr["l0a_win"] = nc.dram_tensor("l0a_win", [D, 4384], F32, kind="ExternalInput")
        dr["l0a_wout"] = nc.dram_tensor("l0a_wout", [2048, D], F32, kind="ExternalInput")
        dr["l0a_cw"] = nc.dram_tensor("l0a_cw", [P, 18 * 4], F32, kind="ExternalInput")
        dr["l0a_cb"] = nc.dram_tensor("l0a_cb", [P, 18], F32, kind="ExternalInput")
        dr["l0a_dtb"] = nc.dram_tensor("l0a_dtb", [P, 32], F32, kind="ExternalInput")
        dr["l0a_alog"] = nc.dram_tensor("l0a_alog", [P, 32], F32, kind="ExternalInput")
        dr["l0a_dpar"] = nc.dram_tensor("l0a_dpar", [P, 32], F32, kind="ExternalInput")
        dr["l0a_nw"] = nc.dram_tensor("l0a_nw", [P, 2048], F32, kind="ExternalInput")
        dr["l0a_triu"] = nc.dram_tensor("l0a_triu", [P, P], F32, kind="ExternalInput")
        dr["l0a_ones"] = nc.dram_tensor("l0a_ones", [P, P], F32, kind="ExternalInput")
        dr["l0a_hsel"] = nc.dram_tensor("l0a_hsel", [32, 32 * P], F32, kind="ExternalInput")
    if "l1a" in stages:
        dr["l1a_wqkvT"] = nc.dram_tensor("l1a_wqkvT", [D, 3072], F32, kind="ExternalInput")
        dr["l1a_woutT"] = nc.dram_tensor("l1a_woutT", [D, D], F32, kind="ExternalInput")
        dr["l1a_bout"] = nc.dram_tensor("l1a_bout", [P, D], F32, kind="ExternalInput")
    if "l1f" in stages:
        dr["l1f_gate"] = nc.dram_tensor("l1f_gate", [8, D, 2048], F32, kind="ExternalInput")
        dr["l1f_up"] = nc.dram_tensor("l1f_up", [8, D, 2048], F32, kind="ExternalInput")
        dr["l1f_down"] = nc.dram_tensor("l1f_down", [8, 2048, D], F32, kind="ExternalInput")
        dr["l1f_shg"] = nc.dram_tensor("l1f_shg", [D, 2048], F32, kind="ExternalInput")
        dr["l1f_shu"] = nc.dram_tensor("l1f_shu", [D, 2048], F32, kind="ExternalInput")
        dr["l1f_shd"] = nc.dram_tensor("l1f_shd", [2048, D], F32, kind="ExternalInput")
        dr["l1f_router"] = nc.dram_tensor("l1f_router", [D, 8], F32, kind="ExternalInput")
        dr["l1f_lnw"] = nc.dram_tensor("l1f_lnw", [P, D], F32, kind="ExternalInput")
        dr["l1f_lnb"] = nc.dram_tensor("l1f_lnb", [P, D], F32, kind="ExternalInput")
        dr["l1f_e8"] = nc.dram_tensor("l1f_e8", [8, 8 * P], F32, kind="ExternalInput")
        dr["l1f_hms"] = nc.dram_tensor("l1f_hms", [9, 16, P, TOK], BF16, kind="Internal")
    n_mid = max(0, len(stages) - 1)
    mids = [nc.dram_tensor(f"xs{i}", [TOK, D], F32, kind="Internal")
            for i in range(n_mid)]

    with stack:
        ctx.sb_ident = stack.enter_context(nc.sbuf_tensor("c_ident", [P, P], BF16))
        ctx.sb_identf = stack.enter_context(nc.sbuf_tensor("c_identf", [P, P], F32))
        ctx.get_sem("c_ld")
        with nc.Block() as blk:
            @blk.sync
            def _(sy):
                sy.dma_start(ctx.sb_ident[:, :], ctx.dram["ident"].ap()
                             ).then_inc(ctx.sem["c_ld"], 16)
                ctx.inc("c_ld", 16)
                sy.dma_start(ctx.sb_identf[:, :], ctx.dram["identf"].ap()
                             ).then_inc(ctx.sem["c_ld"], 16)
                ctx.inc("c_ld", 16)
                sy.wait_ge(ctx.sem["c_ld"], ctx.cnt["c_ld"])

        cur = dr["x"]
        snap = (nc.sbuf_base, nc.sbuf_top, nc.psum_base, nc.psum_top,
                nc.local_dram_base, nc.shared_dram_base)
        for i, sname in enumerate(stages):
            nxt = dr["out"] if i == len(stages) - 1 else mids[i]
            if sname == "l0a":
                emit_l0_attn(ctx, cur, nxt)
            elif sname == "l1a":
                emit_l1_attn(ctx, cur, nxt)
            elif sname == "l0f":
                emit_l0_ffn(ctx, cur, nxt)
            elif sname == "l1f":
                emit_l1_ffn(ctx, cur, nxt)
            else:
                raise NotImplementedError(sname)
            cur = nxt
            if i < len(stages) - 1:
                ctx.sem.clear()
                ctx.cnt.clear()
                nc.reset(previous_sbuf_base=snap[0], previous_sbuf_top=snap[1],
                         previous_psum_base=snap[2], previous_psum_top=snap[3],
                         previous_local_dram_base=snap[4],
                         previous_shared_dram_base=snap[5])
    return nc



def _prep_const_inputs(inputs, stages=ALL_STAGES):
    com = {}
    com["ident"] = np.eye(P, dtype=ml_dtypes.bfloat16)
    com["identf"] = np.eye(P, dtype=np.float32)
    if "l0a" in stages:
        com["l0a_win"] = np.ascontiguousarray(np.asarray(inputs["L0_attn_in_proj_w"], dtype=np.float32))
        com["l0a_wout"] = np.ascontiguousarray(np.asarray(inputs["L0_attn_out_proj_w"], dtype=np.float32))
        cw = np.asarray(inputs["L0_attn_conv_weight"], dtype=np.float32)[:, 0, :]
        com["l0a_cw"] = np.ascontiguousarray(cw.reshape(18, P, 4).transpose(1, 0, 2).reshape(P, 72))
        cb = np.asarray(inputs["L0_attn_conv_bias"], dtype=np.float32)
        com["l0a_cb"] = np.ascontiguousarray(cb.reshape(18, P).T)
        com["l0a_dtb"] = np.broadcast_to(np.asarray(inputs["L0_attn_dt_bias"], dtype=np.float32), (P, 32)).copy()
        com["l0a_alog"] = np.broadcast_to(np.asarray(inputs["L0_attn_A_log"], dtype=np.float32), (P, 32)).copy()
        com["l0a_dpar"] = np.broadcast_to(np.asarray(inputs["L0_attn_D_param"], dtype=np.float32), (P, 32)).copy()
        com["l0a_nw"] = np.broadcast_to(np.asarray(inputs["L0_attn_norm_weight"], dtype=np.float32), (P, 2048)).copy()
        ii = np.arange(P)
        com["l0a_triu"] = (ii[:, None] <= ii[None, :]).astype(np.float32)
        com["l0a_ones"] = np.ones((P, P), np.float32)
        hs = np.zeros((32, 32 * P), np.float32)
        for h in range(32):
            hs[h, h * P:(h + 1) * P] = 1.0
        com["l0a_hsel"] = hs
    if "l0f" in stages:
        com["l0f_wgT"] = np.ascontiguousarray(np.asarray(inputs["L0_ffn_w_gate"]).T)
        com["l0f_wuT"] = np.ascontiguousarray(np.asarray(inputs["L0_ffn_w_up"]).T)
        com["l0f_wdT"] = np.ascontiguousarray(np.asarray(inputs["L0_ffn_w_down"]).T)
    if "l1a" in stages:
        com["l1a_wqkvT"] = np.ascontiguousarray(np.asarray(inputs["L1_attn_Wqkv"], dtype=np.float32).T)
        com["l1a_woutT"] = np.ascontiguousarray(np.asarray(inputs["L1_attn_Wout"], dtype=np.float32).T)
        com["l1a_bout"] = np.broadcast_to(np.asarray(inputs["L1_attn_bout"], dtype=np.float32), (P, D)).copy()
    if "l1f" in stages:
        com["l1f_gate"] = np.ascontiguousarray(np.asarray(inputs["L1_ffn_gate_w"], dtype=np.float32))
        com["l1f_up"] = np.ascontiguousarray(np.asarray(inputs["L1_ffn_up_w"], dtype=np.float32))
        com["l1f_down"] = np.ascontiguousarray(np.asarray(inputs["L1_ffn_down_w"], dtype=np.float32))
        com["l1f_shg"] = np.ascontiguousarray(np.asarray(inputs["L1_ffn_sh_gate"], dtype=np.float32))
        com["l1f_shu"] = np.ascontiguousarray(np.asarray(inputs["L1_ffn_sh_up"], dtype=np.float32))
        com["l1f_shd"] = np.ascontiguousarray(np.asarray(inputs["L1_ffn_sh_down"], dtype=np.float32))
        com["l1f_router"] = np.ascontiguousarray(np.asarray(inputs["L1_ffn_router_w"], dtype=np.float32))
        com["l1f_lnw"] = np.broadcast_to(np.asarray(inputs["L1_ffn_ln_w"], dtype=np.float32), (P, D)).copy()
        com["l1f_lnb"] = np.broadcast_to(np.asarray(inputs["L1_ffn_ln_b"], dtype=np.float32), (P, D)).copy()
        e8 = np.zeros((8, 8 * P), dtype=np.float32)
        for e in range(8):
            e8[e, e * P:(e + 1) * P] = 1.0
        com["l1f_e8"] = e8
    return com


def run_stages(x_full, inputs, stages=ALL_STAGES):
    nc = build_program(stages)
    com = _prep_const_inputs(inputs, stages)
    in_maps = []
    for c in range(8):
        m = dict(com)
        m["x"] = np.ascontiguousarray(
            np.asarray(x_full[2 * c:2 * c + 2], dtype=np.float32).reshape(TOK, D))
        in_maps.append(m)
    res = run_bass_kernel_spmd(nc, in_maps, core_ids=list(range(8)))
    out = np.stack([np.asarray(r["out"]) for r in res.results])
    return out.reshape(16, 2048, 1024)


def kernel(**inputs):
    x = np.asarray(inputs["L0_attn_u"], dtype=np.float32)
    return run_stages(x, inputs, ALL_STAGES).astype(np.float32)


# version 04
# speedup vs baseline: 1.0559x; 1.0559x over previous
import math
import contextlib
import numpy as np
import ml_dtypes
import concourse.bass as bass
import concourse.mybir as mybir
from concourse.bass_utils import run_bass_kernel_spmd

F32 = mybir.dt.float32
BF16 = mybir.dt.bfloat16
I32 = mybir.dt.int32
AF = mybir.ActivationFunctionType
ALU = mybir.AluOpType
AX = mybir.AxisListType

P = 128
TOK = 4096
NT = TOK // P
D = 1024
KD = D // P
EPS = 1e-5

ALL_STAGES = ("l0a", "l0f", "l1a", "l1f")


class Ctx:

    def __init__(self, nc, stack):
        self.nc = nc
        self.stack = stack
        self.dram = {}
        self.sem = {}
        self.cnt = {}

    def get_sem(self, name):
        if name not in self.sem:
            self.sem[name] = self.nc.alloc_semaphore(name=name)
            self.cnt[name] = 0
        return self.sem[name]

    def inc(self, name, by=1):
        self.cnt[name] += by
        return self.cnt[name]

    def mark(self, ins, name="vch"):
        ins.then_inc(self.get_sem(name), 1)
        return self.inc(name)


LN2C = math.log(2.0)


class RstdPlan:

    ITERS = 4

    def __init__(self, ctx, v_ap, out_ap, z, e, t, pow_scale=-0.5,
                 raw_ln=False):
        self.ctx = ctx
        self.v_ap, self.out_ap = v_ap, out_ap
        self.z, self.e, self.t = z, e, t
        self.pow_scale = pow_scale
        self.raw_ln = raw_ln
        ctx.get_sem("r_s"), ctx.get_sem("r_v")
        self.s0 = ctx.cnt["r_s"]
        self.v0 = ctx.cnt["r_v"]
        ctx.cnt["r_s"] += self.ITERS + 1
        ctx.cnt["r_v"] += self.ITERS + 1
        self.done_val = self.s0 + self.ITERS + 1

    def emit_vector(self, v):
        ctx = self.ctx
        sem = ctx.sem
        ctx.mark(v.tensor_copy(self.z, self.v_ap.bitcast(I32)))
        v.wait_ge(sem["vch"], ctx.cnt["vch"])
        v.tensor_scalar(self.z, self.z, LN2C / (1 << 23), -127.0 * LN2C,
                        ALU.mult, ALU.add).then_inc(sem["r_v"], 1)
        for i in range(self.ITERS):
            v.wait_ge(sem["r_s"], self.s0 + 1 + i)
            ctx.mark(v.tensor_mul(self.t, self.v_ap, self.e))
            v.wait_ge(sem["vch"], ctx.cnt["vch"])
            v.scalar_tensor_tensor(self.z, self.t, -1.0, self.z,
                                   ALU.add, ALU.add).then_inc(sem["r_v"], 1)

    def emit_scalar(self, sc):
        sem = self.ctx.sem
        for i in range(self.ITERS):
            sc.wait_ge(sem["r_v"], self.v0 + 1 + i)
            sc.activation(self.e, self.z, AF.Exp, scale=-1.0
                          ).then_inc(sem["r_s"], 1)
        sc.wait_ge(sem["r_v"], self.v0 + 1 + self.ITERS)
        if self.raw_ln:
            sc.activation(self.out_ap, self.z, AF.Copy).then_inc(sem["r_s"], 1)
        else:
            sc.activation(self.out_ap, self.z, AF.Exp, scale=self.pow_scale
                          ).then_inc(sem["r_s"], 1)



def emit_l0_ffn(ctx, x_in, x_out):
    nc = ctx.nc
    H = 3072
    NHT = H // P
    SPAN = 1024
    NSP = TOK // SPAN
    TPS = SPAN // P
    S = 1.0 / 127.0

    for s in ("f1x", "f1c", "f2x", "fw", "f_tr", "f_ln", "f_lnT_ev",
              "f_gu_mm", "f_gu_ev", "f_th", "f_dn_mm", "f_r_ev", "f_res",
              "f_st", "f_st2", "fwd", "f_xr", "f_xr2", "fw2",
              "f_dm", "f_dbg", "vch", "r_s", "r_v"):
        ctx.get_sem(s)
    sem = ctx.sem

    wgT = ctx.dram["l0f_wgT"]
    wuT = ctx.dram["l0f_wuT"]
    wdT = ctx.dram["l0f_wdT"]

    xr = x_in.ap().rearrange("(n p) d -> n p d", p=P)
    xo = x_out.ap().rearrange("(n p) d -> n p d", p=P)

    with contextlib.ExitStack() as st:
        E2 = st.enter_context
        ident = ctx.sb_ident
        xsp = E2(nc.sbuf_tensor("f_xsp", [P, TPS * D], F32))
        xres = [E2(nc.sbuf_tensor(f"f_xres{i}", [P, D], F32)) for i in range(2)]
        s1 = E2(nc.sbuf_tensor("f_s1", [P, NT], F32))
        s2 = E2(nc.sbuf_tensor("f_s2", [P, NT], F32))
        mu = E2(nc.sbuf_tensor("f_mu", [P, NT], F32))
        rstd = E2(nc.sbuf_tensor("f_rstd", [P, NT], F32))
        nsc = [E2(nc.sbuf_tensor(f"f_nsc{i}", [P, NT], F32)) for i in range(2)]
        dump = E2(nc.sbuf_tensor("f_dump", [P, D], F32))
        lnst = E2(nc.sbuf_tensor("f_lnst", [P, D], BF16))
        lnT = E2(nc.sbuf_tensor("f_lnT", [P, KD * SPAN], BF16))
        hmT = E2(nc.sbuf_tensor("f_hmT", [P, NHT * SPAN], BF16))
        wgt = [E2(nc.sbuf_tensor(f"f_wg{i}", [P, KD * P], BF16)) for i in range(2)]
        wut = [E2(nc.sbuf_tensor(f"f_wu{i}", [P, KD * P], BF16)) for i in range(2)]
        wdres = E2(nc.sbuf_tensor("f_wdres", [P, NHT * D], BF16))
        tanh_t = [E2(nc.sbuf_tensor(f"f_tanh{i}", [P, SPAN], F32)) for i in range(2)]
        us = E2(nc.sbuf_tensor("f_us", [P, SPAN], F32))
        rsp = E2(nc.sbuf_tensor("f_rsp", [P, 4 * D], F32))
        t1 = E2(nc.sbuf_tensor("f_t1", [P, NT], F32))
        s1b = E2(nc.sbuf_tensor("f_s1b", [P, TPS], F32))
        s2b = E2(nc.sbuf_tensor("f_s2b", [P, TPS], F32))
        mu2 = E2(nc.sbuf_tensor("f_mu2", [P, TPS], F32))
        rstd2 = E2(nc.sbuf_tensor("f_rstd2", [P, TPS], F32))
        outst = [E2(nc.sbuf_tensor(f"f_outst{i}", [P, D], F32)) for i in range(2)]
        pars = E2(nc.psum_tensor("f_pars", [P, 4096], F32))
        parsb = pars.bitcast(BF16)

        with nc.Block() as blk:
            @blk.sync
            def _(sy):
                for sp in range(NSP):
                    if sp >= 1:
                        sy.wait_ge(sem["f_dm"], 2 * TPS * sp)
                    sy.dma_start(
                        xsp[:, :],
                        xr[sp * TPS:(sp + 1) * TPS, :, :].transpose([1, 0, 2]),
                    ).then_inc(sem["f1x"], 16)
                    ctx.inc("f1x", 16)

            @blk.gpsimd
            def _(g):
                g.dma_start(
                    wdres[:, :],
                    wdT.ap().rearrange("(n p) d -> p n d", p=P),
                ).then_inc(sem["fwd"], 16)
                ctx.inc("fwd", 16)

            @blk.scalar
            def _(sc):
                for sp in range(NSP):
                    sc.wait_ge(sem["f1x"], 16 * (sp + 1))
                    for t in range(TPS):
                        g = sp * TPS + t
                        src = xsp[:, t * D:(t + 1) * D]
                        if g > 0:
                            sc.wait_ge(sem["f_dm"], 2 * g)
                        sc.activation(dump[:, :], src, AF.Copy,
                                      accum_out=s1[:, g:g + 1]
                                      ).then_inc(sem["f_dm"], 1)
                        ctx.inc("f_dm", 1)
                        sc.wait_ge(sem["f_dm"], ctx.cnt["f_dm"])
                        ins = sc.activation(dump[:, :], src, AF.Square,
                                            accum_out=s2[:, g:g + 1]
                                            ).then_inc(sem["f_dm"], 1)
                        ctx.inc("f_dm", 1)

        if getattr(ctx, "debug", False):
            with nc.Block() as blk:
                @blk.gpsimd
                def _(g):
                    g.dma_start(ctx.dram["d_s1"].ap(), s1[:, :]).then_inc(sem["f_dbg"], 16)
                    g.dma_start(ctx.dram["d_s2"].ap(), s2[:, :]).then_inc(sem["f_dbg"], 16)
                    ctx.inc("f_dbg", 32)
                    g.wait_ge(sem["f_dbg"], ctx.cnt["f_dbg"])

        plan_pre = RstdPlan(ctx, s2[:, :], rstd[:, :], nsc[0][:, :],
                            nsc[1][:, :], t1[:, :])
        with nc.Block() as blk:
            @blk.vector
            def _(v):
                ctx.mark(v.tensor_scalar_mul(mu[:, :], s1[:, :], 1.0 / D))
                ctx.mark(v.tensor_scalar_mul(s2[:, :], s2[:, :], 1.0 / D))
                v.wait_ge(sem["vch"], ctx.cnt["vch"])
                ctx.mark(v.tensor_mul(t1[:, :], mu[:, :], mu[:, :]))
                v.wait_ge(sem["vch"], ctx.cnt["vch"])
                ctx.mark(v.tensor_sub(s2[:, :], s2[:, :], t1[:, :]))
                v.wait_ge(sem["vch"], ctx.cnt["vch"])
                ctx.mark(v.tensor_scalar_add(s2[:, :], s2[:, :], EPS))
                v.wait_ge(sem["vch"], ctx.cnt["vch"])
                plan_pre.emit_vector(v)
                v.wait_ge(sem["r_s"], plan_pre.done_val)
                ctx.mark(v.tensor_mul(mu[:, :], mu[:, :], rstd[:, :]))
                v.wait_ge(sem["vch"], ctx.cnt["vch"])
                ctx.mark(v.tensor_scalar_mul(mu[:, :], mu[:, :], -1.0))
                v.wait_ge(sem["vch"], ctx.cnt["vch"])

            @blk.scalar
            def _(sc):
                plan_pre.emit_scalar(sc)

        z2 = E2(nc.sbuf_tensor("f_z2", [P, 4], F32))
        e2 = E2(nc.sbuf_tensor("f_e2", [P, 4], F32))
        tt2 = E2(nc.sbuf_tensor("f_tt2", [P, 4], F32))
        plans2 = [RstdPlan(ctx, s2b[:, 0:4], rstd2[:, 0:4], z2[:, :],
                           e2[:, :], tt2[:, :]) for _ in range(NSP * 2)]

        if getattr(ctx, "debug", False):
            with nc.Block() as blk:
                @blk.gpsimd
                def _(g):
                    g.dma_start(ctx.dram["d_rstd"].ap(), rstd[:, :]).then_inc(sem["f_dbg"], 16)
                    g.dma_start(ctx.dram["d_mu"].ap(), mu[:, :]).then_inc(sem["f_dbg"], 16)
                    ctx.inc("f_dbg", 32)
                    g.wait_ge(sem["f_dbg"], ctx.cnt["f_dbg"])

        with nc.Block() as blk:
            @blk.sync
            def _(sy):
                def res_io(sp):
                    for t in range(TPS):
                        g = sp * TPS + t
                        if g >= 2:
                            sy.wait_ge(sem["f_res"], g - 1)
                        sy.dma_start(xres[g % 2][:, :], xr[g, :, :]
                                     ).then_inc(sem["f_xr" if g % 2 == 0 else "f_xr2"], 16)
                        ctx.inc("f_xr" if g % 2 == 0 else "f_xr2", 16)
                        sy.wait_ge(sem["f_res"], g + 1)
                        sy.dma_start(xo[g, :, :],
                                     outst[g % 2][:, :]
                                     ).then_inc(sem["f_st" if g % 2 == 0 else "f_st2"], 16)
                        ctx.inc("f_st" if g % 2 == 0 else "f_st2", 16)
                for sp in range(NSP):
                    if sp >= 1:
                        sy.wait_ge(sem["f_ln"], sp * TPS)
                    sy.dma_start(
                        xsp[:, :],
                        xr[sp * TPS:(sp + 1) * TPS, :, :].transpose([1, 0, 2]),
                    ).then_inc(sem["f2x"], 16)
                    ctx.inc("f2x", 16)
                    res_io(sp)
                sy.wait_ge(sem["f_st"], ctx.cnt["f_st"])
                sy.wait_ge(sem["f_st2"], ctx.cnt["f_st2"])

            @blk.gpsimd
            def _(g):
                for sp in range(NSP):
                    for ht in range(NHT):
                        idx = sp * NHT + ht
                        if idx >= 2:
                            g.wait_ge(sem["f_gu_ev"], idx - 1)
                        g.dma_start(
                            wgt[ht % 2][:, :],
                            wgT.ap()[:, ht * P:(ht + 1) * P]
                               .rearrange("(k p) h -> p k h", p=P),
                        ).then_inc(sem["fw" if ht % 2 == 0 else "fw2"], 16)
                        g.dma_start(
                            wut[ht % 2][:, :],
                            wuT.ap()[:, ht * P:(ht + 1) * P]
                               .rearrange("(k p) h -> p k h", p=P),
                        ).then_inc(sem["fw" if ht % 2 == 0 else "fw2"], 16)
                        ctx.inc("fw" if ht % 2 == 0 else "fw2", 32)

            @blk.tensor
            def _(te):
                te.wait_ge(sem["fwd"], 16)
                for sp in range(NSP):
                    for t in range(TPS):
                        g = sp * TPS + t
                        te.wait_ge(sem["f_ln"], g + 1)
                        if g >= 2:
                            te.wait_ge(sem["f_lnT_ev"], g - 1)
                        if sp >= 1 and t == 0:
                            te.wait_ge(sem["f_r_ev"], sp * 4)
                        pcb = (t % 2) * 1024
                        for k in range(KD):
                            ins = te.transpose(
                                parsb[:, pcb + k * P:pcb + (k + 1) * P],
                                lnst[:, k * P:(k + 1) * P], ident[:, :])
                        ins.then_inc(sem["f_tr"], 1)
                        ctx.inc("f_tr", 1)
                    for ht in range(NHT):
                        idx = sp * NHT + ht
                        te.wait_ge(sem["fw" if ht % 2 == 0 else "fw2"],
                                   32 * (idx // 2 + 1))
                        if ht == 0:
                            te.wait_ge(sem["f_lnT_ev"], (sp + 1) * TPS)
                        if idx >= 2:
                            te.wait_ge(sem["f_gu_ev"], idx - 1)
                        gc = (ht % 2) * 1024
                        uc = 2048 + (ht % 2) * 1024
                        for k in range(KD):
                            wgap = wgt[ht % 2][:, k * P:(k + 1) * P]
                            wuap = wut[ht % 2][:, k * P:(k + 1) * P]
                            mv0 = lnT[:, k * SPAN:k * SPAN + 512]
                            mv1 = lnT[:, k * SPAN + 512:k * SPAN + 1024]
                            te.matmul(pars[:, gc:gc + 512], wgap, mv0,
                                      start=(k == 0), stop=(k == KD - 1))
                            te.matmul(pars[:, gc + 512:gc + 1024], wgap, mv1,
                                      start=(k == 0), stop=(k == KD - 1))
                            te.matmul(pars[:, uc:uc + 512], wuap, mv0,
                                      start=(k == 0), stop=(k == KD - 1))
                            ins = te.matmul(pars[:, uc + 512:uc + 1024], wuap, mv1,
                                            start=(k == 0), stop=(k == KD - 1))
                        ins.then_inc(sem["f_gu_mm"], 1)
                        ctx.inc("f_gu_mm", 1)
                    for q in range(4):
                        qi = sp * 4 + q
                        dc = (q % 2) * 2048
                        if q == 0:
                            te.wait_ge(sem["f_gu_ev"], (sp + 1) * NHT)
                        if qi >= 2:
                            te.wait_ge(sem["f_r_ev"], qi - 1)
                        if q >= 2:
                            te.wait_ge(sem["f_res"], sp * TPS + 2 * (q - 1))
                        for ht in range(NHT):
                            te.wait_ge(sem["f_gu_ev"], sp * NHT + ht + 1)
                            for tt in range(2):
                                stat = hmT[:, ht * SPAN + q * 256 + tt * P:
                                           ht * SPAN + q * 256 + (tt + 1) * P]
                                for nb in range(2):
                                    ins = te.matmul(
                                        pars[:, dc + tt * 1024 + nb * 512:
                                             dc + tt * 1024 + (nb + 1) * 512],
                                        stat,
                                        wdres[:, ht * D + nb * 512:ht * D + (nb + 1) * 512],
                                        start=(ht == 0), stop=(ht == NHT - 1))
                        ins.then_inc(sem["f_dn_mm"], 1)
                        ctx.inc("f_dn_mm", 1)

            @blk.scalar
            def _(sc):
                for sp in range(NSP):
                    for ht in range(NHT):
                        idx = sp * NHT + ht
                        sc.wait_ge(sem["f_gu_mm"], idx + 1)
                        if idx >= 2:
                            sc.wait_ge(sem["f_gu_ev"], idx - 1)
                        gc = (ht % 2) * 1024
                        sc.activation(tanh_t[ht % 2][:, :], pars[:, gc:gc + 1024],
                                      AF.Tanh, scale=0.5 * S,
                                      ).then_inc(sem["f_th"], 1)
                        ctx.inc("f_th", 1)
                    for half in range(2):
                        plans2[sp * 2 + half].emit_scalar(sc)

            @blk.vector
            def _(v):
                for sp in range(NSP):
                    v.wait_ge(sem["f2x"], 16 * (sp + 1))
                    for t in range(TPS):
                        g = sp * TPS + t
                        v.wait_ge(sem["f_tr"], g)
                        v.tensor_scalar(
                            lnst[:, :], xsp[:, t * D:(t + 1) * D],
                            rstd[:, g:g + 1], mu[:, g:g + 1], ALU.mult, ALU.add,
                        ).then_inc(sem["f_ln"], 1)
                        ctx.inc("f_ln", 1)
                        v.wait_ge(sem["f_tr"], g + 1)
                        pcb = (t % 2) * 1024
                        v.tensor_copy(
                            bass.AP(lnT, t * P, [[KD * SPAN, P], [SPAN, KD], [1, P]]),
                            parsb[:, pcb:pcb + KD * P],
                        ).then_inc(sem["f_lnT_ev"], 1)
                        ctx.inc("f_lnT_ev", 1)
                    for ht in range(NHT):
                        idx = sp * NHT + ht
                        gc = (ht % 2) * 1024
                        uc = 2048 + (ht % 2) * 1024
                        v.wait_ge(sem["f_th"], idx + 1)
                        ctx.mark(v.tensor_scalar_mul(us[:, :],
                                                     pars[:, uc:uc + 1024],
                                                     0.5 * S * S))
                        v.wait_ge(sem["vch"], ctx.cnt["vch"])
                        ctx.mark(v.tensor_mul(dump[:, :], pars[:, gc:gc + 1024],
                                              us[:, :]))
                        v.wait_ge(sem["vch"], ctx.cnt["vch"])
                        v.scalar_tensor_tensor(
                            hmT[:, ht * SPAN:(ht + 1) * SPAN],
                            tanh_t[ht % 2][:, :], 1.0,
                            dump[:, :], ALU.add, ALU.mult,
                        ).then_inc(sem["f_gu_ev"], 1)
                        ctx.inc("f_gu_ev", 1)
                    for half in range(2):
                        for q in (2 * half, 2 * half + 1):
                            qi = sp * 4 + q
                            v.wait_ge(sem["f_dn_mm"], qi + 1)
                            dc = (q % 2) * 2048
                            v.tensor_scalar_mul(rsp[:, (q % 2) * 2048:(q % 2) * 2048 + 2048],
                                                pars[:, dc:dc + 2048], S,
                                                ).then_inc(sem["f_r_ev"], 1)
                            ctx.inc("f_r_ev", 1)
                        v.wait_ge(sem["f_r_ev"], ctx.cnt["f_r_ev"])
                        for j in range(4):
                            src = rsp[:, j * D:(j + 1) * D]
                            v.wait_ge(sem["vch"], ctx.cnt["vch"])
                            ctx.mark(v.reduce_sum(s1b[:, j:j + 1], src, axis=AX.X))
                            ctx.mark(v.tensor_mul(dump[:, :], src, src))
                            v.wait_ge(sem["vch"], ctx.cnt["vch"])
                            ctx.mark(v.reduce_sum(s2b[:, j:j + 1], dump[:, :],
                                                  axis=AX.X))
                        v.wait_ge(sem["vch"], ctx.cnt["vch"])
                        ctx.mark(v.tensor_scalar_mul(mu2[:, 0:4], s1b[:, 0:4],
                                                     1.0 / D))
                        v.tensor_scalar_mul(s2b[:, 0:4], s2b[:, 0:4], 1.0 / D)
                        v.wait_ge(sem["vch"], ctx.cnt["vch"])
                        ctx.mark(v.tensor_mul(t1[:, 0:4], mu2[:, 0:4], mu2[:, 0:4]))
                        v.wait_ge(sem["vch"], ctx.cnt["vch"])
                        ctx.mark(v.tensor_sub(s2b[:, 0:4], s2b[:, 0:4], t1[:, 0:4]))
                        v.wait_ge(sem["vch"], ctx.cnt["vch"])
                        ctx.mark(v.tensor_scalar_add(s2b[:, 0:4], s2b[:, 0:4], EPS))
                        v.wait_ge(sem["vch"], ctx.cnt["vch"])
                        pl = plans2[sp * 2 + half]
                        pl.emit_vector(v)
                        v.wait_ge(sem["r_s"], pl.done_val)
                        ctx.mark(v.tensor_mul(mu2[:, 0:4], mu2[:, 0:4],
                                              rstd2[:, 0:4]))
                        v.wait_ge(sem["vch"], ctx.cnt["vch"])
                        ctx.mark(v.tensor_scalar_mul(mu2[:, 0:4], mu2[:, 0:4],
                                                     -1.0))
                        v.wait_ge(sem["vch"], ctx.cnt["vch"])
                        for j in range(4):
                            t = half * 4 + j
                            g = sp * TPS + t
                            if g >= 2:
                                v.wait_ge(sem["f_st" if g % 2 == 0 else "f_st2"],
                                          16 * ((g - 2) // 2 + 1))
                            v.wait_ge(sem["f_xr" if g % 2 == 0 else "f_xr2"],
                                      16 * (g // 2 + 1))
                            ob = outst[g % 2]
                            ctx.mark(v.tensor_scalar(
                                ob[:, :], rsp[:, j * D:(j + 1) * D],
                                rstd2[:, j:j + 1], mu2[:, j:j + 1], ALU.mult, ALU.add))
                            v.wait_ge(sem["vch"], ctx.cnt["vch"])
                            v.tensor_add(ob[:, :], ob[:, :],
                                         xres[g % 2][:, :],
                                         ).then_inc(sem["f_res"], 1)
                            ctx.inc("f_res", 1)






def emit_l0_attn(ctx, x_in, x_out):
    nc = ctx.nc
    DI = 2048
    NH = 32
    HD = 64
    CD = 2304
    NJ = CD // P
    DIP = 4384
    SPAN = 512
    NSP2 = getattr(ctx, "dbg_nsp2", TOK // SPAN)
    CPS = SPAN // P
    HALO = 4
    JW = SPAN + HALO

    sems = ("a1x", "a1y", "a2y", "a_xr2", "a_st2", "a_dm", "a2x", "a_ln", "a_tr", "a_lnT_ev", "aw",
            "a_z_mm", "a_z_ev", "a_xb_mm", "a_xb_ev", "a_cvacc", "a_sg",
            "a_cv", "a_dt_mm", "a_pc", "a_cs_mm", "a_tr7", "a_b7",
            "a_g_mm", "a_g_ev", "a_r_mm", "a_ex2", "a_mt", "a_yd_mm",
            "a_sbf", "a_yo_mm", "a_bt_rdy", "a_sc_mm", "a_sp_ev", "a_st_mm",
            "a_su", "a_ep", "a_ye", "a_sgz", "a_gat", "a_zg", "a_sq",
            "a_yn", "a_ynT_ev", "a_op_mm", "a_op_ev", "a_xr", "a_res",
            "a_st", "a_ex", "a_bt_ev", "a_dtA", "a_dsc", "a_csd", "a_dsub",
            "a_dsr", "a_efac", "a_btd", "a_wo", "a_oph", "a_zg1", "a_dmk")
    for s in sems:
        ctx.get_sem(s)
    sem = ctx.sem

    win = ctx.dram["l0a_win"]
    wout = ctx.dram["l0a_wout"]
    xr = x_in.ap().rearrange("(n p) d -> n p d", p=P)
    xo = x_out.ap().rearrange("(n p) d -> n p d", p=P)

    with contextlib.ExitStack() as st:
        E2 = st.enter_context
        ident = ctx.sb_ident
        pars = E2(nc.psum_tensor("a_pars", [P, 4096], F32))
        parsb = pars.bitcast(BF16)
        s1 = E2(nc.sbuf_tensor("a_s1", [P, NT], F32))
        s2 = E2(nc.sbuf_tensor("a_s2", [P, NT], F32))
        a1 = E2(nc.sbuf_tensor("a_a1", [P, NT], F32))
        c1 = E2(nc.sbuf_tensor("a_c1", [P, NT], F32))
        nz = [E2(nc.sbuf_tensor(f"a_nz{i}", [P, NT], F32)) for i in range(3)]
        xsp = [E2(nc.sbuf_tensor(f"a_xsp{i}", [P, D], F32)) for i in range(2)]
        lnst = E2(nc.sbuf_tensor("a_lnst", [P, D], BF16))
        lnTs = E2(nc.sbuf_tensor("a_lnTs", [P, KD * SPAN], BF16))
        winb = E2(nc.sbuf_tensor("a_winb", [P, KD * DIP], BF16))
        woutb = E2(nc.sbuf_tensor("a_woutb", [P, 8 * D], BF16))
        cwb = E2(nc.sbuf_tensor("a_cwb", [P, NJ * 4], F32))
        cbb = E2(nc.sbuf_tensor("a_cbb", [P, NJ], F32))
        dtbb = E2(nc.sbuf_tensor("a_dtbb", [P, NH], F32))
        ab = E2(nc.sbuf_tensor("a_ab", [P, NH], F32))
        dpb = E2(nc.sbuf_tensor("a_dpb", [P, NH], F32))
        nwb = E2(nc.sbuf_tensor("a_nwb", [P, DI], BF16))
        triub = E2(nc.sbuf_tensor("a_triub", [P, P], F32))
        onesb = E2(nc.sbuf_tensor("a_onesb", [P, P], F32))
        hselb = E2(nc.sbuf_tensor("a_hselb", [32, NH * P], F32))
        xbc = E2(nc.sbuf_tensor("a_xbc", [P, NJ * JW], BF16))
        xtail = E2(nc.sbuf_tensor("a_xtail", [P, NJ * HALO], BF16))
        cvt = E2(nc.sbuf_tensor("a_cvt", [P, SPAN], F32))
        sgc = E2(nc.sbuf_tensor("a_sgc", [P, SPAN], BF16))
        zsb = E2(nc.sbuf_tensor("a_zsb", [P, DI], BF16))
        dtc = E2(nc.sbuf_tensor("a_dtc", [P, NH], F32))
        act_ = E2(nc.sbuf_tensor("a_ac", [P, NH], F32))
        acsT = E2(nc.sbuf_tensor("a_acsT", [P, NH], F32))
        nacsT = E2(nc.sbuf_tensor("a_nacsT", [P, NH], F32))
        acshm = E2(nc.sbuf_tensor("a_acshm", [32, P], F32))
        dsdt = E2(nc.sbuf_tensor("a_dsdt", [P, NH], F32))
        einb = E2(nc.sbuf_tensor("a_einb", [P, NH], F32))
        etbv = E2(nc.sbuf_tensor("a_etbv", [P, NH], F32))
        tA = E2(nc.sbuf_tensor("a_tA", [P, NH], F32))
        jnk = E2(nc.sbuf_tensor("a_jnk", [P, 2], F32))
        tB = E2(nc.sbuf_tensor("a_tB", [P, NH], F32))
        tC = E2(nc.sbuf_tensor("a_tC", [P, NH], F32))
        zdt = E2(nc.sbuf_tensor("a_zdt", [P, NH], F32))
        edt = E2(nc.sbuf_tensor("a_edt", [P, NH], F32))
        tdt = E2(nc.sbuf_tensor("a_tdt", [P, NH], F32))
        lnp = E2(nc.sbuf_tensor("a_lnp", [P, NH], F32))
        x_tm = E2(nc.sbuf_tensor("a_x_tm", [P, DI], BF16))
        btm = E2(nc.sbuf_tensor("a_btm", [P, P], BF16))
        bsc = E2(nc.sbuf_tensor("a_bsc", [P, P], BF16))
        gtm = E2(nc.sbuf_tensor("a_gtm", [P, P], F32))
        expdt = E2(nc.sbuf_tensor("a_expdt", [P, P], F32))
        dmm = E2(nc.sbuf_tensor("a_dmm", [P, P], F32))
        mtb = E2(nc.sbuf_tensor("a_mtb", [P, P], BF16))
        sbf = E2(nc.sbuf_tensor("a_sbf", [P, HD], BF16))
        spr = E2(nc.sbuf_tensor("a_spr", [64, P], BF16))
        S = E2(nc.sbuf_tensor("a_S", [P, NH * HD], F32))
        epart = E2(nc.sbuf_tensor("a_epart", [P, HD], F32))
        t2b = E2(nc.sbuf_tensor("a_t2b", [P, HD], F32))
        ych = E2(nc.sbuf_tensor("a_ych", [P, DI], F32))
        ynb = E2(nc.sbuf_tensor("a_ynb", [P, DI], BF16))
        ynT = E2(nc.sbuf_tensor("a_ynT", [P, 16 * P], BF16))
        s2c = E2(nc.sbuf_tensor("a_s2c", [P, 1], F32))
        rstdc = E2(nc.sbuf_tensor("a_rstdc", [P, 1], F32))
        zc1 = E2(nc.sbuf_tensor("a_zc1", [P, 1], F32))
        ec1 = E2(nc.sbuf_tensor("a_ec1", [P, 1], F32))
        tc1 = E2(nc.sbuf_tensor("a_tc1", [P, 1], F32))
        s1r = E2(nc.sbuf_tensor("a_s1r", [P, 1], F32))
        s2r = E2(nc.sbuf_tensor("a_s2r", [P, 1], F32))
        mur = E2(nc.sbuf_tensor("a_mur", [P, 1], F32))
        rstdr = E2(nc.sbuf_tensor("a_rstdr", [P, 1], F32))
        rsb = E2(nc.sbuf_tensor("a_rsb", [P, D], F32))
        xres = [E2(nc.sbuf_tensor(f"a_xres{i}", [P, D], F32)) for i in range(2)]
        outst = [E2(nc.sbuf_tensor(f"a_outst{i}", [P, D], F32)) for i in range(2)]

        plans_dt, plans_rm, plans_po = [], [], []

        with nc.Block() as blk:
            @blk.gpsimd
            def _(g):
                g.dma_start(winb[:, :], win.ap().rearrange("(k p) d -> p k d", p=P)
                            ).then_inc(sem["aw"], 16)
                g.dma_start(woutb[:, :],
                            wout.ap()[0:1024, :].rearrange("(k p) d -> p k d", p=P)
                            ).then_inc(sem["aw"], 16)
                for t, nmk in ((cwb, "l0a_cw"), (cbb, "l0a_cb"), (dtbb, "l0a_dtb"),
                               (ab, "l0a_alog"), (dpb, "l0a_dpar"), (nwb, "l0a_nw"),
                               (triub, "l0a_triu"), (onesb, "l0a_ones"),
                               (hselb, "l0a_hsel"),):
                    g.dma_start(t[:, :], ctx.dram[nmk].ap()).then_inc(sem["aw"], 16)
                ctx.inc("aw", 16 * 11)

            @blk.sync
            def _(sy):
                for g in range(NT):
                    if g >= 2:
                        sy.wait_ge(sem["a_dm"], 4 * (g - 1))
                    sy.dma_start(xsp[g % 2][:, :], xr[g, :, :]
                                 ).then_inc(sem["a1x" if g % 2 == 0 else "a1y"], 16)
                    ctx.inc("a1x" if g % 2 == 0 else "a1y", 16)

            @blk.scalar
            def _(sc):
                for g in range(NT):
                    sc.wait_ge(sem["a1x" if g % 2 == 0 else "a1y"],
                               16 * (g // 2 + 1))
                    src = xsp[g % 2][:, :]
                    if g > 0:
                        sc.wait_ge(sem["a_dm"], 4 * g)
                    sc.activation(ych[:, 0:D], src, AF.Copy,
                                  accum_out=s1[:, g:g + 1]).then_inc(sem["a_dm"], 1)
                    ctx.inc("a_dm", 1)
                    sc.wait_ge(sem["a_dm"], ctx.cnt["a_dm"])
                    sc.activation(ych[:, 0:D], src, AF.Square,
                                  accum_out=s2[:, g:g + 1]).then_inc(sem["a_dm"], 3)
                    ctx.inc("a_dm", 3)

        planA = RstdPlan(ctx, s2[:, :], a1[:, :], nz[0][:, :], nz[1][:, :],
                         nz[2][:, :])
        with nc.Block() as blk:
            @blk.vector
            def _(v):
                ctx.mark(v.tensor_scalar_mul(c1[:, :], s1[:, :], 1.0 / D))
                ctx.mark(v.tensor_scalar_mul(s2[:, :], s2[:, :], 1.0 / D))
                v.wait_ge(sem["vch"], ctx.cnt["vch"])
                ctx.mark(v.tensor_mul(nz[0][:, :], c1[:, :], c1[:, :]))
                v.wait_ge(sem["vch"], ctx.cnt["vch"])
                ctx.mark(v.tensor_sub(s2[:, :], s2[:, :], nz[0][:, :]))
                v.wait_ge(sem["vch"], ctx.cnt["vch"])
                ctx.mark(v.tensor_scalar_add(s2[:, :], s2[:, :], EPS))
                v.wait_ge(sem["vch"], ctx.cnt["vch"])
                planA.emit_vector(v)
                v.wait_ge(sem["r_s"], planA.done_val)
                ctx.mark(v.tensor_mul(c1[:, :], c1[:, :], a1[:, :]))
                v.wait_ge(sem["vch"], ctx.cnt["vch"])
                ctx.mark(v.tensor_scalar_mul(c1[:, :], c1[:, :], -1.0))
                v.wait_ge(sem["vch"], ctx.cnt["vch"])
                v.wait_ge(sem["a_ex"], 1)
                ctx.mark(v.tensor_scalar_mul(ab[:, :], ab[:, :], -1.0))
                v.wait_ge(sem["vch"], ctx.cnt["vch"])
                ctx.mark(v.memset(S[:, :], 0.0))
                ctx.mark(v.memset(jnk[:, :], 0.0))
                v.wait_ge(sem["vch"], ctx.cnt["vch"])

            @blk.scalar
            def _(sc):
                planA.emit_scalar(sc)
                sc.wait_ge(sem["aw"], 16 * 11)
                sc.activation(ab[:, :], ab[:, :], AF.Exp).then_inc(sem["a_ex"], 1)
                ctx.inc("a_ex", 1)

        for _ch in range(NT):
            plans_dt.append(RstdPlan(ctx, tB[:, :], lnp[:, :], zdt[:, :],
                                     edt[:, :], tdt[:, :], raw_ln=True))
            plans_rm.append(RstdPlan(ctx, s2c[:, :], rstdc[:, :], zc1[:, :],
                                     ec1[:, :], tc1[:, :]))
            plans_po.append(RstdPlan(ctx, s2r[:, :], rstdr[:, :], zc1[:, :],
                                     ec1[:, :], tc1[:, :]))

        with nc.Block() as blk:
            @blk.sync
            def _(sy):
                for sp in range(NSP2):
                    for t in range(CPS):
                        g = sp * CPS + t
                        if g >= 2:
                            sy.wait_ge(sem["a_ln"], g - 1)
                        sy.dma_start(xsp[g % 2][:, :], xr[g, :, :]
                                     ).then_inc(sem["a2x" if g % 2 == 0 else "a2y"], 16)
                        ctx.inc("a2x" if g % 2 == 0 else "a2y", 16)
                    for cl in range(CPS):
                        ch = sp * CPS + cl
                        if ch >= 2:
                            sy.wait_ge(sem["a_res"], ch - 1)
                        sy.dma_start(xres[ch % 2][:, :], xr[ch, :, :]
                                     ).then_inc(sem["a_xr" if ch % 2 == 0 else "a_xr2"], 16)
                        ctx.inc("a_xr" if ch % 2 == 0 else "a_xr2", 16)
                        sy.wait_ge(sem["a_res"], ch + 1)
                        sy.dma_start(xo[ch, :, :], outst[ch % 2][:, :]
                                     ).then_inc(sem["a_st" if ch % 2 == 0 else "a_st2"], 16)
                        ctx.inc("a_st" if ch % 2 == 0 else "a_st2", 16)
                sy.wait_ge(sem["a_st"], ctx.cnt["a_st"])
                sy.wait_ge(sem["a_st2"], ctx.cnt["a_st2"])

            @blk.gpsimd
            def _(g):
                for ch in range(NSP2 * CPS):
                    for half in range(2):
                        hi = ch * 2 + half
                        if hi == 0:
                            continue
                        g.wait_ge(sem["a_oph"], hi)
                        src_half = (hi % 2)
                        g.dma_start(
                            woutb[:, :],
                            wout.ap()[src_half * 1024:(src_half + 1) * 1024, :]
                                .rearrange("(k p) d -> p k d", p=P)
                        ).then_inc(sem["a_wo"], 16)
                        ctx.inc("a_wo", 16)

            @blk.tensor
            def _(te):
                te.wait_ge(sem["aw"], 16 * 11)
                for sp in range(NSP2):
                    xb = xbc
                    for t in range(CPS):
                        g = sp * CPS + t
                        te.wait_ge(sem["a_ln"], g + 1)
                        if g >= 2:
                            te.wait_ge(sem["a_lnT_ev"], g - 1)
                        if sp >= 1 and t == 0:
                            te.wait_ge(sem["a_dmk"], sp * CPS * NH)
                            te.wait_ge(sem["a_z_ev"], sp * CPS)
                            te.wait_ge(sem["a_g_ev"], sp * CPS)
                        pcb = (t % 2) * 1024
                        for k in range(KD):
                            ins = te.transpose(parsb[:, pcb + k * P:pcb + (k + 1) * P],
                                               lnst[:, k * P:(k + 1) * P],
                                               ident[:, :])
                        ins.then_inc(sem["a_tr"], 1)
                        ctx.inc("a_tr", 1)
                    te.wait_ge(sem["a_lnT_ev"], (sp + 1) * CPS)
                    if sp >= 1:
                        te.wait_ge(sem["a_op_ev"], sp * CPS)
                        te.wait_ge(sem["a_su"], sp * CPS * NH)
                        te.wait_ge(sem["a_ep"], sp * CPS * NH)
                    for j in range(NJ):
                        jj = sp * NJ + j
                        pc = 2048 + (j % 3) * 512
                        if jj >= 3:
                            te.wait_ge(sem["a_xb_ev"], jj - 2)
                        for k in range(KD):
                            ins = te.matmul(pars[:, pc:pc + 512],
                                            winb[:, k * DIP + 2048 + j * P:
                                                 k * DIP + 2048 + (j + 1) * P],
                                            lnTs[:, k * SPAN:(k + 1) * SPAN],
                                            start=(k == 0), stop=(k == KD - 1))
                        ins.then_inc(sem["a_xb_mm"], 1)
                        ctx.inc("a_xb_mm", 1)
                    for cl in range(CPS):
                        ch = sp * CPS + cl
                        co = HALO + cl * P
                        if ch >= 1:
                            te.wait_ge(sem["a_ye"], ch * NH)
                            te.wait_ge(sem["a_dmk"], ch * NH)
                            te.wait_ge(sem["a_zg"], ch)
                        for k in range(KD):
                            stat = lnTs[:, k * SPAN + cl * P:k * SPAN + (cl + 1) * P]
                            for nb in range(4):
                                ins = te.matmul(pars[:, nb * 512:(nb + 1) * 512],
                                                stat,
                                                winb[:, k * DIP + nb * 512:
                                                     k * DIP + (nb + 1) * 512],
                                                start=(k == 0), stop=(k == KD - 1))
                        ins.then_inc(sem["a_z_mm"], 1)
                        ctx.inc("a_z_mm", 1)
                        te.wait_ge(sem["a_b7"], 6 * ch)
                        for k in range(KD):
                            stat = lnTs[:, k * SPAN + cl * P:k * SPAN + (cl + 1) * P]
                            ins = te.matmul(pars[:, 3584:3616], stat,
                                            winb[:, k * DIP + 4352:k * DIP + 4384],
                                            start=(k == 0), stop=(k == KD - 1))
                        ins.then_inc(sem["a_dt_mm"], 1)
                        ctx.inc("a_dt_mm", 1)
                        te.wait_ge(sem["a_pc"], ch + 1)
                        te.wait_ge(sem["a_b7"], 6 * ch + 1)
                        te.matmul(pars[:, 3616:3648], triub[:, :], act_[:, :],
                                  start=True, stop=True)
                        te.matmul(bass.AP(pars, 3648, [[4096, 32], [1, P]]),
                                  act_[:, :], triub[:, :], start=True, stop=True)
                        te.matmul(pars[:, 3776:3808], onesb[:, :], act_[:, :],
                                  start=True, stop=True).then_inc(sem["a_cs_mm"], 1)
                        ctx.inc("a_cs_mm", 1)
                        te.wait_ge(sem["a_cv"], sp * NJ + NJ)
                        for grp in range(2):
                            te.wait_ge(sem["a_b7"], 6 * ch + 2 + grp)
                            for jx in range(grp * 8, grp * 8 + 8):
                                ins = te.transpose(
                                    parsb[:, 7168 + (jx % 8) * P:
                                          7168 + (jx % 8) * P + P],
                                    xb[:, jx * JW + co:jx * JW + co + P],
                                    ident[:, :])
                            ins.then_inc(sem["a_tr7"], 1)
                            ctx.inc("a_tr7", 1)
                        if ch >= 1:
                            te.wait_ge(sem["a_su"], ch * NH)
                        te.transpose(parsb[:, 6144:6144 + P],
                                     xb[:, 16 * JW + co:16 * JW + co + P],
                                     ident[:, :]).then_inc(sem["a_bt_ev"], 1)
                        ctx.inc("a_bt_ev", 1)
                        te.wait_ge(sem["a_z_ev"], ch + 1)
                        if ch >= 1:
                            te.wait_ge(sem["a_g_ev"], ch)
                        te.matmul(pars[:, 0:P],
                                  xb[:, 16 * JW + co:16 * JW + co + P],
                                  xb[:, 17 * JW + co:17 * JW + co + P],
                                  start=True, stop=True).then_inc(sem["a_g_mm"], 1)
                        ctx.inc("a_g_mm", 1)
                        for h in range(NH):
                            hp = h % 2
                            hidx = ch * NH + h
                            if h == 0:
                                te.wait_ge(sem["a_g_ev"], ch + 1)
                            if h >= 2:
                                te.wait_ge(sem["a_dmk"], hidx - 1)
                            te.matmul(pars[:, hp * 512:hp * 512 + P],
                                      hselb[:, h * P:(h + 1) * P], acshm[:, :],
                                      start=True, stop=True).then_inc(sem["a_r_mm"], 1)
                            ctx.inc("a_r_mm", 1)
                            te.wait_ge(sem["a_mt"], hidx + 1)
                            if h >= 2:
                                te.wait_ge(sem["a_ye"], hidx - 1)
                            te.matmul(pars[:, 1024 + hp * 512:1024 + hp * 512 + HD],
                                      mtb[:, :], x_tm[:, h * HD:(h + 1) * HD],
                                      start=True, stop=True).then_inc(sem["a_yd_mm"], 1)
                            ctx.inc("a_yd_mm", 1)
                            te.wait_ge(sem["a_sbf"], hidx + 1)
                            if h >= 2:
                                te.wait_ge(sem["a_ep"], hidx - 1)
                            elif ch >= 1:
                                te.wait_ge(sem["a_op_ev"], ch)
                            te.matmul(pars[:, 2048 + hp * 512:2048 + hp * 512 + HD],
                                      xb[:, 17 * JW + co:17 * JW + co + P],
                                      sbf[:, :], start=True, stop=True
                                      ).then_inc(sem["a_yo_mm"], 1)
                            ctx.inc("a_yo_mm", 1)
                            te.wait_ge(sem["a_bt_rdy"], hidx + 1)
                            if h == 0:
                                pass
                            else:
                                te.wait_ge(sem["a_sp_ev"], hidx)
                            te.matmul(bass.AP(pars, 3072, [[4096, HD], [1, P]]),
                                      x_tm[:, h * HD:(h + 1) * HD], bsc[:, :],
                                      start=True, stop=True).then_inc(sem["a_sc_mm"], 1)
                            ctx.inc("a_sc_mm", 1)
                            te.wait_ge(sem["a_sp_ev"], hidx + 1)
                            if h >= 1:
                                te.wait_ge(sem["a_su"], hidx)
                            te.transpose(parsb[:, 6400:6400 + HD], spr[:, :],
                                         ident[0:HD, 0:HD]).then_inc(sem["a_st_mm"], 1)
                            ctx.inc("a_st_mm", 1)
                        te.wait_ge(sem["a_yn"], ch + 1)
                        for grp in range(2):
                            te.wait_ge(sem["a_b7"], 6 * ch + 4 + grp)
                            for k2 in range(grp * 8, grp * 8 + 8):
                                ins = te.transpose(
                                    parsb[:, 7168 + (k2 % 8) * P:
                                          7168 + (k2 % 8) * P + P],
                                    ynb[:, k2 * P:(k2 + 1) * P], ident[:, :])
                            ins.then_inc(sem["a_tr7"], 1)
                            ctx.inc("a_tr7", 1)
                        te.wait_ge(sem["a_ye"], (ch + 1) * NH)
                        te.wait_ge(sem["a_ep"], (ch + 1) * NH)
                        te.wait_ge(sem["a_ynT_ev"], ch + 1)
                        for half in range(2):
                            hi = ch * 2 + half
                            if hi > 0:
                                te.wait_ge(sem["a_wo"], 16 * hi)
                            for k2h in range(8):
                                k2 = half * 8 + k2h
                                stat = ynT[:, k2 * P:(k2 + 1) * P]
                                for nb in range(2):
                                    ins = te.matmul(pars[:, 2048 + nb * 512:
                                                         2048 + (nb + 1) * 512],
                                                    stat,
                                                    woutb[:, k2h * D + nb * 512:
                                                          k2h * D + (nb + 1) * 512],
                                                    start=(k2 == 0), stop=(k2 == 15))
                            ins.then_inc(sem["a_oph"], 1)
                            ctx.inc("a_oph", 1)

            @blk.scalar
            def _(sc):
                for sp in range(NSP2):
                    for j in range(NJ):
                        jj = sp * NJ + j
                        sc.wait_ge(sem["a_cvacc"], jj + 1)
                        if jj >= 1:
                            sc.wait_ge(sem["a_cv"], jj)
                        sc.activation(sgc[:, :], cvt[:, :], AF.Sigmoid
                                      ).then_inc(sem["a_sg"], 1)
                        ctx.inc("a_sg", 1)
                    for cl in range(CPS):
                        ch = sp * CPS + cl
                        sc.wait_ge(sem["a_dtA"], ch + 1)
                        sc.activation(tC[:, :], tA[:, :], AF.Abs).then_inc(sem["a_dsc"], 1)
                        ctx.inc("a_dsc", 1)
                        sc.wait_ge(sem["a_dsc"], ctx.cnt["a_dsc"])
                        sc.activation(tC[:, :], tC[:, :], AF.Exp, scale=-1.0
                                      ).then_inc(sem["a_dsc"], 1)
                        ctx.inc("a_dsc", 1)
                        plans_dt[ch].emit_scalar(sc)
                        sc.wait_ge(sem["a_dsc"], ctx.cnt["a_dsc"] + 0)
                        sc.activation(tC[:, :], tA[:, :], AF.Relu).then_inc(sem["a_dsc"], 1)
                        ctx.inc("a_dsc", 1)
                        sc.wait_ge(sem["a_csd"], 2 * ch + 1)
                        sc.activation(einb[:, :], acsT[:, :], AF.Exp
                                      ).then_inc(sem["a_efac"], 1)
                        ctx.inc("a_efac", 1)
                        sc.wait_ge(sem["a_csd"], 2 * ch + 2)
                        sc.activation(etbv[:, :], tB[:, :], AF.Exp).then_inc(sem["a_efac"], 1)
                        ctx.inc("a_efac", 1)
                        sc.wait_ge(sem["a_dsub"], ch + 1)
                        sc.activation(dsdt[:, :], dsdt[:, :], AF.Exp
                                      ).then_inc(sem["a_efac"], 1)
                        ctx.inc("a_efac", 1)
                        for h in range(NH):
                            hp = h % 2
                            hidx = ch * NH + h
                            sc.wait_ge(sem["a_dmk"], hidx + 1)
                            if h >= 1:
                                sc.wait_ge(sem["a_mt"], hidx)
                            sc.activation(expdt[:, :], dmm[:, :], AF.Exp
                                          ).then_inc(sem["a_ex2"], 1)
                            ctx.inc("a_ex2", 1)
                            sc.wait_ge(sem["a_yo_mm"], hidx + 1)
                            if h >= 1:
                                sc.wait_ge(sem["a_ye"], hidx)
                            sc.activation(epart[:, :],
                                          pars[:, 2048 + hp * 512:2048 + hp * 512 + HD],
                                          AF.Copy, scale=einb[:, h:h + 1]
                                          ).then_inc(sem["a_ep"], 1)
                            ctx.inc("a_ep", 1)
                        sc.wait_ge(sem["a_zg1"], ch + 1)
                        sc.activation(zsb[:, :], zsb[:, :], AF.Sigmoid
                                      ).then_inc(sem["a_sgz"], 1)
                        ctx.inc("a_sgz", 1)
                        sc.wait_ge(sem["a_gat"], ch + 1)
                        sc.activation(ynb[:, :], ych[:, :], AF.Square,
                                      accum_out=s2c[:, :]).then_inc(sem["a_sq"], 1)
                        ctx.inc("a_sq", 1)
                        plans_rm[ch].emit_scalar(sc)
                        plans_po[ch].emit_scalar(sc)

            @blk.vector
            def _(v):
                for sp in range(NSP2):
                    xb = xbc
                    for t in range(CPS):
                        g = sp * CPS + t
                        v.wait_ge(sem["a2x" if g % 2 == 0 else "a2y"],
                                  16 * (g // 2 + 1))
                        v.wait_ge(sem["a_tr"], g)
                        v.tensor_scalar(lnst[:, :], xsp[g % 2][:, :],
                                        a1[:, g:g + 1], c1[:, g:g + 1],
                                        ALU.mult, ALU.add).then_inc(sem["a_ln"], 1)
                        ctx.inc("a_ln", 1)
                        v.wait_ge(sem["a_tr"], g + 1)
                        pcb = (t % 2) * 1024
                        v.tensor_copy(
                            bass.AP(lnTs, t * P, [[KD * SPAN, P], [SPAN, KD], [1, P]]),
                            parsb[:, pcb:pcb + KD * P]).then_inc(sem["a_lnT_ev"], 1)
                        ctx.inc("a_lnT_ev", 1)
                    for j in range(NJ):
                        jj = sp * NJ + j
                        pc = 2048 + (j % 3) * 512
                        v.wait_ge(sem["a_xb_mm"], jj + 1)
                        if sp % 4 == 0:
                            ctx.mark(v.memset(xb[:, j * JW:j * JW + HALO], 0.0))
                        else:
                            ctx.mark(v.tensor_copy(
                                xb[:, j * JW:j * JW + HALO],
                                xtail[:, j * HALO:(j + 1) * HALO]))
                        ctx.mark(v.tensor_copy(xb[:, j * JW + HALO:j * JW + JW],
                                               pars[:, pc:pc + 512]))
                        v.wait_ge(sem["vch"], ctx.cnt["vch"])
                        v.engine_nop().then_inc(sem["a_xb_ev"], 1)
                        ctx.inc("a_xb_ev", 1)
                        ctx.mark(v.tensor_copy(xtail[:, j * HALO:(j + 1) * HALO],
                                               xb[:, j * JW + SPAN:j * JW + JW]))
                        v.wait_ge(sem["vch"], ctx.cnt["vch"])
                        if jj >= 1:
                            v.wait_ge(sem["a_sg"], jj)
                        ctx.mark(v.tensor_scalar(cvt[:, :],
                                                 xb[:, j * JW + 1:j * JW + 1 + SPAN],
                                                 cwb[:, 4 * j:4 * j + 1], None,
                                                 ALU.mult))
                        for i in (1, 2, 3):
                            v.wait_ge(sem["vch"], ctx.cnt["vch"])
                            ctx.mark(v.scalar_tensor_tensor(
                                cvt[:, :], xb[:, j * JW + 1 + i:j * JW + 1 + i + SPAN],
                                cwb[:, 4 * j + i:4 * j + i + 1], cvt[:, :],
                                ALU.mult, ALU.add))
                        v.wait_ge(sem["vch"], ctx.cnt["vch"])
                        ctx.mark(v.tensor_scalar(cvt[:, :], cvt[:, :],
                                                 cbb[:, j:j + 1], None, ALU.add))
                        v.wait_ge(sem["vch"], ctx.cnt["vch"])
                        v.engine_nop().then_inc(sem["a_cvacc"], 1)
                        ctx.inc("a_cvacc", 1)
                        v.wait_ge(sem["a_sg"], jj + 1)
                        v.tensor_mul(xb[:, j * JW + HALO:j * JW + JW],
                                     cvt[:, :], sgc[:, :]).then_inc(sem["a_cv"], 1)
                        ctx.inc("a_cv", 1)
                    for cl in range(CPS):
                        ch = sp * CPS + cl
                        co = HALO + cl * P
                        v.wait_ge(sem["a_dt_mm"], ch + 1)
                        if ch >= 1:
                            v.wait_ge(sem["a_dsc"], 3 * ch)
                        ctx.mark(v.tensor_copy(tA[:, :], pars[:, 3584:3616]))
                        v.wait_ge(sem["vch"], ctx.cnt["vch"])
                        ctx.mark(v.tensor_add(tA[:, :], tA[:, :], dtbb[:, :]))
                        v.wait_ge(sem["vch"], ctx.cnt["vch"])
                        v.engine_nop().then_inc(sem["a_dtA"], 1)
                        ctx.inc("a_dtA", 1)
                        v.engine_nop().then_inc(sem["a_b7"], 1)
                        ctx.inc("a_b7", 1)
                        v.wait_ge(sem["a_dsc"], 2 + 3 * ch)
                        ctx.mark(v.tensor_scalar_add(tB[:, :], tC[:, :], 1.0))
                        v.wait_ge(sem["vch"], ctx.cnt["vch"])
                        pld = plans_dt[ch]
                        pld.emit_vector(v)
                        v.wait_ge(sem["r_s"], pld.done_val)
                        v.wait_ge(sem["a_dsc"], 3 + 3 * ch)
                        ctx.mark(v.tensor_add(dtc[:, :], lnp[:, :], tC[:, :]))
                        v.wait_ge(sem["vch"], ctx.cnt["vch"])
                        ctx.mark(v.tensor_mul(act_[:, :], dtc[:, :], ab[:, :]))
                        v.wait_ge(sem["vch"], ctx.cnt["vch"])
                        v.engine_nop().then_inc(sem["a_pc"], 1)
                        ctx.inc("a_pc", 1)
                        v.wait_ge(sem["a_cs_mm"], ch + 1)
                        ctx.mark(v.tensor_copy(acsT[:, :], pars[:, 3616:3648]))
                        ctx.mark(v.tensor_copy(acshm[:, :],
                                               bass.AP(pars, 3648, [[4096, 32], [1, P]])))
                        ctx.mark(v.tensor_copy(tB[:, :], pars[:, 3776:3808]))
                        v.wait_ge(sem["vch"], ctx.cnt["vch"])
                        v.engine_nop().then_inc(sem["a_b7"], 1)
                        ctx.inc("a_b7", 1)
                        ctx.mark(v.tensor_scalar_mul(nacsT[:, :], acsT[:, :], -1.0))
                        v.wait_ge(sem["vch"], ctx.cnt["vch"])
                        v.engine_nop().then_inc(sem["a_csd"], 1)
                        ctx.inc("a_csd", 1)
                        v.engine_nop().then_inc(sem["a_csd"], 1)
                        ctx.inc("a_csd", 1)
                        ctx.mark(v.tensor_sub(dsdt[:, :], tB[:, :], acsT[:, :]))
                        v.wait_ge(sem["vch"], ctx.cnt["vch"])
                        v.engine_nop().then_inc(sem["a_dsub"], 1)
                        ctx.inc("a_dsub", 1)
                        v.wait_ge(sem["a_efac"], 3 * ch + 3)
                        ctx.mark(v.tensor_mul(dsdt[:, :], dsdt[:, :], dtc[:, :]))
                        v.wait_ge(sem["vch"], ctx.cnt["vch"])
                        v.engine_nop().then_inc(sem["a_dsr"], 1)
                        ctx.inc("a_dsr", 1)
                        v.wait_ge(sem["a_z_mm"], ch + 1)
                        if ch >= 1:
                            v.wait_ge(sem["a_gat"], ch)
                        ctx.mark(v.tensor_copy(zsb[:, :], pars[:, 0:2048]))
                        v.wait_ge(sem["vch"], ctx.cnt["vch"])
                        v.engine_nop().then_inc(sem["a_z_ev"], 1)
                        ctx.inc("a_z_ev", 1)
                        for grp in range(2):
                            v.wait_ge(sem["a_tr7"], ch * 4 + grp + 1)
                            ctx.mark(v.tensor_copy(
                                x_tm[:, grp * 8 * P:(grp + 1) * 8 * P],
                                parsb[:, 7168:7168 + 8 * P]))
                            v.wait_ge(sem["vch"], ctx.cnt["vch"])
                            v.engine_nop().then_inc(sem["a_b7"], 1)
                            ctx.inc("a_b7", 1)
                        v.wait_ge(sem["a_bt_ev"], ch + 1)
                        ctx.mark(v.tensor_copy(btm[:, :], parsb[:, 6144:6144 + P]))
                        v.wait_ge(sem["vch"], ctx.cnt["vch"])
                        v.engine_nop().then_inc(sem["a_btd"], 1)
                        ctx.inc("a_btd", 1)
                        v.wait_ge(sem["a_g_mm"], ch + 1)
                        ctx.mark(v.tensor_mul(gtm[:, :], pars[:, 0:P], triub[:, :]))
                        v.wait_ge(sem["vch"], ctx.cnt["vch"])
                        v.engine_nop().then_inc(sem["a_g_ev"], 1)
                        ctx.inc("a_g_ev", 1)
                        if ch % 16 == 0 and ch > 0:
                            ctx.mark(v.memset(S[:, :], 0.0))
                            v.wait_ge(sem["vch"], ctx.cnt["vch"])
                        for h in range(NH):
                            hp = h % 2
                            hidx = ch * NH + h
                            v.wait_ge(sem["a_r_mm"], hidx + 1)
                            if h >= 1:
                                v.wait_ge(sem["a_ex2"], hidx)
                            v.scalar_tensor_tensor(
                                dmm[:, :], pars[:, hp * 512:hp * 512 + P],
                                nacsT[:, h:h + 1], triub[:, :], ALU.add, ALU.mult
                                ).then_inc(sem["a_dmk"], 1)
                            ctx.inc("a_dmk", 1)
                            if h >= 1:
                                v.wait_ge(sem["a_yo_mm"], hidx)
                            v.tensor_copy(sbf[:, :], S[:, h * HD:(h + 1) * HD]
                                          ).then_inc(sem["a_sbf"], 1)
                            ctx.inc("a_sbf", 1)
                            if h >= 1:
                                v.wait_ge(sem["a_sc_mm"], hidx)
                            v.wait_ge(sem["a_btd"], ch + 1)
                            v.wait_ge(sem["a_dsr"], ch + 1)
                            v.tensor_scalar(bsc[:, :], btm[:, :],
                                            dsdt[:, h:h + 1], None,
                                            ALU.mult).then_inc(sem["a_bt_rdy"], 1)
                            ctx.inc("a_bt_rdy", 1)
                            v.wait_ge(sem["a_ex2"], hidx + 1)
                            if h >= 1:
                                v.wait_ge(sem["a_yd_mm"], hidx)
                            v.scalar_tensor_tensor(
                                mtb[:, :], expdt[:, :], dtc[:, h:h + 1], gtm[:, :],
                                ALU.mult, ALU.mult).then_inc(sem["a_mt"], 1)
                            ctx.inc("a_mt", 1)
                            v.wait_ge(sem["a_sc_mm"], hidx + 1)
                            v.tensor_copy(
                                spr[:, :], bass.AP(pars, 3072, [[4096, HD], [1, P]])
                                ).then_inc(sem["a_sp_ev"], 1)
                            ctx.inc("a_sp_ev", 1)
                            v.wait_ge(sem["a_ep"], hidx + 1)
                            ctx.mark(v.scalar_tensor_tensor(
                                t2b[:, :], x_tm[:, h * HD:(h + 1) * HD],
                                dpb[:, h:h + 1], epart[:, :], ALU.mult, ALU.add))
                            v.wait_ge(sem["vch"], ctx.cnt["vch"])
                            v.wait_ge(sem["a_yd_mm"], hidx + 1)
                            v.tensor_add(ych[:, h * HD:(h + 1) * HD], t2b[:, :],
                                         pars[:, 1024 + hp * 512:1024 + hp * 512 + HD]
                                         ).then_inc(sem["a_ye"], 1)
                            ctx.inc("a_ye", 1)
                            v.wait_ge(sem["a_st_mm"], hidx + 1)
                            v.wait_ge(sem["a_efac"], 3 * ch + 3)
                            ctx.mark(v.scalar_tensor_tensor(
                                S[:, h * HD:(h + 1) * HD], S[:, h * HD:(h + 1) * HD],
                                etbv[:, h:h + 1], parsb[:, 6400:6400 + HD],
                                ALU.mult, ALU.add))
                            v.wait_ge(sem["vch"], ctx.cnt["vch"])
                            v.engine_nop().then_inc(sem["a_su"], 1)
                            ctx.inc("a_su", 1)
                        ctx.mark(v.tensor_mul(ych[:, :], ych[:, :], zsb[:, :]))
                        v.wait_ge(sem["vch"], ctx.cnt["vch"])
                        v.engine_nop().then_inc(sem["a_zg1"], 1)
                        ctx.inc("a_zg1", 1)
                        v.wait_ge(sem["a_sgz"], ch + 1)
                        ctx.mark(v.tensor_mul(ych[:, :], ych[:, :], zsb[:, :]))
                        v.wait_ge(sem["vch"], ctx.cnt["vch"])
                        v.engine_nop().then_inc(sem["a_gat"], 1)
                        ctx.inc("a_gat", 1)
                        v.engine_nop().then_inc(sem["a_zg"], 1)
                        ctx.inc("a_zg", 1)
                        v.wait_ge(sem["a_sq"], ch + 1)
                        ctx.mark(v.tensor_scalar(s2c[:, :], s2c[:, :], 1.0 / DI,
                                                 EPS, ALU.mult, ALU.add))
                        v.wait_ge(sem["vch"], ctx.cnt["vch"])
                        plr = plans_rm[ch]
                        plr.emit_vector(v)
                        v.wait_ge(sem["r_s"], plr.done_val)
                        ctx.mark(v.tensor_scalar(ych[:, :], ych[:, :],
                                                 rstdc[:, 0:1], None, ALU.mult))
                        v.wait_ge(sem["vch"], ctx.cnt["vch"])
                        ctx.mark(v.tensor_mul(ych[:, :], ych[:, :], nwb[:, :]))
                        v.wait_ge(sem["vch"], ctx.cnt["vch"])
                        if ch >= 1:
                            v.wait_ge(sem["a_ynT_ev"], ch)
                        ctx.mark(v.tensor_copy(ynb[:, :], ych[:, :]))
                        v.wait_ge(sem["vch"], ctx.cnt["vch"])
                        v.engine_nop().then_inc(sem["a_yn"], 1)
                        ctx.inc("a_yn", 1)
                        for grp in range(2):
                            v.wait_ge(sem["a_tr7"], ch * 4 + 2 + grp + 1)
                            ctx.mark(v.tensor_copy(
                                ynT[:, grp * 8 * P:(grp + 1) * 8 * P],
                                parsb[:, 7168:7168 + 8 * P]))
                            v.wait_ge(sem["vch"], ctx.cnt["vch"])
                            v.engine_nop().then_inc(sem["a_b7"], 1)
                            ctx.inc("a_b7", 1)
                        v.engine_nop().then_inc(sem["a_ynT_ev"], 1)
                        ctx.inc("a_ynT_ev", 1)
                        v.wait_ge(sem["a_oph"], 2 * (ch + 1))
                        ctx.mark(v.tensor_copy(rsb[:, :], pars[:, 2048:3072]))
                        v.wait_ge(sem["vch"], ctx.cnt["vch"])
                        v.engine_nop().then_inc(sem["a_op_ev"], 1)
                        ctx.inc("a_op_ev", 1)
                        ctx.mark(v.reduce_sum(s1r[:, :], rsb[:, :], axis=AX.X))
                        ctx.mark(v.tensor_mul(x_tm.bitcast(F32)[:, 0:D],
                                              rsb[:, :], rsb[:, :]))
                        v.wait_ge(sem["vch"], ctx.cnt["vch"])
                        ctx.mark(v.reduce_sum(s2r[:, :], x_tm.bitcast(F32)[:, 0:D],
                                              axis=AX.X))
                        v.wait_ge(sem["vch"], ctx.cnt["vch"])
                        ctx.mark(v.tensor_scalar_mul(mur[:, :], s1r[:, :], 1.0 / D))
                        v.tensor_scalar_mul(s2r[:, :], s2r[:, :], 1.0 / D)
                        v.wait_ge(sem["vch"], ctx.cnt["vch"])
                        ctx.mark(v.tensor_mul(tc1[:, :], mur[:, :], mur[:, :]))
                        v.wait_ge(sem["vch"], ctx.cnt["vch"])
                        ctx.mark(v.tensor_sub(s2r[:, :], s2r[:, :], tc1[:, :]))
                        v.wait_ge(sem["vch"], ctx.cnt["vch"])
                        ctx.mark(v.tensor_scalar_add(s2r[:, :], s2r[:, :], EPS))
                        v.wait_ge(sem["vch"], ctx.cnt["vch"])
                        plp = plans_po[ch]
                        plp.emit_vector(v)
                        v.wait_ge(sem["r_s"], plp.done_val)
                        ctx.mark(v.tensor_mul(mur[:, :], mur[:, :], rstdr[:, :]))
                        v.wait_ge(sem["vch"], ctx.cnt["vch"])
                        ctx.mark(v.tensor_scalar_mul(mur[:, :], mur[:, :], -1.0))
                        v.wait_ge(sem["vch"], ctx.cnt["vch"])
                        if ch >= 2:
                            v.wait_ge(sem["a_st" if ch % 2 == 0 else "a_st2"],
                                      16 * ((ch - 2) // 2 + 1))
                        v.wait_ge(sem["a_xr" if ch % 2 == 0 else "a_xr2"],
                                  16 * (ch // 2 + 1))
                        ob = outst[ch % 2]
                        ctx.mark(v.tensor_scalar(ob[:, :], rsb[:, :],
                                                 rstdr[:, 0:1], mur[:, 0:1],
                                                 ALU.mult, ALU.add))
                        v.wait_ge(sem["vch"], ctx.cnt["vch"])
                        v.tensor_add(ob[:, :], ob[:, :], xres[ch % 2][:, :]
                                     ).then_inc(sem["a_res"], 1)
                        ctx.inc("a_res", 1)

        if getattr(ctx, "debug_l0a", False):
            with nc.Block() as blk:
                @blk.gpsimd
                def _(g):
                    for nm, buf in (("da_lnTs", lnTs), ("da_xbc", xbc), ("da_xtm", x_tm),
                                    ("da_zsb", zsb), ("da_ych", ych),
                                    ("da_dtc", dtc), ("da_acsT", acsT),
                                    ("da_S", S), ("da_ynT", ynT),
                                    ("da_rsb", rsb), ("da_dsdt", dsdt),
                                    ("da_einb", einb), ("da_etbv", etbv)):
                        g.dma_start(ctx.dram[nm].ap(), buf[:, :]).then_inc(
                            ctx.get_sem("a_dbg"), 16)
                        ctx.inc("a_dbg", 16)
                    g.wait_ge(sem["a_dbg"], ctx.cnt["a_dbg"])




def emit_l1_attn(ctx, x_in, x_out):
    nc = ctx.nc
    NB = 2
    SEQ = 2048
    PADT = 2112
    NST = 26
    NHD = 16
    HD = 64
    SC = 0.125
    SHIFT = 3.0

    sems = ("t1x", "t_dm", "t2x", "t_ln", "t_tr", "t_lnT_ev", "tw", "t_wq",
            "t_qk_mm", "t_qk_ev", "t_lsc", "t_v_mm", "t_v_ev", "t_d_mm",
            "t_ex", "t_av_mm", "t_rc", "t_ao_ev", "t_at_mm", "t_at_ev",
            "t_op_mm", "t_op_ev", "t_xr", "t_res", "t_st", "t_pz", "t_wo",
            "t_wq2")
    for s in sems:
        ctx.get_sem(s)
    sem = ctx.sem

    wqkvT = ctx.dram["l1a_wqkvT"]
    woutT = ctx.dram["l1a_woutT"]
    boutc = ctx.dram["l1a_bout"]

    xflat = x_in.ap()
    oflat = x_out.ap()
    xr = xflat.rearrange("(n p) d -> n p d", p=P)

    def nrows(w, pr):
        base = 40 * w + pr
        return max(0, min(20, (SEQ - base + 1) // 2))

    with contextlib.ExitStack() as st:
        E2 = st.enter_context
        ident = ctx.sb_ident
        pars = E2(nc.psum_tensor("t_pars", [P, 4096], F32))
        parsb = pars.bitcast(BF16)
        s1 = E2(nc.sbuf_tensor("t_s1", [P, NT], F32))
        s2 = E2(nc.sbuf_tensor("t_s2", [P, NT], F32))
        a1 = E2(nc.sbuf_tensor("t_a1", [P, NT], F32))
        c1 = E2(nc.sbuf_tensor("t_c1", [P, NT], F32))
        nz = [E2(nc.sbuf_tensor(f"t_nz{i}", [P, NT], F32)) for i in range(3)]
        xsp = [E2(nc.sbuf_tensor("t_xsp0", [P, D], F32))] * 2
        lnst = E2(nc.sbuf_tensor("t_lnst", [P, D], BF16))
        ln2T = E2(nc.sbuf_tensor("t_ln2T", [P, KD * PADT], BF16))
        qkT = E2(nc.sbuf_tensor("t_qkT", [P, 16 * PADT], BF16))
        vslot = E2(nc.sbuf_tensor("t_vslot", [P, NST * D], BF16))
        wqb = [E2(nc.sbuf_tensor(f"t_wqb{i}", [P, KD * P], BF16)) for i in range(2)]
        wob = E2(nc.sbuf_tensor("t_wob", [P, KD * D], BF16))
        wvb = wob
        bob = E2(nc.sbuf_tensor("t_bob", [P, D], BF16))
        lnslot = [E2(nc.sbuf_tensor("t_lnslot0", [P, KD * P], BF16))] * 2
        ones20 = E2(nc.sbuf_tensor("t_ones20", [P, 1], BF16))
        shcol = E2(nc.sbuf_tensor("t_shcol", [P, 1], F32))
        attnu = E2(nc.sbuf_tensor("t_attnu", [P, 2 * P], BF16))
        rcol = E2(nc.sbuf_tensor("t_rcol", [P, 1], F32))
        asoT = E2(nc.sbuf_tensor("t_asoT", [P, 8 * P], BF16))
        rsb = E2(nc.sbuf_tensor("t_rsb", [P, D], F32))
        s1b = E2(nc.sbuf_tensor("t_s1b", [P, 1], F32))
        s2b = E2(nc.sbuf_tensor("t_s2b", [P, 1], F32))
        mub = E2(nc.sbuf_tensor("t_mub", [P, 1], F32))
        rstdb = E2(nc.sbuf_tensor("t_rstdb", [P, 1], F32))
        zc1 = E2(nc.sbuf_tensor("t_zc1", [P, 1], F32))
        ec1 = E2(nc.sbuf_tensor("t_ec1", [P, 1], F32))
        tc1 = E2(nc.sbuf_tensor("t_tc1", [P, 1], F32))
        xres = [E2(nc.sbuf_tensor("t_xres0", [P, D], F32))] * 2
        outst = [E2(nc.sbuf_tensor("t_outst0", [P, D], F32))] * 2

        with nc.Block() as blk:
            @blk.gpsimd
            def _(g):
                g.dma_start(bob[:, :], boutc.ap()).then_inc(sem["tw"], 16)
                g.dma_start(wvb[:, :],
                            wqkvT.ap()[:, 2048:3072].rearrange("(k p) h -> p k h", p=P)
                            ).then_inc(sem["tw"], 16)
                ctx.inc("tw", 32)

            @blk.sync
            def _(sy):
                for g in range(NT):
                    if g >= 1:
                        sy.wait_ge(sem["t_dm"], 4 * g)
                    sy.dma_start(xsp[0][:, :], xr[g, :, :]
                                 ).then_inc(sem["t1x"], 16)
                    ctx.inc("t1x", 16)

            @blk.scalar
            def _(sc):
                for g in range(NT):
                    sc.wait_ge(sem["t1x"], 16 * (g + 1))
                    src = xsp[g % 2][:, :]
                    if g > 0:
                        sc.wait_ge(sem["t_dm"], 4 * g)
                    sc.activation(rsb[:, :], src, AF.Copy,
                                  accum_out=s1[:, g:g + 1]).then_inc(sem["t_dm"], 1)
                    ctx.inc("t_dm", 1)
                    sc.wait_ge(sem["t_dm"], ctx.cnt["t_dm"])
                    sc.activation(rsb[:, :], src, AF.Square,
                                  accum_out=s2[:, g:g + 1]).then_inc(sem["t_dm"], 3)
                    ctx.inc("t_dm", 3)

            @blk.vector
            def _(v):
                ctx.mark(v.memset(ones20[:, :], 1.0))
                ctx.mark(v.memset(shcol[:, :], -SHIFT))
                ctx.mark(v.memset(xres[0][:, :], 0.0))
                ctx.mark(v.memset(attnu[:, :], 0.0))
                v.wait_ge(sem["vch"], ctx.cnt["vch"])

        planT = RstdPlan(ctx, s2[:, :], a1[:, :], nz[0][:, :], nz[1][:, :],
                         nz[2][:, :])
        with nc.Block() as blk:
            @blk.vector
            def _(v):
                ctx.mark(v.tensor_scalar_mul(c1[:, :], s1[:, :], 1.0 / D))
                ctx.mark(v.tensor_scalar_mul(s2[:, :], s2[:, :], 1.0 / D))
                v.wait_ge(sem["vch"], ctx.cnt["vch"])
                ctx.mark(v.tensor_mul(nz[0][:, :], c1[:, :], c1[:, :]))
                v.wait_ge(sem["vch"], ctx.cnt["vch"])
                ctx.mark(v.tensor_sub(s2[:, :], s2[:, :], nz[0][:, :]))
                v.wait_ge(sem["vch"], ctx.cnt["vch"])
                ctx.mark(v.tensor_scalar_add(s2[:, :], s2[:, :], EPS))
                v.wait_ge(sem["vch"], ctx.cnt["vch"])
                planT.emit_vector(v)
                v.wait_ge(sem["r_s"], planT.done_val)
                ctx.mark(v.tensor_mul(c1[:, :], c1[:, :], a1[:, :]))
                v.wait_ge(sem["vch"], ctx.cnt["vch"])
                ctx.mark(v.tensor_scalar_mul(c1[:, :], c1[:, :], -1.0))
                v.wait_ge(sem["vch"], ctx.cnt["vch"])

            @blk.scalar
            def _(sc):
                planT.emit_scalar(sc)

        plans_t = [RstdPlan(ctx, s2b[:, :], rstdb[:, :], zc1[:, :], ec1[:, :],
                            tc1[:, :]) for _ in range(NB * NST)]

        def dbg_dump(names_bufs):
            if not getattr(ctx, "debug_l1a", False):
                return
            with nc.Block() as blk2:
                @blk2.gpsimd
                def _(g):
                    for nm, buf in names_bufs:
                        g.dma_start(ctx.dram[nm].ap(), buf[:, :]).then_inc(
                            ctx.get_sem("t_dbg"), 16)
                        ctx.inc("t_dbg", 16)
                    g.wait_ge(sem["t_dbg"], ctx.cnt["t_dbg"])

        for b in range(NB):
            with nc.Block() as blk:
                @blk.sync
                def _(sy):
                    for t in range(16):
                        g = b * 16 + t
                        if t >= 1:
                            sy.wait_ge(sem["t_ln"], b * 16 + t)
                        sy.dma_start(xsp[0][:, :], xr[g, :, :]
                                     ).then_inc(sem["t2x"], 16)
                        ctx.inc("t2x", 16)
                    for st_ in range(NST):
                        sti = b * NST + st_
                        if st_ >= 1:
                            sy.wait_ge(sem["t_res"], b * NST + st_)
                        for i in range(4):
                            w, pr = (4 * st_ + i) // 2, (4 * st_ + i) % 2
                            nr = nrows(w, pr)
                            off = (b * SEQ + 40 * w + pr)
                            sy.dma_start(
                                xres[st_ % 2][32 * i:32 * i + nr, :],
                                bass.AP(xflat.tensor, off * D, [[2 * D, nr], [1, D]])
                            ).then_inc(sem["t_xr"], 16)
                            ctx.inc("t_xr", 16)
                        sy.wait_ge(sem["t_res"], b * NST + st_ + 1)
                        for i in range(4):
                            w, pr = (4 * st_ + i) // 2, (4 * st_ + i) % 2
                            nr = nrows(w, pr)
                            off = (b * SEQ + 40 * w + pr)
                            sy.dma_start(
                                bass.AP(oflat.tensor, off * D, [[2 * D, nr], [1, D]]),
                                outst[st_ % 2][32 * i:32 * i + nr, :]
                            ).then_inc(sem["t_st"], 16)
                            ctx.inc("t_st", 16)
                    if b == NB - 1:
                        sy.wait_ge(sem["t_st"], ctx.cnt["t_st"])

                @blk.gpsimd
                def _(g):
                    for dt_ in range(16):
                        idx = b * 16 + dt_
                        if idx >= 2:
                            g.wait_ge(sem["t_qk_mm"], idx - 1)
                        g.dma_start(
                            wqb[idx % 2][:, :],
                            wqkvT.ap()[:, dt_ * P:(dt_ + 1) * P]
                                 .rearrange("(k p) h -> p k h", p=P)
                        ).then_inc(sem["t_wq" if idx % 2 == 0 else "t_wq2"], 16)
                        ctx.inc("t_wq" if idx % 2 == 0 else "t_wq2", 16)
                    g.wait_ge(sem["t_v_mm"], (b + 1) * NST)
                    g.dma_start(wob[:, :],
                                woutT.ap().rearrange("(k p) d -> p k d", p=P)
                                ).then_inc(sem["t_wo"], 16)
                    ctx.inc("t_wo", 16)
                    if b < NB - 1:
                        g.wait_ge(sem["t_op_mm"], (b + 1) * NST)
                        g.dma_start(wvb[:, :],
                                    wqkvT.ap()[:, 2048:3072]
                                    .rearrange("(k p) h -> p k h", p=P)
                                    ).then_inc(sem["t_wo"], 16)
                        ctx.inc("t_wo", 16)
                    if getattr(ctx, "debug_l1a", False) and b == 0:
                        g.wait_ge(sem["t_pz"], 1)
                        g.wait_ge(sem["t_lnT_ev"], 16)
                        g.dma_start(ctx.dram["dt_ln2Te"].ap(), ln2T[:, :]
                                    ).then_inc(ctx.get_sem("t_dbg"), 16)
                        ctx.inc("t_dbg", 16)
                        g.wait_ge(sem["t_qk_mm"], 16)
                        g.dma_start(ctx.dram["dt_ln2Tq"].ap(), ln2T[:, :]
                                    ).then_inc(sem["t_dbg"], 16)
                        ctx.inc("t_dbg", 16)
                        g.wait_ge(sem["t_v_ev"], 26)
                        g.dma_start(ctx.dram["dt_ln2Tv"].ap(), ln2T[:, :]
                                    ).then_inc(sem["t_dbg"], 16)
                        ctx.inc("t_dbg", 16)
                        g.wait_ge(sem["t_av_mm"], 26 * 16)
                        g.dma_start(ctx.dram["dt_ln2Ta"].ap(), ln2T[:, :]
                                    ).then_inc(sem["t_dbg"], 16)
                        ctx.inc("t_dbg", 16)

                @blk.tensor
                def _(te):
                    te.wait_ge(sem["tw"], 32)
                    for t in range(16):
                        g = b * 16 + t
                        te.wait_ge(sem["t_ln"], g + 1)
                        if g >= 2:
                            te.wait_ge(sem["t_lnT_ev"], g - 1)
                        if b > 0 and t == 0:
                            te.wait_ge(sem["t_op_ev"], b * NST)
                        pcb = (t % 2) * 1024
                        for k in range(KD):
                            ins = te.transpose(parsb[:, pcb + k * P:pcb + (k + 1) * P],
                                               lnst[:, k * P:(k + 1) * P],
                                               ident[:, :])
                        ins.then_inc(sem["t_tr"], 1)
                        ctx.inc("t_tr", 1)
                    for dt_ in range(16):
                        idx = b * 16 + dt_
                        te.wait_ge(sem["t_wq" if idx % 2 == 0 else "t_wq2"],
                                   16 * (idx // 2 + 1))
                        if dt_ == 0:
                            te.wait_ge(sem["t_lnT_ev"], (b + 1) * 16)
                            te.wait_ge(sem["t_pz"], b + 1)
                        if idx >= 1:
                            te.wait_ge(sem["t_qk_ev"], 5 * idx)
                        for k in range(KD):
                            wk = wqb[idx % 2][:, k * P:(k + 1) * P]
                            for tb in range(5):
                                nn = 512 if tb < 4 else PADT - 2048
                                ins = te.matmul(
                                    pars[:, tb * 512:tb * 512 + nn], wk,
                                    ln2T[:, k * PADT + tb * 512:k * PADT + tb * 512 + nn],
                                    start=(k == 0), stop=(k == KD - 1))
                        ins.then_inc(sem["t_qk_mm"], 1)
                        ctx.inc("t_qk_mm", 1)
                    for st_ in range(NST):
                        sti = b * NST + st_
                        te.wait_ge(sem["t_lsc"], sti + 1)
                        if b > 0:
                            te.wait_ge(sem["t_wo"], 32 * b)
                        if sti >= 2:
                            te.wait_ge(sem["t_v_ev"], sti - 1)
                        if st_ % 2 == 1 and st_ == 1:
                            te.wait_ge(sem["t_qk_ev"], (b * 16 + 16) * 5)
                        pc = 2560 if st_ % 2 == 0 else 0
                        for k in range(KD):
                            for nb2 in range(2):
                                ins = te.matmul(
                                    pars[:, pc + nb2 * 512:pc + (nb2 + 1) * 512],
                                    lnslot[sti % 2][:, k * P:(k + 1) * P],
                                    wvb[:, k * D + nb2 * 512:k * D + (nb2 + 1) * 512],
                                    start=(k == 0), stop=(k == KD - 1))
                        ins.then_inc(sem["t_v_mm"], 1)
                        ctx.inc("t_v_mm", 1)
                    for st_ in range(NST):
                        for h in range(NHD):
                            ai = (b * NST + st_) * NHD + h
                            par = ai % 2
                            if st_ == 0 and h == 0:
                                te.wait_ge(sem["t_v_ev"], (b + 1) * NST)
                            if ai >= 2:
                                te.wait_ge(sem["t_ex"], ai - 1)
                            qd = h // 2
                            qp = (h % 2) * 64
                            kd_ = 8 + h // 2
                            for i in range(4):
                                w, pr = (4 * st_ + i) // 2, (4 * st_ + i) % 2
                                col = 40 * w + pr
                                ins = te.matmul(
                                    pars[32 * i:32 * i + 20,
                                         par * 512 + 32 * i:par * 512 + 32 * i + 20],
                                    qkT[qp:qp + 64,
                                        kd_ * PADT + col:kd_ * PADT + col + 40:2],
                                    qkT[qp:qp + 64,
                                        qd * PADT + col:qd * PADT + col + 40:2],
                                    start=True, stop=True,
                                    tile_position=(qp, 32 * i))
                                if i == 3:
                                    ins.then_inc(sem["t_d_mm"], 1)
                            ctx.inc("t_d_mm", 1)
                            te.wait_ge(sem["t_ex"], ai + 1)
                            if st_ == 0 and h == 0:
                                te.wait_ge(sem["t_qk_ev"], 80 * (b + 1))
                            if ai >= 2:
                                te.wait_ge(sem["t_ao_ev"], ai - 1)
                            avc = 1024 + par * 512
                            te.matmul(pars[:, avc:avc + HD],
                                      attnu[:, par * P:(par + 1) * P],
                                      vslot[:, st_ * D + h * HD:st_ * D + (h + 1) * HD],
                                      start=True, stop=True)
                            ins = te.matmul(pars[:, avc + HD:avc + HD + 1],
                                            attnu[:, par * P:(par + 1) * P],
                                            ones20[:, :],
                                            start=True, stop=True)
                            ins.then_inc(sem["t_av_mm"], 1)
                            ctx.inc("t_av_mm", 1)
                    for st_ in range(NST):
                        sti = b * NST + st_
                        te.wait_ge(sem["t_ao_ev"], (b * NST + st_ + 1) * NHD
                                   if False else (b * NST + st_) * NHD + NHD)
                        if sti >= 2:
                            te.wait_ge(sem["t_at_ev"], sti - 1)
                        pcb = 4096 + (st_ % 2) * 1024
                        for k in range(KD):
                            ins = te.transpose(
                                parsb[:, pcb + k * P:pcb + (k + 1) * P],
                                vslot[:, st_ * D + k * P:st_ * D + (k + 1) * P],
                                ident[:, :])
                        ins.then_inc(sem["t_at_mm"], 1)
                        ctx.inc("t_at_mm", 1)
                        te.wait_ge(sem["t_at_ev"], sti + 1)
                        te.wait_ge(sem["t_wo"], 32 * b + 16)
                        if sti >= 1:
                            te.wait_ge(sem["t_op_ev"], sti)
                        for k in range(KD):
                            for nb2 in range(2):
                                ins = te.matmul(
                                    pars[:, 3072 + nb2 * 512:3072 + (nb2 + 1) * 512],
                                    asoT[:, k * P:(k + 1) * P],
                                    wob[:, k * D + nb2 * 512:k * D + (nb2 + 1) * 512],
                                    start=(k == 0), stop=(k == KD - 1))
                        ins.then_inc(sem["t_op_mm"], 1)
                        ctx.inc("t_op_mm", 1)

                @blk.scalar
                def _(sc):
                    for st_ in range(NST):
                        for h in range(NHD):
                            ai = (b * NST + st_) * NHD + h
                            par = ai % 2
                            sc.wait_ge(sem["t_d_mm"], ai + 1)
                            if ai >= 2:
                                sc.wait_ge(sem["t_av_mm"], ai - 1)
                            for i in range(4):
                                ins = sc.activation(
                                    attnu[32 * i:32 * i + 20,
                                          par * P + 32 * i:par * P + 32 * i + 20],
                                    pars[32 * i:32 * i + 20,
                                         par * 512 + 32 * i:par * 512 + 32 * i + 20],
                                    AF.Exp, scale=SC, bias=shcol[32 * i:32 * i + 20, 0:1])
                            ins.then_inc(sem["t_ex"], 1)
                            ctx.inc("t_ex", 1)
                    for st_ in range(NST):
                        plans_t[b * NST + st_].emit_scalar(sc)

                @blk.vector
                def _(v):
                    for t in range(16):
                        g = b * 16 + t
                        v.wait_ge(sem["t2x"], 16 * (g + 1))
                        v.wait_ge(sem["t_tr"], g)
                        v.tensor_scalar(lnst[:, :], xsp[t % 2][:, :],
                                        a1[:, g:g + 1], c1[:, g:g + 1],
                                        ALU.mult, ALU.add).then_inc(sem["t_ln"], 1)
                        ctx.inc("t_ln", 1)
                        v.wait_ge(sem["t_tr"], g + 1)
                        pcb = (t % 2) * 1024
                        for kk in range(KD):
                            ins = v.tensor_copy(
                                ln2T[:, kk * PADT + t * P:kk * PADT + (t + 1) * P],
                                parsb[:, pcb + kk * P:pcb + (kk + 1) * P])
                        ins.then_inc(sem["t_lnT_ev"], 1)
                        ctx.inc("t_lnT_ev", 1)
                    for kk in range(KD):
                        ctx.mark(v.memset(
                            ln2T[:, kk * PADT + SEQ:(kk + 1) * PADT], 0.0))
                    v.wait_ge(sem["vch"], ctx.cnt["vch"])
                    v.engine_nop().then_inc(sem["t_pz"], 1)
                    ctx.inc("t_pz", 1)
                    for dt_ in range(16):
                        idx = b * 16 + dt_
                        v.wait_ge(sem["t_qk_mm"], idx + 1)
                        for tb in range(5):
                            nn = 512 if tb < 4 else PADT - 2048
                            ctx.mark(v.tensor_copy(
                                qkT[:, dt_ * PADT + tb * 512:dt_ * PADT + tb * 512 + nn],
                                pars[:, tb * 512:tb * 512 + nn]))
                            v.wait_ge(sem["vch"], ctx.cnt["vch"])
                            v.engine_nop().then_inc(sem["t_qk_ev"], 1)
                            ctx.inc("t_qk_ev", 1)
                    for st_ in range(NST):
                        sti = b * NST + st_
                        if sti >= 2:
                            v.wait_ge(sem["t_v_mm"], sti - 1)
                        w0 = 2 * st_
                        for k in range(KD):
                            ctx.mark(v.tensor_copy(
                                lnslot[sti % 2][:, k * P:(k + 1) * P],
                                bass.AP(ln2T, k * PADT + 40 * w0,
                                        [[KD * PADT, P], [40, 2], [1, 2], [2, 32]])))
                        v.wait_ge(sem["vch"], ctx.cnt["vch"])
                        v.engine_nop().then_inc(sem["t_lsc"], 1)
                        ctx.inc("t_lsc", 1)
                        v.wait_ge(sem["t_v_mm"], sti + 1)
                        pc = 2560 if st_ % 2 == 0 else 0
                        ctx.mark(v.tensor_copy(vslot[:, st_ * D:(st_ + 1) * D],
                                               pars[:, pc:pc + 1024]))
                        v.wait_ge(sem["vch"], ctx.cnt["vch"])
                        v.engine_nop().then_inc(sem["t_v_ev"], 1)
                        ctx.inc("t_v_ev", 1)
                    for st_ in range(NST):
                        for h in range(NHD):
                            ai = (b * NST + st_) * NHD + h
                            par = ai % 2
                            avc = 1024 + par * 512
                            v.wait_ge(sem["t_av_mm"], ai + 1)
                            v.wait_ge(sem["t_ao_ev"], ctx.cnt["t_ao_ev"])
                            ctx.mark(v.tensor_scalar_add(
                                rcol[:, :], pars[:, avc + HD:avc + HD + 1], 1e-30))
                            v.wait_ge(sem["vch"], ctx.cnt["vch"])
                            ctx.mark(v.reciprocal(rcol[:, :], rcol[:, :]))
                            v.wait_ge(sem["vch"], ctx.cnt["vch"])
                            v.tensor_scalar(vslot[:, st_ * D + h * HD:
                                                  st_ * D + (h + 1) * HD],
                                            pars[:, avc:avc + HD],
                                            rcol[:, 0:1], None, ALU.mult
                                            ).then_inc(sem["t_ao_ev"], 1)
                            ctx.inc("t_ao_ev", 1)
                    for st_ in range(NST):
                        sti = b * NST + st_
                        v.wait_ge(sem["t_at_mm"], sti + 1)
                        pcb = 4096 + (st_ % 2) * 1024
                        ctx.mark(v.tensor_copy(asoT[:, :], parsb[:, pcb:pcb + 8 * P]))
                        v.wait_ge(sem["vch"], ctx.cnt["vch"])
                        v.engine_nop().then_inc(sem["t_at_ev"], 1)
                        ctx.inc("t_at_ev", 1)
                        v.wait_ge(sem["t_op_mm"], sti + 1)
                        ctx.mark(v.tensor_add(rsb[:, :], pars[:, 3072:4096],
                                              bob[:, :]))
                        v.wait_ge(sem["vch"], ctx.cnt["vch"])
                        v.engine_nop().then_inc(sem["t_op_ev"], 1)
                        ctx.inc("t_op_ev", 1)
                        ctx.mark(v.reduce_sum(s1b[:, :], rsb[:, :], axis=AX.X))
                        ctx.mark(v.tensor_mul(xsp[0][:, :], rsb[:, :], rsb[:, :]))
                        v.wait_ge(sem["vch"], ctx.cnt["vch"])
                        ctx.mark(v.reduce_sum(s2b[:, :], xsp[0][:, :], axis=AX.X))
                        v.wait_ge(sem["vch"], ctx.cnt["vch"])
                        ctx.mark(v.tensor_scalar_mul(mub[:, :], s1b[:, :], 1.0 / D))
                        v.tensor_scalar_mul(s2b[:, :], s2b[:, :], 1.0 / D)
                        v.wait_ge(sem["vch"], ctx.cnt["vch"])
                        ctx.mark(v.tensor_mul(tc1[:, :], mub[:, :], mub[:, :]))
                        v.wait_ge(sem["vch"], ctx.cnt["vch"])
                        ctx.mark(v.tensor_sub(s2b[:, :], s2b[:, :], tc1[:, :]))
                        v.wait_ge(sem["vch"], ctx.cnt["vch"])
                        ctx.mark(v.tensor_scalar_add(s2b[:, :], s2b[:, :], EPS))
                        v.wait_ge(sem["vch"], ctx.cnt["vch"])
                        pl = plans_t[sti]
                        pl.emit_vector(v)
                        v.wait_ge(sem["r_s"], pl.done_val)
                        ctx.mark(v.tensor_mul(mub[:, :], mub[:, :], rstdb[:, :]))
                        v.wait_ge(sem["vch"], ctx.cnt["vch"])
                        ctx.mark(v.tensor_scalar_mul(mub[:, :], mub[:, :], -1.0))
                        v.wait_ge(sem["vch"], ctx.cnt["vch"])
                        if sti >= 1:
                            v.wait_ge(sem["t_st"], 64 * sti)
                        v.wait_ge(sem["t_xr"], 64 * (sti + 1))
                        ob = outst[st_ % 2]
                        ctx.mark(v.tensor_scalar(ob[:, :], rsb[:, :],
                                                 rstdb[:, 0:1], mub[:, 0:1],
                                                 ALU.mult, ALU.add))
                        v.wait_ge(sem["vch"], ctx.cnt["vch"])
                        v.tensor_add(ob[:, :], ob[:, :], xres[st_ % 2][:, :]
                                     ).then_inc(sem["t_res"], 1)
                        ctx.inc("t_res", 1)
            if b == 0:
                dbg_dump((("dt_ln2T", ln2T), ("dt_qkT", qkT), ("dt_vslot", vslot)))



def emit_l1_ffn(ctx, x_in, x_out):
    nc = ctx.nc
    I = 2048
    NHT = I // P
    NE = 8
    NX = 9
    sems = ("m1x", "m_dm", "m2x", "m_hs", "m_dm2", "m3x", "m_ln", "m_tr",
            "m_lnT_ev", "m_rtw", "m_rt_mm", "m_rt_ev", "m_wq", "m_wt_ev",
            "m_wb_mm", "m_wb_ev", "m_xe", "mw", "m_gu_mm", "m_th", "m_gu_ev",
            "m_hm_st", "m_hm_st2", "m_hm_ld", "m_hm_ld2", "m_dw", "m_dw2",
            "mw2", "m_dn_mm", "m_dn_ev", "m_xr2",
            "m_xr3", "m_xr4", "m_xr5", "m_res", "m_st", "m_st2", "m_sc2",
            "m_q", "m_u")
    for s in sems:
        ctx.get_sem(s)
    sem = ctx.sem

    gw = ctx.dram["l1f_gate"]
    uw = ctx.dram["l1f_up"]
    dw = ctx.dram["l1f_down"]
    shg = ctx.dram["l1f_shg"]
    shu = ctx.dram["l1f_shu"]
    shd = ctx.dram["l1f_shd"]
    rw = ctx.dram["l1f_router"]
    lnw = ctx.dram["l1f_lnw"]
    lnb = ctx.dram["l1f_lnb"]
    e8 = ctx.dram["l1f_e8"]
    hm_s = ctx.dram["l1f_hms"]

    xr = x_in.ap().rearrange("(n p) d -> n p d", p=P)
    xo = x_out.ap().rearrange("(n p) d -> n p d", p=P)

    with contextlib.ExitStack() as st:
        E2 = st.enter_context
        ident = ctx.sb_ident
        stP = contextlib.ExitStack()
        EP = stP.enter_context
        s1 = E2(nc.sbuf_tensor("m_s1", [P, NT], F32))
        s2 = E2(nc.sbuf_tensor("m_s2", [P, NT], F32))
        a1 = E2(nc.sbuf_tensor("m_a1", [P, NT], F32))
        c1 = E2(nc.sbuf_tensor("m_c1", [P, NT], F32))
        a2 = E2(nc.sbuf_tensor("m_a2", [P, NT], F32))
        c2 = E2(nc.sbuf_tensor("m_c2", [P, NT], F32))
        nz = [E2(nc.sbuf_tensor(f"m_nz{i}", [P, NT], F32)) for i in range(3)]
        dump = E2(nc.sbuf_tensor("m_dump", [P, D], F32))
        pars = E2(nc.psum_tensor("m_pars", [P, 4096], F32))
        parsb = pars.bitcast(BF16)
        xsp = EP(nc.sbuf_tensor("m_xsp", [P, 8 * D], F32))
        hst = EP(nc.sbuf_tensor("m_hst", [P, D], F32))
        lnst = EP(nc.sbuf_tensor("m_lnst", [P, D], BF16))
        lnT = EP(nc.sbuf_tensor("m_lnT", [P, KD * TOK], BF16))
        rtw = EP(nc.sbuf_tensor("m_rtw", [P, KD * 8], BF16))
        lg = EP(nc.sbuf_tensor("m_lg", [P, NT * 8], F32))
        mx = EP(nc.sbuf_tensor("m_mx", [P, NT], F32))
        scv = EP(nc.sbuf_tensor("m_scv", [P, NT], F32))
        taken = EP(nc.sbuf_tensor("m_taken", [P, NT], F32))
        eqv = EP(nc.sbuf_tensor("m_eqv", [P, NT], F32))
        wts = EP(nc.sbuf_tensor("m_wts", [P, NT * 8], F32))
        wtT = EP(nc.sbuf_tensor("m_wtT", [8, TOK], F32))
        e8sb = EP(nc.sbuf_tensor("m_e8sb", [8, 8 * P], F32))
        lnwb = EP(nc.sbuf_tensor("m_lnwb", [P, D], F32))
        lnbb = EP(nc.sbuf_tensor("m_lnbb", [P, D], F32))

        with nc.Block() as blk:
            @blk.sync
            def _(sy):
                for sp in range(4):
                    if sp >= 1:
                        sy.wait_ge(sem["m_dm"], 32 * sp)
                    sy.dma_start(xsp[:, :], xr[sp * 8:(sp + 1) * 8, :, :]
                                 .transpose([1, 0, 2])).then_inc(sem["m1x"], 16)
                    ctx.inc("m1x", 16)

            @blk.gpsimd
            def _(g):
                g.dma_start(rtw[:, :], rw.ap().rearrange("(k p) e -> p k e", p=P)
                            ).then_inc(sem["m_rtw"], 16)
                g.dma_start(e8sb[:, :], e8.ap()).then_inc(sem["m_rtw"], 16)
                g.dma_start(lnwb[:, :], lnw.ap()).then_inc(sem["m_rtw"], 16)
                g.dma_start(lnbb[:, :], lnb.ap()).then_inc(sem["m_rtw"], 16)
                ctx.inc("m_rtw", 64)

            @blk.scalar
            def _(sc):
                for sp in range(4):
                    sc.wait_ge(sem["m1x"], 16 * (sp + 1))
                    for t in range(8):
                        g = sp * 8 + t
                        src = xsp[:, t * D:(t + 1) * D]
                        if g > 0:
                            sc.wait_ge(sem["m_dm"], 4 * g)
                        sc.activation(dump[:, :], src, AF.Copy,
                                      accum_out=s1[:, g:g + 1]).then_inc(sem["m_dm"], 1)
                        ctx.inc("m_dm", 1)
                        sc.wait_ge(sem["m_dm"], ctx.cnt["m_dm"])
                        sc.activation(dump[:, :], src, AF.Square,
                                      accum_out=s2[:, g:g + 1]).then_inc(sem["m_dm"], 3)
                        ctx.inc("m_dm", 3)

        plan1 = RstdPlan(ctx, s2[:, :], a1[:, :], nz[0][:, :], nz[1][:, :],
                         nz[2][:, :])
        with nc.Block() as blk:
            @blk.vector
            def _(v):
                ctx.mark(v.tensor_scalar_mul(c1[:, :], s1[:, :], 1.0 / D))
                ctx.mark(v.tensor_scalar_mul(s2[:, :], s2[:, :], 1.0 / D))
                v.wait_ge(sem["vch"], ctx.cnt["vch"])
                ctx.mark(v.tensor_mul(nz[0][:, :], c1[:, :], c1[:, :]))
                v.wait_ge(sem["vch"], ctx.cnt["vch"])
                ctx.mark(v.tensor_sub(s2[:, :], s2[:, :], nz[0][:, :]))
                v.wait_ge(sem["vch"], ctx.cnt["vch"])
                ctx.mark(v.tensor_scalar_add(s2[:, :], s2[:, :], EPS))
                v.wait_ge(sem["vch"], ctx.cnt["vch"])
                plan1.emit_vector(v)
                v.wait_ge(sem["r_s"], plan1.done_val)
                ctx.mark(v.tensor_mul(c1[:, :], c1[:, :], a1[:, :]))
                v.wait_ge(sem["vch"], ctx.cnt["vch"])
                ctx.mark(v.tensor_scalar_mul(c1[:, :], c1[:, :], -1.0))
                v.wait_ge(sem["vch"], ctx.cnt["vch"])

            @blk.scalar
            def _(sc):
                plan1.emit_scalar(sc)

        with nc.Block() as blk:
            @blk.sync
            def _(sy):
                for sp in range(4):
                    if sp >= 1:
                        sy.wait_ge(sem["m_hs"], 8 * sp)
                    sy.dma_start(xsp[:, :], xr[sp * 8:(sp + 1) * 8, :, :]
                                 .transpose([1, 0, 2])).then_inc(sem["m2x"], 16)
                    ctx.inc("m2x", 16)

            @blk.vector
            def _(v):
                for sp in range(4):
                    v.wait_ge(sem["m2x"], 16 * (sp + 1))
                    for t in range(8):
                        g = sp * 8 + t
                        if g > 0:
                            v.wait_ge(sem["m_dm2"], 4 * g)
                        v.tensor_scalar(hst[:, :], xsp[:, t * D:(t + 1) * D],
                                        a1[:, g:g + 1], c1[:, g:g + 1],
                                        ALU.mult, ALU.add).then_inc(sem["m_hs"], 1)
                        ctx.inc("m_hs", 1)

            @blk.scalar
            def _(sc):
                for g in range(NT):
                    sc.wait_ge(sem["m_hs"], g + 1)
                    if g > 0:
                        sc.wait_ge(sem["m_dm2"], 4 * g)
                    sc.activation(dump[:, :], hst[:, :], AF.Copy,
                                  accum_out=s1[:, g:g + 1]).then_inc(sem["m_dm2"], 1)
                    ctx.inc("m_dm2", 1)
                    sc.wait_ge(sem["m_dm2"], ctx.cnt["m_dm2"])
                    sc.activation(dump[:, :], hst[:, :], AF.Square,
                                  accum_out=s2[:, g:g + 1]).then_inc(sem["m_dm2"], 3)
                    ctx.inc("m_dm2", 3)

        plan2 = RstdPlan(ctx, s2[:, :], a2[:, :], nz[0][:, :], nz[1][:, :],
                         nz[2][:, :])
        with nc.Block() as blk:
            @blk.vector
            def _(v):
                ctx.mark(v.tensor_scalar_mul(c2[:, :], s1[:, :], 1.0 / D))
                ctx.mark(v.tensor_scalar_mul(s2[:, :], s2[:, :], 1.0 / D))
                v.wait_ge(sem["vch"], ctx.cnt["vch"])
                ctx.mark(v.tensor_mul(nz[0][:, :], c2[:, :], c2[:, :]))
                v.wait_ge(sem["vch"], ctx.cnt["vch"])
                ctx.mark(v.tensor_sub(s2[:, :], s2[:, :], nz[0][:, :]))
                v.wait_ge(sem["vch"], ctx.cnt["vch"])
                ctx.mark(v.tensor_scalar_add(s2[:, :], s2[:, :], EPS))
                v.wait_ge(sem["vch"], ctx.cnt["vch"])
                plan2.emit_vector(v)
                v.wait_ge(sem["r_s"], plan2.done_val)
                ctx.mark(v.tensor_mul(c2[:, :], c2[:, :], a2[:, :]))
                v.wait_ge(sem["vch"], ctx.cnt["vch"])
                ctx.mark(v.tensor_scalar_mul(c2[:, :], c2[:, :], -1.0))
                v.wait_ge(sem["vch"], ctx.cnt["vch"])

            @blk.scalar
            def _(sc):
                plan2.emit_scalar(sc)

        with nc.Block() as blk:
            @blk.sync
            def _(sy):
                for sp in range(4):
                    if sp >= 1:
                        sy.wait_ge(sem["m_ln"], 8 * sp)
                    sy.dma_start(xsp[:, :], xr[sp * 8:(sp + 1) * 8, :, :]
                                 .transpose([1, 0, 2])).then_inc(sem["m3x"], 16)
                    ctx.inc("m3x", 16)

            @blk.tensor
            def _(te):
                te.wait_ge(sem["m_rtw"], 64)
                for g in range(NT):
                    te.wait_ge(sem["m_ln"], g + 1)
                    if g >= 2:
                        te.wait_ge(sem["m_lnT_ev"], g - 1)
                    pcb = (g % 2) * 1024
                    for k in range(KD):
                        ins = te.transpose(parsb[:, pcb + k * P:pcb + (k + 1) * P],
                                           lnst[:, k * P:(k + 1) * P], ident[:, :])
                    ins.then_inc(sem["m_tr"], 1)
                    ctx.inc("m_tr", 1)
                for g in range(NT):
                    te.wait_ge(sem["m_lnT_ev"], g + 1)
                    if g >= 2:
                        te.wait_ge(sem["m_rt_ev"], g - 1)
                    pc = 2048 + (g % 2) * 512
                    for k in range(KD):
                        ins = te.matmul(pars[:, pc:pc + 8],
                                        lnT[:, k * TOK + g * P:k * TOK + (g + 1) * P],
                                        rtw[:, k * 8:(k + 1) * 8],
                                        start=(k == 0), stop=(k == KD - 1))
                    ins.then_inc(sem["m_rt_mm"], 1)
                    ctx.inc("m_rt_mm", 1)

            @blk.vector
            def _(v):
                v.wait_ge(sem["m_rtw"], 64)
                for sp in range(4):
                    v.wait_ge(sem["m3x"], 16 * (sp + 1))
                    for t in range(8):
                        g = sp * 8 + t
                        v.wait_ge(sem["m_tr"], g)
                        ctx.mark(v.tensor_scalar(hst[:, :], xsp[:, t * D:(t + 1) * D],
                                                 a1[:, g:g + 1], c1[:, g:g + 1],
                                                 ALU.mult, ALU.add))
                        v.wait_ge(sem["vch"], ctx.cnt["vch"])
                        ctx.mark(v.tensor_scalar(hst[:, :], hst[:, :],
                                                 a2[:, g:g + 1], c2[:, g:g + 1],
                                                 ALU.mult, ALU.add))
                        v.wait_ge(sem["vch"], ctx.cnt["vch"])
                        ctx.mark(v.tensor_mul(hst[:, :], hst[:, :], lnwb[:, :]))
                        v.wait_ge(sem["vch"], ctx.cnt["vch"])
                        v.tensor_add(lnst[:, :], hst[:, :], lnbb[:, :]
                                     ).then_inc(sem["m_ln"], 1)
                        ctx.inc("m_ln", 1)
                        v.wait_ge(sem["m_tr"], g + 1)
                        pcb = (g % 2) * 1024
                        v.tensor_copy(
                            bass.AP(lnT, g * P, [[KD * TOK, P], [TOK, KD], [1, P]]),
                            parsb[:, pcb:pcb + KD * P],
                        ).then_inc(sem["m_lnT_ev"], 1)
                        ctx.inc("m_lnT_ev", 1)
                for g in range(NT):
                    v.wait_ge(sem["m_rt_mm"], g + 1)
                    pc = 2048 + (g % 2) * 512
                    v.tensor_copy(lg[:, g * 8:(g + 1) * 8], pars[:, pc:pc + 8]
                                  ).then_inc(sem["m_rt_ev"], 1)
                    ctx.inc("m_rt_ev", 1)

        if getattr(ctx, "debug", False):
            with nc.Block() as blk:
                @blk.gpsimd
                def _(g):
                    g.dma_start(ctx.dram["d_lnT2"].ap(), lnT[:, :]).then_inc(ctx.get_sem("m_dbg2"), 16)
                    ctx.inc("m_dbg2", 16)
                    g.wait_ge(sem["m_dbg2"], ctx.cnt["m_dbg2"])

        navail = nz[0]
        with nc.Block() as blk:
            @blk.vector
            def _(v):
                v.wait_ge(sem["m_rt_ev"], NT)
                ctx.mark(v.reduce_max(mx[:, :],
                                      bass.AP(lg, 0, [[NT * 8, P], [8, NT], [1, 8]]),
                                      axis=AX.X))
                v.wait_ge(sem["vch"], ctx.cnt["vch"])
                v.engine_nop().then_inc(sem["m_wq"], 1)
                ctx.inc("m_wq", 1)
                ctx.mark(v.memset(taken[:, :], 0.0))
                v.wait_ge(sem["m_sc2"], 1)
                ctx.mark(v.tensor_scalar_add(scv[:, :], scv[:, :], 1.0))
                v.wait_ge(sem["vch"], ctx.cnt["vch"])
                ctx.mark(v.reciprocal(scv[:, :], scv[:, :]))
                v.wait_ge(sem["vch"], ctx.cnt["vch"])
                for e in range(NE):
                    le = bass.AP(lg, e, [[NT * 8, P], [8, NT]])
                    ctx.mark(v.tensor_tensor(eqv[:, :], le, mx[:, :], ALU.is_ge))
                    ctx.mark(v.tensor_scalar(navail[:, :], taken[:, :], -1.0, 1.0,
                                             ALU.mult, ALU.add))
                    v.wait_ge(sem["vch"], ctx.cnt["vch"])
                    ctx.mark(v.tensor_mul(eqv[:, :], eqv[:, :], navail[:, :]))
                    v.wait_ge(sem["vch"], ctx.cnt["vch"])
                    ctx.mark(v.tensor_add(taken[:, :], taken[:, :], eqv[:, :]))
                    ctx.mark(v.tensor_mul(
                        bass.AP(wts, e, [[NT * 8, P], [8, NT]]),
                        eqv[:, :], scv[:, :]))
                    v.wait_ge(sem["vch"], ctx.cnt["vch"])
                v.engine_nop().then_inc(sem["m_wq"], 1)
                ctx.inc("m_wq", 1)

            @blk.scalar
            def _(sc):
                sc.wait_ge(sem["m_wq"], 1)
                sc.activation(scv[:, :], mx[:, :], AF.Exp, scale=-1.0
                              ).then_inc(sem["m_sc2"], 1)
                ctx.inc("m_sc2", 1)

        with nc.Block() as blk:
            @blk.tensor
            def _(te):
                for g in range(NT):
                    if g >= 2:
                        te.wait_ge(sem["m_wt_ev"], g - 1)
                    pc = (g % 2) * 512
                    te.transpose(pars[0:8, pc:pc + P],
                                 wts[:, g * 8:(g + 1) * 8], ctx.sb_identf[:, :]
                                 ).then_inc(sem["m_wq"], 1)
                    ctx.inc("m_wq", 1)

            @blk.vector
            def _(v):
                for g in range(NT):
                    v.wait_ge(sem["m_wq"], 2 + g + 1)
                    pc = (g % 2) * 512
                    v.tensor_copy(wtT[:, g * P:(g + 1) * P],
                                  bass.AP(pars, pc, [[4096, 8], [1, P]])
                                  ).then_inc(sem["m_wt_ev"], 1)
                    ctx.inc("m_wt_ev", 1)

        if getattr(ctx, "debug", False):
            with nc.Block() as blk:
                @blk.gpsimd
                def _(g):
                    g.dma_start(ctx.dram["d_lg"].ap(), lg[:, :]).then_inc(sem["m_dbg2"], 16)
                    g.dma_start(ctx.dram["d_wts"].ap(), wts[:, :]).then_inc(sem["m_dbg2"], 16)
                    g.dma_start(ctx.dram["d_wtT"].ap(), wtT[:, :]).then_inc(sem["m_dbg2"], 16)
                    ctx.inc("m_dbg2", 48)
                    g.wait_ge(sem["m_dbg2"], ctx.cnt["m_dbg2"])

        with contextlib.ExitStack() as stA:
            EA = stA.enter_context
            wbc = EA(nc.sbuf_tensor("m_wbc", [P, TOK], F32))
            gwt = [EA(nc.sbuf_tensor(f"m_gw{i}", [P, KD * P], BF16)) for i in range(2)]
            uwt = [EA(nc.sbuf_tensor(f"m_uw{i}", [P, KD * P], BF16)) for i in range(2)]
            a1p = EA(nc.sbuf_tensor("m_a1p", [P, I], F32))
            u1p = EA(nc.sbuf_tensor("m_u1p", [P, I], F32))
            qp = EA(nc.sbuf_tensor("m_qp", [P, I], F32))
            thp = EA(nc.sbuf_tensor("m_thp", [P, I], F32))
            hmst = [EA(nc.sbuf_tensor(f"m_hmst{i}", [P, I], BF16)) for i in range(2)]

            with nc.Block() as blk:
                @blk.gpsimd
                def _(g):
                    for e in range(NX):
                        for ht in range(NHT):
                            idx = e * NHT + ht
                            if idx >= 2:
                                g.wait_ge(sem["m_gu_mm"], 2 * (idx - 1))
                            if e < NE:
                                gsrc = gw.ap()[e, :, ht * P:(ht + 1) * P]
                                usrc = uw.ap()[e, :, ht * P:(ht + 1) * P]
                            else:
                                gsrc = shg.ap()[:, ht * P:(ht + 1) * P]
                                usrc = shu.ap()[:, ht * P:(ht + 1) * P]
                            mws = "mw" if idx % 2 == 0 else "mw2"
                            g.dma_start(gwt[idx % 2][:, :],
                                        gsrc.rearrange("(k p) h -> p k h", p=P)
                                        ).then_inc(sem[mws], 16)
                            g.dma_start(uwt[idx % 2][:, :],
                                        usrc.rearrange("(k p) h -> p k h", p=P)
                                        ).then_inc(sem[mws], 16)
                            ctx.inc(mws, 32)

                @blk.tensor
                def _(te):
                    te.wait_ge(sem["m_wq"], ctx.cnt["m_wq"])
                    te.wait_ge(sem["m_wt_ev"], NT)
                    for e in range(NX):
                        if e < NE:
                            te.wait_ge(sem["m_xe"], e)
                            for g2 in range(NT):
                                pc = 2048 + (g2 % 2) * 512
                                if g2 >= 2:
                                    te.wait_ge(sem["m_wb_ev"], e * NT + g2 - 1)
                                te.matmul(pars[:, pc:pc + P],
                                          e8sb[:, e * P:(e + 1) * P],
                                          wtT[:, g2 * P:(g2 + 1) * P],
                                          start=True, stop=True
                                          ).then_inc(sem["m_wb_mm"], 1)
                                ctx.inc("m_wb_mm", 1)
                        for ht in range(NHT):
                            idx = e * NHT + ht
                            te.wait_ge(sem["mw" if idx % 2 == 0 else "mw2"],
                                       32 * (idx // 2 + 1))
                            if e < NE and ht == 0:
                                te.wait_ge(sem["m_wb_ev"], (e + 1) * NT)
                            for half in range(2):
                                hi = idx * 2 + half
                                if hi >= 1:
                                    te.wait_ge(sem["m_q"], hi)
                                    te.wait_ge(sem["m_u"], hi)
                                for k in range(KD):
                                    mvb = lnT[:, k * TOK + half * 2048:
                                              k * TOK + (half + 1) * 2048]
                                    wga = gwt[idx % 2][:, k * P:(k + 1) * P]
                                    wua = uwt[idx % 2][:, k * P:(k + 1) * P]
                                    for nb in range(4):
                                        te.matmul(pars[:, nb * 512:(nb + 1) * 512],
                                                  wga, mvb[:, nb * 512:(nb + 1) * 512],
                                                  start=(k == 0), stop=(k == KD - 1))
                                    for nb in range(4):
                                        ins = te.matmul(
                                            pars[:, 2048 + nb * 512:2048 + (nb + 1) * 512],
                                            wua, mvb[:, nb * 512:(nb + 1) * 512],
                                            start=(k == 0), stop=(k == KD - 1))
                                ins.then_inc(sem["m_gu_mm"], 1)
                                ctx.inc("m_gu_mm", 1)

                @blk.scalar
                def _(sc):
                    for e in range(NX):
                        for ht in range(NHT):
                            for half in range(2):
                                hi = (e * NHT + ht) * 2 + half
                                sc.wait_ge(sem["m_q"], hi + 1)
                                sc.activation(thp[:, :], a1p[:, :], AF.Sigmoid,
                                              scale=1.0).then_inc(sem["m_th"], 1)
                                ctx.inc("m_th", 1)

                @blk.vector
                def _(v):
                    for e in range(NX):
                        if e < NE:
                            for g2 in range(NT):
                                v.wait_ge(sem["m_wb_mm"], e * NT + g2 + 1)
                                pc = 2048 + (g2 % 2) * 512
                                v.tensor_copy(wbc[:, g2 * P:(g2 + 1) * P],
                                              pars[:, pc:pc + P]
                                              ).then_inc(sem["m_wb_ev"], 1)
                                ctx.inc("m_wb_ev", 1)
                        for ht in range(NHT):
                            for half in range(2):
                                hi = (e * NHT + ht) * 2 + half
                                v.wait_ge(sem["m_gu_mm"], hi + 1)
                                if e < NE:
                                    ctx.mark(v.tensor_mul(
                                        a1p[:, :], pars[:, 0:2048],
                                        wbc[:, half * 2048:(half + 1) * 2048]))
                                    v.wait_ge(sem["vch"], ctx.cnt["vch"])
                                    v.engine_nop().then_inc(sem["m_q"], 1)
                                    ctx.inc("m_q", 1)
                                    ctx.mark(v.tensor_mul(
                                        u1p[:, :], pars[:, 2048:4096],
                                        wbc[:, half * 2048:(half + 1) * 2048]))
                                else:
                                    ctx.mark(v.tensor_copy(a1p[:, :], pars[:, 0:2048]))
                                    v.wait_ge(sem["vch"], ctx.cnt["vch"])
                                    v.engine_nop().then_inc(sem["m_q"], 1)
                                    ctx.inc("m_q", 1)
                                    ctx.mark(v.tensor_copy(u1p[:, :], pars[:, 2048:4096]))
                                v.wait_ge(sem["vch"], ctx.cnt["vch"])
                                v.engine_nop().then_inc(sem["m_u"], 1)
                                ctx.inc("m_u", 1)
                                ctx.mark(v.tensor_mul(qp[:, :], a1p[:, :], u1p[:, :]))
                                v.wait_ge(sem["vch"], ctx.cnt["vch"])
                                v.wait_ge(sem["m_th"], hi + 1)
                                if hi >= 2:
                                    v.wait_ge(sem["m_hm_st" if hi % 2 == 0
                                                  else "m_hm_st2"],
                                              16 * ((hi - 2) // 2 + 1))
                                v.tensor_mul(
                                    hmst[hi % 2][:, :], thp[:, :], qp[:, :]
                                    ).then_inc(sem["m_gu_ev"], 2)
                                ctx.inc("m_gu_ev", 2)
                        if e < NE:
                            v.engine_nop().then_inc(sem["m_xe"], 1)
                            ctx.inc("m_xe", 1)

                @blk.sync
                def _(sy):
                    for e in range(NX):
                        for ht in range(NHT):
                            for half in range(2):
                                hi = (e * NHT + ht) * 2 + half
                                sy.wait_ge(sem["m_gu_ev"], 2 * (hi + 1))
                                sy.dma_start(
                                    hm_s.ap()[e, ht, :, half * 2048:(half + 1) * 2048],
                                    hmst[hi % 2][:, :]
                                    ).then_inc(sem["m_hm_st" if hi % 2 == 0
                                                   else "m_hm_st2"], 16)
                                ctx.inc("m_hm_st" if hi % 2 == 0 else "m_hm_st2", 16)
                    sy.wait_ge(sem["m_hm_st"], ctx.cnt["m_hm_st"])
                    sy.wait_ge(sem["m_hm_st2"], ctx.cnt["m_hm_st2"])

        stP.close()

        with contextlib.ExitStack() as stB:
            EB = stB.enter_context
            racc = EB(nc.sbuf_tensor("m_racc", [P, 16 * D], F32))
            dwb = [EB(nc.sbuf_tensor(f"m_dwb{i}", [P, NHT * D], BF16)) for i in range(2)]
            hmb = [EB(nc.sbuf_tensor(f"m_hmb{i}", [P, NHT * 512], BF16)) for i in range(2)]
            xres = [EB(nc.sbuf_tensor(f"m_xres{i}", [P, D], F32)) for i in range(4)]
            outst = [EB(nc.sbuf_tensor(f"m_outst{i}", [P, D], F32)) for i in range(2)]
            rin = EB(nc.sbuf_tensor("m_rin", [P, D], F32))
            s1b = EB(nc.sbuf_tensor("m_s1b", [P, 4], F32))
            s2b = EB(nc.sbuf_tensor("m_s2b", [P, 4], F32))
            mu3 = EB(nc.sbuf_tensor("m_mu3", [P, 4], F32))
            rstd3 = EB(nc.sbuf_tensor("m_rstd3", [P, 4], F32))
            z3 = EB(nc.sbuf_tensor("m_z3", [P, 4], F32))
            e3 = EB(nc.sbuf_tensor("m_e3", [P, 4], F32))
            t3 = EB(nc.sbuf_tensor("m_t3", [P, 4], F32))
            plans3 = [RstdPlan(ctx, s2b[:, :], rstd3[:, :], z3[:, :], e3[:, :],
                               t3[:, :]) for _ in range(8)]

            with nc.Block() as blk:
                @blk.gpsimd
                def _(g):
                    for sweep in range(2):
                        for e in range(NX):
                            ei = sweep * NX + e
                            if ei >= 2:
                                g.wait_ge(sem["m_dn_mm"], (ei - 1) * 4)
                            src = (dw.ap()[e].rearrange("(n p) d -> p n d", p=P)
                                   if e < NE else
                                   shd.ap().rearrange("(n p) d -> p n d", p=P))
                            g.dma_start(dwb[ei % 2][:, :], src
                                        ).then_inc(sem["m_dw" if ei % 2 == 0
                                                       else "m_dw2"], 16)
                            ctx.inc("m_dw" if ei % 2 == 0 else "m_dw2", 16)

                @blk.sync
                def _(sy):
                    for sweep in range(2):
                        for e in range(NX):
                            for gi in range(4):
                                grp = sweep * 4 + gi
                                li = (sweep * NX + e) * 4 + gi
                                if li >= 2:
                                    sy.wait_ge(sem["m_dn_mm"], li - 1)
                                sy.dma_start(
                                    hmb[li % 2][:, :],
                                    hm_s.ap()[e, :, :, grp * 512:(grp + 1) * 512]
                                        .transpose([1, 0, 2])
                                ).then_inc(sem["m_hm_ld" if li % 2 == 0
                                                   else "m_hm_ld2"], 16)
                                ctx.inc("m_hm_ld" if li % 2 == 0 else "m_hm_ld2", 16)
                        for grp4 in range(4):
                            for j in range(4):
                                gt = sweep * 16 + grp4 * 4 + j
                                if gt >= 4:
                                    sy.wait_ge(sem["m_res"], gt - 3)
                                xrs = ("m_xr2", "m_xr3", "m_xr4", "m_xr5")[gt % 4]
                                sy.dma_start(xres[gt % 4][:, :], xr[gt, :, :]
                                             ).then_inc(sem[xrs], 16)
                                ctx.inc(xrs, 16)
                            for j in range(4):
                                gt = sweep * 16 + grp4 * 4 + j
                                sy.wait_ge(sem["m_res"], gt + 1)
                                sy.dma_start(xo[gt, :, :], outst[gt % 2][:, :]
                                             ).then_inc(sem["m_st" if gt % 2 == 0
                                                            else "m_st2"], 16)
                                ctx.inc("m_st" if gt % 2 == 0 else "m_st2", 16)
                    sy.wait_ge(sem["m_st"], ctx.cnt["m_st"])
                    sy.wait_ge(sem["m_st2"], ctx.cnt["m_st2"])

                @blk.tensor
                def _(te):
                    for sweep in range(2):
                        for e in range(NX):
                            ei = sweep * NX + e
                            te.wait_ge(sem["m_dw" if ei % 2 == 0 else "m_dw2"],
                                       16 * (ei // 2 + 1))
                            for gi in range(4):
                                li = ei * 4 + gi
                                te.wait_ge(sem["m_hm_ld" if li % 2 == 0
                                                   else "m_hm_ld2"],
                                           16 * (li // 2 + 1))
                                if li >= 1:
                                    te.wait_ge(sem["m_dn_ev"], li)
                                for ht in range(NHT):
                                    for tt in range(4):
                                        stat = hmb[li % 2][:, ht * 512 + tt * P:
                                                           ht * 512 + (tt + 1) * P]
                                        for nb in range(2):
                                            ins = te.matmul(
                                                pars[:, tt * 1024 + nb * 512:
                                                     tt * 1024 + (nb + 1) * 512],
                                                stat,
                                                dwb[ei % 2][:, ht * D + nb * 512:
                                                            ht * D + (nb + 1) * 512],
                                                start=(ht == 0), stop=(ht == NHT - 1))
                                ins.then_inc(sem["m_dn_mm"], 1)
                                ctx.inc("m_dn_mm", 1)

                @blk.vector
                def _(v):
                    for sweep in range(2):
                        ctx.mark(v.memset(racc[:, :], 0.0))
                        v.wait_ge(sem["vch"], ctx.cnt["vch"])
                        for e in range(NX):
                            ei = sweep * NX + e
                            for gi in range(4):
                                li = ei * 4 + gi
                                v.wait_ge(sem["m_dn_mm"], li + 1)
                                sl = racc[:, gi * 4096:(gi + 1) * 4096]
                                ctx.mark(v.tensor_add(sl, sl, pars[:, 0:4096]))
                                v.wait_ge(sem["vch"], ctx.cnt["vch"])
                                v.engine_nop().then_inc(sem["m_dn_ev"], 1)
                                ctx.inc("m_dn_ev", 1)
                        for gi in range(4):
                            grp = sweep * 4 + gi
                            for j in range(4):
                                gt = grp * 4 + j
                                v.wait_ge(
                                    sem[("m_xr2", "m_xr3", "m_xr4", "m_xr5")[gt % 4]],
                                    16 * (gt // 4 + 1))
                                ctx.mark(v.tensor_scalar(
                                    rin[:, :], xres[gt % 4][:, :],
                                    a1[:, gt:gt + 1], c1[:, gt:gt + 1],
                                    ALU.mult, ALU.add))
                                v.wait_ge(sem["vch"], ctx.cnt["vch"])
                                ctx.mark(v.tensor_add(
                                    rin[:, :], rin[:, :],
                                    racc[:, (gi * 4 + j) * D:(gi * 4 + j + 1) * D]))
                                v.wait_ge(sem["vch"], ctx.cnt["vch"])
                                ctx.mark(v.tensor_copy(
                                    racc[:, (gi * 4 + j) * D:(gi * 4 + j + 1) * D],
                                    rin[:, :]))
                                ctx.mark(v.reduce_sum(s1b[:, j:j + 1], rin[:, :],
                                                      axis=AX.X))
                                ctx.mark(v.tensor_mul(dump[:, :], rin[:, :], rin[:, :]))
                                v.wait_ge(sem["vch"], ctx.cnt["vch"])
                                ctx.mark(v.reduce_sum(s2b[:, j:j + 1], dump[:, :],
                                                      axis=AX.X))
                            v.wait_ge(sem["vch"], ctx.cnt["vch"])
                            ctx.mark(v.tensor_scalar_mul(mu3[:, :], s1b[:, :], 1.0 / D))
                            v.tensor_scalar_mul(s2b[:, :], s2b[:, :], 1.0 / D)
                            v.wait_ge(sem["vch"], ctx.cnt["vch"])
                            ctx.mark(v.tensor_mul(t3[:, :], mu3[:, :], mu3[:, :]))
                            v.wait_ge(sem["vch"], ctx.cnt["vch"])
                            ctx.mark(v.tensor_sub(s2b[:, :], s2b[:, :], t3[:, :]))
                            v.wait_ge(sem["vch"], ctx.cnt["vch"])
                            ctx.mark(v.tensor_scalar_add(s2b[:, :], s2b[:, :], EPS))
                            v.wait_ge(sem["vch"], ctx.cnt["vch"])
                            pl = plans3[grp]
                            pl.emit_vector(v)
                            v.wait_ge(sem["r_s"], pl.done_val)
                            ctx.mark(v.tensor_mul(mu3[:, :], mu3[:, :], rstd3[:, :]))
                            v.wait_ge(sem["vch"], ctx.cnt["vch"])
                            ctx.mark(v.tensor_scalar_mul(mu3[:, :], mu3[:, :], -1.0))
                            v.wait_ge(sem["vch"], ctx.cnt["vch"])
                            for j in range(4):
                                gt = grp * 4 + j
                                if gt >= 2:
                                    v.wait_ge(sem["m_st" if gt % 2 == 0 else "m_st2"],
                                              16 * ((gt - 2) // 2 + 1))
                                ob = outst[gt % 2]
                                ctx.mark(v.tensor_scalar(
                                    ob[:, :],
                                    racc[:, (gi * 4 + j) * D:(gi * 4 + j + 1) * D],
                                    rstd3[:, j:j + 1], mu3[:, j:j + 1],
                                    ALU.mult, ALU.add))
                                v.wait_ge(sem["vch"], ctx.cnt["vch"])
                                v.tensor_add(ob[:, :], ob[:, :], xres[gt % 4][:, :]
                                             ).then_inc(sem["m_res"], 1)
                                ctx.inc("m_res", 1)

                @blk.scalar
                def _(sc):
                    for grp in range(8):
                        plans3[grp].emit_scalar(sc)

        with nc.Block() as blk:
            @blk.vector
            def _(v):
                v.wait_ge(sem["m_st"], ctx.cnt["m_st"])
                v.wait_ge(sem["m_st2"], ctx.cnt["m_st2"])





def build_program(stages=ALL_STAGES, debug=False, debug_l0a=False, dbg_nsp2=None,
                  debug_l1a=False):
    import os
    nc = bass.Bass("TRN2", target_bir_lowering=False,
                   detect_race_conditions=bool(os.environ.get("RACEDET")))
    stack = contextlib.ExitStack()
    ctx = Ctx(nc, stack)
    ctx.debug = debug
    ctx.debug_l0a = debug_l0a
    ctx.debug_l1a = debug_l1a
    if dbg_nsp2 is not None:
        ctx.dbg_nsp2 = dbg_nsp2
    if debug_l1a:
        for nm, shp, dt in (("dt_ln2Te", [P, KD * 2112], BF16),
                            ("dt_ln2Tq", [P, KD * 2112], BF16),
                            ("dt_ln2Tv", [P, KD * 2112], BF16),
                            ("dt_ln2Ta", [P, KD * 2112], BF16),
                            ("dt_ln2T", [P, KD * 2112], BF16),
                            ("dt_qkT", [P, 16 * 2112], BF16),
                            ("dt_vslot", [P, 26 * D], BF16)):
            ctx.dram[nm] = nc.dram_tensor(nm, shp, dt, kind="ExternalOutput")
    if debug_l0a:
        for nm, shp, dt in (("da_lnTs", [P, 8 * 512], BF16),
                            ("da_xbc", [P, 18 * 516], BF16), ("da_xtm", [P, 2048], BF16),
                            ("da_zsb", [P, 2048], BF16), ("da_ych", [P, 2048], F32),
                            ("da_dtc", [P, 32], F32), ("da_acsT", [P, 32], F32),
                            ("da_S", [P, 2048], F32), ("da_ynT", [P, 2048], BF16),
                            ("da_rsb", [P, 1024], F32), ("da_dsdt", [P, 32], F32),
                            ("da_einb", [P, 32], F32), ("da_etbv", [P, 32], F32)):
            ctx.dram[nm] = nc.dram_tensor(nm, shp, dt, kind="ExternalOutput")
    if debug:
        for nm, shp, dt in (("d_lnT2", [P, KD * TOK], BF16), ("d_lg", [P, NT * 8], F32),
                            ("d_wts", [P, NT * 8], F32), ("d_wtT", [8, TOK], F32),
                            ("d_s1", [P, NT], F32), ("d_s2", [P, NT], F32),
                            ("d_rstd", [P, NT], F32), ("d_mu", [P, NT], F32),
                            ("d_lnT", [P, 8192], BF16), ("d_hmT", [P, 24576], BF16),
                            ("d_rsp", [P, 4096], F32)):
            ctx.dram[nm] = nc.dram_tensor(nm, shp, dt, kind="ExternalOutput")

    dr = ctx.dram
    dr["x"] = nc.dram_tensor("x", [TOK, D], F32, kind="ExternalInput")
    dr["out"] = nc.dram_tensor("out", [TOK, D], F32, kind="ExternalOutput")
    dr["ident"] = nc.dram_tensor("ident", [P, P], BF16, kind="ExternalInput")
    dr["identf"] = nc.dram_tensor("identf", [P, P], F32, kind="ExternalInput")
    if "l0f" in stages:
        dr["l0f_wgT"] = nc.dram_tensor("l0f_wgT", [D, 3072], I32, kind="ExternalInput")
        dr["l0f_wuT"] = nc.dram_tensor("l0f_wuT", [D, 3072], I32, kind="ExternalInput")
        dr["l0f_wdT"] = nc.dram_tensor("l0f_wdT", [3072, D], I32, kind="ExternalInput")
    if "l0a" in stages:
        dr["l0a_win"] = nc.dram_tensor("l0a_win", [D, 4384], F32, kind="ExternalInput")
        dr["l0a_wout"] = nc.dram_tensor("l0a_wout", [2048, D], F32, kind="ExternalInput")
        dr["l0a_cw"] = nc.dram_tensor("l0a_cw", [P, 18 * 4], F32, kind="ExternalInput")
        dr["l0a_cb"] = nc.dram_tensor("l0a_cb", [P, 18], F32, kind="ExternalInput")
        dr["l0a_dtb"] = nc.dram_tensor("l0a_dtb", [P, 32], F32, kind="ExternalInput")
        dr["l0a_alog"] = nc.dram_tensor("l0a_alog", [P, 32], F32, kind="ExternalInput")
        dr["l0a_dpar"] = nc.dram_tensor("l0a_dpar", [P, 32], F32, kind="ExternalInput")
        dr["l0a_nw"] = nc.dram_tensor("l0a_nw", [P, 2048], F32, kind="ExternalInput")
        dr["l0a_triu"] = nc.dram_tensor("l0a_triu", [P, P], F32, kind="ExternalInput")
        dr["l0a_ones"] = nc.dram_tensor("l0a_ones", [P, P], F32, kind="ExternalInput")
        dr["l0a_hsel"] = nc.dram_tensor("l0a_hsel", [32, 32 * P], F32, kind="ExternalInput")
    if "l1a" in stages:
        dr["l1a_wqkvT"] = nc.dram_tensor("l1a_wqkvT", [D, 3072], F32, kind="ExternalInput")
        dr["l1a_woutT"] = nc.dram_tensor("l1a_woutT", [D, D], F32, kind="ExternalInput")
        dr["l1a_bout"] = nc.dram_tensor("l1a_bout", [P, D], F32, kind="ExternalInput")
    if "l1f" in stages:
        dr["l1f_gate"] = nc.dram_tensor("l1f_gate", [8, D, 2048], F32, kind="ExternalInput")
        dr["l1f_up"] = nc.dram_tensor("l1f_up", [8, D, 2048], F32, kind="ExternalInput")
        dr["l1f_down"] = nc.dram_tensor("l1f_down", [8, 2048, D], F32, kind="ExternalInput")
        dr["l1f_shg"] = nc.dram_tensor("l1f_shg", [D, 2048], F32, kind="ExternalInput")
        dr["l1f_shu"] = nc.dram_tensor("l1f_shu", [D, 2048], F32, kind="ExternalInput")
        dr["l1f_shd"] = nc.dram_tensor("l1f_shd", [2048, D], F32, kind="ExternalInput")
        dr["l1f_router"] = nc.dram_tensor("l1f_router", [D, 8], F32, kind="ExternalInput")
        dr["l1f_lnw"] = nc.dram_tensor("l1f_lnw", [P, D], F32, kind="ExternalInput")
        dr["l1f_lnb"] = nc.dram_tensor("l1f_lnb", [P, D], F32, kind="ExternalInput")
        dr["l1f_e8"] = nc.dram_tensor("l1f_e8", [8, 8 * P], F32, kind="ExternalInput")
        dr["l1f_hms"] = nc.dram_tensor("l1f_hms", [9, 16, P, TOK], BF16, kind="Internal")
    n_mid = max(0, len(stages) - 1)
    mids = [nc.dram_tensor(f"xs{i}", [TOK, D], F32, kind="Internal")
            for i in range(n_mid)]

    with stack:
        ctx.sb_ident = stack.enter_context(nc.sbuf_tensor("c_ident", [P, P], BF16))
        ctx.sb_identf = stack.enter_context(nc.sbuf_tensor("c_identf", [P, P], F32))
        ctx.get_sem("c_ld")
        with nc.Block() as blk:
            @blk.sync
            def _(sy):
                sy.dma_start(ctx.sb_ident[:, :], ctx.dram["ident"].ap()
                             ).then_inc(ctx.sem["c_ld"], 16)
                ctx.inc("c_ld", 16)
                sy.dma_start(ctx.sb_identf[:, :], ctx.dram["identf"].ap()
                             ).then_inc(ctx.sem["c_ld"], 16)
                ctx.inc("c_ld", 16)
                sy.wait_ge(ctx.sem["c_ld"], ctx.cnt["c_ld"])

        cur = dr["x"]
        snap = (nc.sbuf_base, nc.sbuf_top, nc.psum_base, nc.psum_top,
                nc.local_dram_base, nc.shared_dram_base)
        for i, sname in enumerate(stages):
            nxt = dr["out"] if i == len(stages) - 1 else mids[i]
            if sname == "l0a":
                emit_l0_attn(ctx, cur, nxt)
            elif sname == "l1a":
                emit_l1_attn(ctx, cur, nxt)
            elif sname == "l0f":
                emit_l0_ffn(ctx, cur, nxt)
            elif sname == "l1f":
                emit_l1_ffn(ctx, cur, nxt)
            else:
                raise NotImplementedError(sname)
            cur = nxt
            if i < len(stages) - 1:
                ctx.sem.clear()
                ctx.cnt.clear()
                nc.reset(previous_sbuf_base=snap[0], previous_sbuf_top=snap[1],
                         previous_psum_base=snap[2], previous_psum_top=snap[3],
                         previous_local_dram_base=snap[4],
                         previous_shared_dram_base=snap[5])
    return nc



def _prep_const_inputs(inputs, stages=ALL_STAGES):
    com = {}
    com["ident"] = np.eye(P, dtype=ml_dtypes.bfloat16)
    com["identf"] = np.eye(P, dtype=np.float32)
    if "l0a" in stages:
        com["l0a_win"] = np.ascontiguousarray(np.asarray(inputs["L0_attn_in_proj_w"], dtype=np.float32))
        com["l0a_wout"] = np.ascontiguousarray(np.asarray(inputs["L0_attn_out_proj_w"], dtype=np.float32))
        cw = np.asarray(inputs["L0_attn_conv_weight"], dtype=np.float32)[:, 0, :]
        com["l0a_cw"] = np.ascontiguousarray(cw.reshape(18, P, 4).transpose(1, 0, 2).reshape(P, 72))
        cb = np.asarray(inputs["L0_attn_conv_bias"], dtype=np.float32)
        com["l0a_cb"] = np.ascontiguousarray(cb.reshape(18, P).T)
        com["l0a_dtb"] = np.broadcast_to(np.asarray(inputs["L0_attn_dt_bias"], dtype=np.float32), (P, 32)).copy()
        com["l0a_alog"] = np.broadcast_to(np.asarray(inputs["L0_attn_A_log"], dtype=np.float32), (P, 32)).copy()
        com["l0a_dpar"] = np.broadcast_to(np.asarray(inputs["L0_attn_D_param"], dtype=np.float32), (P, 32)).copy()
        com["l0a_nw"] = np.broadcast_to(np.asarray(inputs["L0_attn_norm_weight"], dtype=np.float32), (P, 2048)).copy()
        ii = np.arange(P)
        com["l0a_triu"] = (ii[:, None] <= ii[None, :]).astype(np.float32)
        com["l0a_ones"] = np.ones((P, P), np.float32)
        hs = np.zeros((32, 32 * P), np.float32)
        for h in range(32):
            hs[h, h * P:(h + 1) * P] = 1.0
        com["l0a_hsel"] = hs
    if "l0f" in stages:
        com["l0f_wgT"] = np.ascontiguousarray(np.asarray(inputs["L0_ffn_w_gate"]).T)
        com["l0f_wuT"] = np.ascontiguousarray(np.asarray(inputs["L0_ffn_w_up"]).T)
        com["l0f_wdT"] = np.ascontiguousarray(np.asarray(inputs["L0_ffn_w_down"]).T)
    if "l1a" in stages:
        com["l1a_wqkvT"] = np.ascontiguousarray(np.asarray(inputs["L1_attn_Wqkv"], dtype=np.float32).T)
        com["l1a_woutT"] = np.ascontiguousarray(np.asarray(inputs["L1_attn_Wout"], dtype=np.float32).T)
        com["l1a_bout"] = np.broadcast_to(np.asarray(inputs["L1_attn_bout"], dtype=np.float32), (P, D)).copy()
    if "l1f" in stages:
        com["l1f_gate"] = np.ascontiguousarray(np.asarray(inputs["L1_ffn_gate_w"], dtype=np.float32))
        com["l1f_up"] = np.ascontiguousarray(np.asarray(inputs["L1_ffn_up_w"], dtype=np.float32))
        com["l1f_down"] = np.ascontiguousarray(np.asarray(inputs["L1_ffn_down_w"], dtype=np.float32))
        com["l1f_shg"] = np.ascontiguousarray(np.asarray(inputs["L1_ffn_sh_gate"], dtype=np.float32))
        com["l1f_shu"] = np.ascontiguousarray(np.asarray(inputs["L1_ffn_sh_up"], dtype=np.float32))
        com["l1f_shd"] = np.ascontiguousarray(np.asarray(inputs["L1_ffn_sh_down"], dtype=np.float32))
        com["l1f_router"] = np.ascontiguousarray(np.asarray(inputs["L1_ffn_router_w"], dtype=np.float32))
        com["l1f_lnw"] = np.broadcast_to(np.asarray(inputs["L1_ffn_ln_w"], dtype=np.float32), (P, D)).copy()
        com["l1f_lnb"] = np.broadcast_to(np.asarray(inputs["L1_ffn_ln_b"], dtype=np.float32), (P, D)).copy()
        e8 = np.zeros((8, 8 * P), dtype=np.float32)
        for e in range(8):
            e8[e, e * P:(e + 1) * P] = 1.0
        com["l1f_e8"] = e8
    return com


def run_stages(x_full, inputs, stages=ALL_STAGES):
    nc = build_program(stages)
    com = _prep_const_inputs(inputs, stages)
    in_maps = []
    for c in range(8):
        m = dict(com)
        m["x"] = np.ascontiguousarray(
            np.asarray(x_full[2 * c:2 * c + 2], dtype=np.float32).reshape(TOK, D))
        in_maps.append(m)
    res = run_bass_kernel_spmd(nc, in_maps, core_ids=list(range(8)))
    out = np.stack([np.asarray(r["out"]) for r in res.results])
    return out.reshape(16, 2048, 1024)


def kernel(**inputs):
    x = np.asarray(inputs["L0_attn_u"], dtype=np.float32)
    return run_stages(x, inputs, ALL_STAGES).astype(np.float32)
